# version 16
# baseline (speedup 1.0000x reference)
import contextlib
import numpy as np

import concourse.bass as bass
import concourse.mybir as mybir
from concourse.bass_utils import run_bass_kernel_spmd

F32 = mybir.dt.float32
BF16 = mybir.dt.bfloat16
AF = mybir.ActivationFunctionType
ALU = mybir.AluOpType
AX = mybir.AxisListType

B, T, D = 4, 4096, 1024
TC = 2048
H, HD, KV, QL = 16, 64, 64, 512
NCORES = 8

ENGINES = ("tensor", "vector", "scalar", "gpsimd", "sync")


class Sem:
    __slots__ = ("h", "n")

    def __init__(self, h):
        self.h = h
        self.n = 0


class DSem:
    __slots__ = ("hw", "sw")

    def __init__(self, hw, sw):
        self.hw = hw
        self.sw = sw

    @property
    def n(self):
        return (self.hw.n, self.sw.n)


class Prog:

    def __init__(self, nc, ctx):
        self.nc = nc
        self.ctx = ctx
        self.ops = {e: [] for e in ENGINES}

    def sem(self, name):
        return Sem(self.ctx.enter_context(self.nc.semaphore(name)))

    def dsem(self, name):
        return DSem(self.sem(name + "h"), self.sem(name + "s"))

    def op(self, eng, fn, incs=()):
        if isinstance(incs, tuple) and incs and isinstance(incs[0], (Sem, DSem)):
            incs = [incs]
        captured = []
        for s, k in incs:
            if isinstance(s, DSem):
                s = s.sw if eng == "gpsimd" else s.hw
            s.n += k
            captured.append((s.h, k))
        self.ops[eng].append(("op", fn, captured))

    def wait(self, eng, sem, val=None):
        if isinstance(sem, DSem):
            if isinstance(val, int):
                assert val == 0, "int wait value for DSem must be 0"
                return
            vhw, vsw = sem.n if val is None else val
            if vhw:
                self.ops[eng].append(("wait", sem.hw.h, vhw))
            if vsw:
                self.ops[eng].append(("wait", sem.sw.h, vsw))
            return
        v = sem.n if val is None else val
        self.ops[eng].append(("wait", sem.h, v))

    def barrier_all(self, sems):
        for eng in ENGINES:
            for s in sems:
                self.wait(eng, s)

    def dma(self, eng, out, in_, sem):
        self.op(eng, lambda e, o=out, i=in_: e.dma_start(out=o, in_=i),
                incs=(sem, 16))

    def flush(self):
        pend = {e: lst for e, lst in self.ops.items() if lst}
        if not pend:
            return
        with self.nc.Block() as block:
            for eng, lst in pend.items():
                def make(lst=lst):
                    def body(e):
                        for item in lst:
                            if item[0] == "wait":
                                e.wait_ge(item[1], item[2])
                            else:
                                r = item[1](e)
                                for h, k in item[2]:
                                    r = r.then_inc(h, k)
                    return body
                getattr(block, eng)(make())
        self.ops = {e: [] for e in ENGINES}



def build_launch1():
    nc = bass.Bass("TRN2", target_bir_lowering=False, detect_race_conditions=False)
    dt = nc.dram_tensor
    x_own = dt("x_own", [TC, D], F32, kind="ExternalInput").ap()
    x_oth = dt("x_oth", [TC, D], F32, kind="ExternalInput").ap()
    wq_d = dt("wq_d", [QL, D], F32, kind="ExternalInput").ap()
    w_qk = dt("w_qk", [D, QL], F32, kind="ExternalInput").ap()
    wkv_d = dt("wkv_d", [KV, D], F32, kind="ExternalInput").ap()
    wv_u = dt("wv_u", [D, KV], F32, kind="ExternalInput").ap()
    w_proj = dt("w_proj", [D, D], F32, kind="ExternalInput").ap()
    bq_cols = dt("bq_cols", [128, 4], F32, kind="ExternalInput").ap()
    bqk_cols = dt("bqk_cols", [128, 8], F32, kind="ExternalInput").ap()
    bkv_col = dt("bkv_col", [64, 1], F32, kind="ExternalInput").ap()
    bvu_row = dt("bvu_row", [1, D], F32, kind="ExternalInput").ap()
    bprj_row = dt("bprj_row", [1, D], F32, kind="ExternalInput").ap()
    ones_row = dt("ones_row", [1, 128], F32, kind="ExternalInput").ap()
    ident_in = dt("ident_in", [128, 128], F32, kind="ExternalInput").ap()
    halves = dt("halves", [128, 2], F32, kind="ExternalInput").ap()
    e16 = dt("e16", [8, 16, 128], F32, kind="ExternalInput").ap()
    x1_out = dt("x1", [TC, D], F32, kind="ExternalOutput").ap()

    with contextlib.ExitStack() as ctx:
        p = Prog(nc, ctx)
        sLD = p.dsem("sLD")
        sPE = p.sem("sPE")
        sACT = p.sem("sACT")
        sDVE = p.sem("sDVE")
        sST = p.dsem("sST")

        def sb(name, shape, dtyp):
            return ctx.enter_context(nc.sbuf_tensor(name, shape, dtyp))

        onesr_bf = sb("onesr_bf", [1, 128], BF16)
        ident_bf = sb("ident_bf", [128, 128], BF16)
        halves_bf = sb("halves_bf", [128, 2], BF16)
        e16_bf = sb("e16_bf", [16, 8 * 128], BF16)
        bq_sb = sb("bq_sb", [128, 4], F32)
        bqk_sb = sb("bqk_sb", [128, 8], F32)
        bkv_sb = sb("bkv_sb", [64, 1], F32)
        bprj_hi = sb("bprj_hi", [1, D], BF16)
        bprj_lo = sb("bprj_lo", [1, D], BF16)
        maxq = sb("maxq", [1, 16], F32)
        maxk = sb("maxk", [1, 2], F32)
        bndrow = sb("bndrow", [1, 16], F32)
        bnd_bf = sb("bnd_bf", [1, 16], BF16)
        negbnd = sb("negbnd", [128, 16], F32)
        epsc = sb("epsc", [128, 2], F32)
        srow = sb("srow", [64, 1024], F32)

        wq_dT = sb("wq_dT", [128, 8 * QL], BF16)
        w_qkT = sb("w_qkT", [128, 4 * D], BF16)
        wkv_dT = sb("wkv_dT", [128, 8 * KV], BF16)
        wv_uT = sb("wv_uT", [65, D], BF16)
        w_projT = sb("w_projT", [128, 8 * D], BF16)
        ckvT = sb("ckvT", [128, T], BF16)
        ckv_tok = sb("ckv_tok", [128, 32 * 65], BF16)
        qT = sb("qT", [128, 8 * TC], BF16)

        g = "gpsimd"
        p.dma(g, onesr_bf[:, :], ones_row, sLD)
        p.dma(g, ident_bf[:, :], ident_in, sLD)
        p.dma(g, halves_bf[:, :], halves, sLD)
        for c in range(8):
            p.dma(g, e16_bf[:, 128 * c:128 * (c + 1)], e16[c], sLD)
        p.dma("sync", bq_sb[:, :], bq_cols, sLD)
        p.dma("sync", bqk_sb[:, :], bqk_cols, sLD)
        p.dma("sync", bkv_sb[:, :], bkv_col, sLD)
        p.dma("sync", srow[0:1, 0:D], bprj_row, sLD)
        p.dma(g, wv_uT[64:65, :], bvu_row, sLD)
        n_pre = sLD.n

        p.op("vector", lambda e: e.memset(epsc[:, 0:1], 1e-20), incs=(sDVE, 1))
        p.op("vector", lambda e: e.memset(epsc[:, 1:2], 1e-5), incs=(sDVE, 1))
        p.wait("vector", sLD, n_pre)
        p.op("vector", lambda e: e.tensor_copy(bprj_hi[:, :], srow[0:1, 0:D]), incs=(sDVE, 1))
        p.wait("vector", sDVE)
        p.op("vector", lambda e: e.tensor_tensor(srow[0:1, 0:D], srow[0:1, 0:D], bprj_hi[:, :], ALU.subtract), incs=(sDVE, 1))
        p.wait("vector", sDVE)
        p.op("vector", lambda e: e.tensor_copy(bprj_lo[:, :], srow[0:1, 0:D]), incs=(sDVE, 1))

        with nc.psum_tensor("pT0", [128, 1024], BF16) as pT0, \
             nc.psum_tensor("pT1", [128, 1024], BF16) as pT1, \
             nc.sbuf_tensor("xT", [128, 8 * T], BF16) as xT, \
             nc.sbuf_tensor("c_qT", [128, 4 * TC], BF16) as c_qT, \
             nc.sbuf_tensor("qsq", [128, TC], BF16) as qsq, \
             nc.sbuf_tensor("ksq", [64, T], BF16) as ksq, \
             nc.sbuf_tensor("stgA", [128, 1024], BF16) as stgA, \
             nc.sbuf_tensor("stgB", [128, 1024], BF16) as stgB:
            pTs = [pT0, pT1]
            tcnt = [0]

            def tload(dram, R, C, dest_fn):
                nR = (R + 127) // 128
                nCc = (C + 127) // 128
                cw = min(128, C)
                assert C % 128 == 0 or nCc == 1
                for r in range(nR):
                    Rw = min(128, R - 128 * r)
                    stg = stgA if tcnt[0] % 2 == 0 else stgB
                    pt = pTs[tcnt[0] % 2]
                    tcnt[0] += 1
                    p.wait(g, sPE)
                    p.dma(g, stg[0:Rw, 0:C], dram[128 * r:128 * r + Rw, :], sLD)
                    p.wait("tensor", sLD)
                    p.wait("tensor", sDVE)
                    for c in range(nCc):
                        ccw = min(128, C - 128 * c)
                        p.op("tensor", lambda e, stg=stg, pt=pt, r=r, c=c, Rw=Rw, ccw=ccw: e.transpose(
                            pt[0:ccw, 128 * c:128 * c + Rw],
                            stg[0:Rw, 128 * c:128 * c + ccw],
                            ident_bf[0:Rw, 0:Rw]),
                            incs=(sPE, 1) if c == nCc - 1 else ())
                    p.wait("vector", sPE)
                    if nCc == 1:
                        src = pt[0:cw, 0:Rw]
                    else:
                        src = pt.ap().rearrange("p (c n) -> p c n", c=8)[:, 0:nCc, 0:Rw]
                    p.op("vector", lambda e, d=dest_fn(r), s=src: e.tensor_copy(d, s),
                         incs=(sDVE, 1))

            xT_r = xT.ap().rearrange("p (c n) -> p c n", c=8)
            tload(x_own, TC, D, lambda r: xT_r[:, :, 128 * r:128 * r + 128])
            tload(x_oth, TC, D, lambda r: xT_r[:, :, TC + 128 * r:TC + 128 * r + 128])
            wqT_r = wq_dT.ap().rearrange("p (c n) -> p c n", c=8)
            tload(wq_d, QL, D, lambda r: wqT_r[:, :, 128 * r:128 * r + 128])
            wqkT_r = w_qkT.ap().rearrange("p (c n) -> p c n", c=4)
            tload(w_qk, D, QL, lambda r: wqkT_r[:, 0:4, 128 * r:128 * r + 128])
            wkvT_r = wkv_dT.ap().rearrange("p (c n) -> p c n", c=8)
            tload(wkv_d, KV, D, lambda r: wkvT_r[:, :, 0:64])
            tload(wv_u, D, KV, lambda r: wv_uT[0:64, 128 * r:128 * r + 128])
            wpT_r = w_projT.ap().rearrange("p (c n) -> p c n", c=8)
            tload(w_proj, D, D, lambda r: wpT_r[:, :, 128 * r:128 * r + 128])

            with nc.psum_tensor("pmA", [128, 512], F32) as pmA, \
                 nc.psum_tensor("pmB", [128, 512], F32) as pmB, \
                 nc.psum_tensor("pq", [1, 2048], F32) as pq:
                pms = [pmA, pmB]
                pm_i = [0]
                p.wait("tensor", sDVE)
                p.wait("scalar", sLD, n_pre)

                def mm_chain(out_view, lhsTs, rhss, bias_ap):
                    pm = pms[pm_i[0] % 2]
                    pm_i[0] += 1
                    p.wait("tensor", sDVE)
                    nk = len(lhsTs)
                    M = out_view.shape[0]
                    N = out_view.shape[-1]
                    for k in range(nk):
                        p.op("tensor", lambda e, k=k, pm=pm, M=M, N=N, lh=lhsTs[k], rh=rhss[k], nk=nk: e.matmul(
                            pm[0:M, 0:N], lh, rh, start=(k == 0), stop=(k == nk - 1)),
                            incs=(sPE, 1) if k == nk - 1 else ())
                    p.wait("vector", sPE)
                    p.op("vector", lambda e, pm=pm, M=M, N=N, ov=out_view, b=bias_ap: e.tensor_scalar(
                        ov, pm[0:M, 0:N], b, None, ALU.add),
                        incs=(sDVE, 1))

                for sl in range(8):
                    mm_chain(ckvT[0:64, 512 * sl:512 * (sl + 1)],
                             [wkv_dT[:, 64 * k:64 * (k + 1)] for k in range(8)],
                             [xT[:, 4096 * k + 512 * sl: 4096 * k + 512 * (sl + 1)] for k in range(8)],
                             bkv_sb[:, 0:1])
                    pm_last = pms[(pm_i[0] - 1) % 2]
                    p.op("vector", lambda e, sl=sl, pm=pm_last: e.tensor_scalar(
                        ckvT[64:128, 512 * sl:512 * (sl + 1)], pm[0:64, 0:512],
                        bkv_sb[:, 0:1], None, ALU.add), incs=(sDVE, 1))
                for oc in range(4):
                    for ts in range(4):
                        mm_chain(c_qT[:, 2048 * oc + 512 * ts: 2048 * oc + 512 * (ts + 1)],
                                 [wq_dT[:, 512 * k + 128 * oc: 512 * k + 128 * (oc + 1)] for k in range(8)],
                                 [xT[:, 4096 * k + 512 * ts: 4096 * k + 512 * (ts + 1)] for k in range(8)],
                                 bq_sb[:, oc:oc + 1])
                for oc in range(8):
                    for ts in range(4):
                        mm_chain(qT[:, 2048 * oc + 512 * ts: 2048 * oc + 512 * (ts + 1)],
                                 [w_qkT[:, 1024 * k + 128 * oc: 1024 * k + 128 * (oc + 1)] for k in range(4)],
                                 [c_qT[:, 2048 * k + 512 * ts: 2048 * k + 512 * (ts + 1)] for k in range(4)],
                                 bqk_sb[:, oc:oc + 1])

                tok_r = ckv_tok.ap().rearrange("p (s n) -> p s n", s=32)
                p.wait("tensor", sDVE)
                for gq in range(8):
                    pt = pTs[gq % 2]
                    p.wait("tensor", sDVE)
                    for j in range(4):
                        st = 4 * gq + j
                        p.op("tensor", lambda e, st=st, j=j, pt=pt: e.transpose(
                            pt[:, 64 * j:64 * j + 64],
                            ckvT[0:64, 128 * st:128 * (st + 1)],
                            ident_bf[0:64, 0:64]),
                            incs=(sPE, 1) if j == 3 else ())
                    p.wait("vector", sPE)
                    p.op("vector", lambda e, gq=gq, pt=pt: e.tensor_copy(
                        tok_r[:, 4 * gq:4 * (gq + 1), 0:64],
                        pt.ap().rearrange("p (c n) -> p c n", c=16)[:, 0:4, 0:64]),
                        incs=(sDVE, 1))
                p.op("vector", lambda e: e.memset(tok_r[:, :, 64:65], 1.0), incs=(sDVE, 1))

                p.wait("vector", sACT)
                p.op("vector", lambda e: e.tensor_tensor(ksq[:, :], ckvT[0:64, :], ckvT[0:64, :], ALU.mult), incs=(sDVE, 1))
                p.wait("tensor", sDVE)
                for rnd in range(2):
                    for sl in range(4):
                        p.op("tensor", lambda e, rnd=rnd, sl=sl: e.matmul(
                            pq[0:1, 512 * sl:512 * (sl + 1)], halves_bf[0:64, 0:1],
                            ksq[0:64, 512 * (4 * rnd + sl):512 * (4 * rnd + sl + 1)],
                            start=True, stop=True),
                            incs=(sPE, 1) if sl == 3 else ())
                    p.wait("vector", sPE)
                    p.op("vector", lambda e, rnd=rnd: e.tensor_reduce(
                        maxk[0:1, rnd:rnd + 1], pq[0:1, :], AX.X, ALU.max), incs=(sDVE, 1))
                    p.wait("tensor", sDVE)
                p.wait("vector", sDVE)
                p.op("vector", lambda e: e.tensor_tensor(
                    maxk[0:1, 0:1], maxk[0:1, 0:1], maxk[0:1, 1:2], ALU.max), incs=(sDVE, 1))

                for h in range(16):
                    c, half = h // 2, h % 2
                    p.wait("vector", sDVE)
                    p.op("vector", lambda e, c=c: e.tensor_tensor(
                        qsq[:, :], qT[:, 2048 * c:2048 * (c + 1)], qT[:, 2048 * c:2048 * (c + 1)], ALU.mult),
                        incs=(sDVE, 1))
                    p.wait("tensor", sDVE)
                    for ts in range(4):
                        p.op("tensor", lambda e, ts=ts, half=half: e.matmul(
                            pq[0:1, 512 * ts:512 * (ts + 1)],
                            halves_bf[:, half:half + 1], qsq[:, 512 * ts:512 * (ts + 1)],
                            start=True, stop=True),
                            incs=(sPE, 1) if ts == 3 else ())
                    p.wait("vector", sPE)
                    p.op("vector", lambda e, h=h: e.tensor_reduce(
                        maxq[0:1, h:h + 1], pq[0:1, :], AX.X, ALU.max), incs=(sDVE, 1))
                    p.wait("tensor", sDVE)

                p.wait("vector", sDVE)
                p.op("vector", lambda e: e.tensor_scalar(
                    bndrow[:, :], maxq[:, :], maxk[0:1, 0:1], None, ALU.mult), incs=(sDVE, 1))
                p.wait("scalar", sDVE)
                p.op("scalar", lambda e: e.activation(bndrow[:, :], bndrow[:, :], AF.Ln, bias=epsc[0:1, 0:1], scale=1.0), incs=(sACT, 1))
                p.wait("scalar", sACT)
                p.op("scalar", lambda e: e.activation(bndrow[:, :], bndrow[:, :], AF.Exp, bias=0.0, scale=0.5), incs=(sACT, 1))
                p.wait("vector", sACT)
                p.op("vector", lambda e: e.tensor_scalar(
                    bndrow[:, :], bndrow[:, :], -0.13125, -0.5, ALU.mult, ALU.add), incs=(sDVE, 1))
                p.wait("vector", sDVE)
                p.op("vector", lambda e: e.tensor_copy(bnd_bf[:, :], bndrow[:, :]), incs=(sDVE, 1))
                p.wait("tensor", sDVE)
                p.op("tensor", lambda e: e.matmul(
                    pmA[0:128, 0:16], onesr_bf[:, :], bnd_bf[:, :], start=True, stop=True), incs=(sPE, 1))
                p.wait("vector", sPE)
                p.op("vector", lambda e: e.tensor_copy(negbnd[:, :], pmA[0:128, 0:16]), incs=(sDVE, 1))
                p.flush()

        sums16 = sb("sums16", [16, TC], F32)
        rec_hi = sb("rec_hi", [16, TC], BF16)
        rec_lo = sb("rec_lo", [16, TC], BF16)
        yallT = sb("yallT", [128, 8 * TC], BF16)

        groups = [(0, 4), (4, 2), (6, 4), (10, 2), (12, 4), (16, 2), (18, 4),
                  (22, 2), (24, 4), (28, 2), (30, 2)]
        NG = len(groups)

        with nc.sbuf_tensor("stripA", [128, 2048], BF16) as stripA, \
             nc.sbuf_tensor("stripB", [128, 1024], BF16) as stripB, \
             nc.sbuf_tensor("zsb0", [65, 512], BF16) as zsb0, \
             nc.sbuf_tensor("zsb1", [65, 512], BF16) as zsb1, \
             nc.psum_tensor("psA", [128, 2048], F32) as psA, \
             nc.psum_tensor("psB", [128, 1024], F32) as psB, \
             nc.psum_tensor("psZ0", [65, 512], F32) as psZ0, \
             nc.psum_tensor("psZ1", [65, 512], F32) as psZ1:
            zsbs = [zsb0, zsb1]
            psG = [psA, psB]
            psZ = [psZ0, psZ1]
            tok_r = ckv_tok.ap().rearrange("p (s n) -> p s n", s=32)

            last_exp = [0, 0]
            scores_done = {}
            exp_done = {}
            yraw_done = [0, 0]
            srow_dma = [0, 0]
            prev_iter = [None]

            p.wait("tensor", sACT)
            p.wait("tensor", sDVE)
            p.wait("scalar", sDVE)

            def emit_scores(h, t5, gi):
                s0, glen = groups[gi]
                c, qrow = h // 2, (h % 2) * 64
                par = gi % 2
                pg = psG[par]
                p.wait("tensor", sACT, last_exp[par])
                for j in range(glen):
                    st = s0 + j
                    p.op("tensor", lambda e, st=st, j=j, pg=pg, c=c, qrow=qrow, t5=t5: e.matmul(
                        pg[:, 512 * j:512 * (j + 1)],
                        ckvT[qrow:qrow + 64, 128 * st:128 * (st + 1)],
                        qT[qrow:qrow + 64, 2048 * c + 512 * t5: 2048 * c + 512 * (t5 + 1)],
                        start=True, stop=True),
                        incs=(sPE, 1) if j == glen - 1 else ())
                scores_done[gi] = sPE.n

            def emit_exp(h, t5, gi):
                s0, glen = groups[gi]
                par = gi % 2
                pg = psG[par]
                strip = stripA if par == 0 else stripB
                p.wait("scalar", sPE, scores_done[gi])
                p.op("scalar", lambda e, pg=pg, strip=strip, glen=glen, h=h: e.activation(
                    strip[:, 0:512 * glen], pg[:, 0:512 * glen], AF.Exp,
                    bias=negbnd[:, h:h + 1], scale=0.125),
                    incs=(sACT, 1))
                exp_done[gi] = sACT.n
                last_exp[par] = sACT.n

            def emit_stage1(h, t5, gi, zp):
                s0, glen = groups[gi]
                par = gi % 2
                strip = stripA if par == 0 else stripB
                p.wait("tensor", sACT, exp_done[gi])
                if gi == 0:
                    p.wait("tensor", sDVE, yraw_done[zp])
                for j in range(glen):
                    st = s0 + j
                    p.op("tensor", lambda e, st=st, j=j, strip=strip, zp=zp,
                         first=(gi == 0 and j == 0), last=(gi == NG - 1 and j == glen - 1): e.matmul(
                        psZ[zp][:, :], tok_r[:, st, :], strip[:, 512 * j:512 * (j + 1)],
                        start=first, stop=last),
                        incs=(sPE, 1) if j == glen - 1 else ())

            def emit_tail(h, t5, zp):
                p.wait("scalar", sPE)
                p.op("scalar", lambda e, zp=zp: e.activation(
                    zsbs[zp][:, :], psZ[zp][:, :], AF.Copy, bias=0.0, scale=1.0),
                    incs=(sACT, 1))
                a_after = sACT.n
                slot = (4 * h + t5) % 2
                p.wait("vector", sACT, a_after)
                p.wait("vector", sST, srow_dma[slot])
                p.op("vector", lambda e, zp=zp, slot=slot: e.tensor_copy(
                    srow[0:1, 512 * slot:512 * (slot + 1)], psZ[zp][64:65, :]),
                    incs=(sDVE, 1))
                d_after = sDVE.n
                p.wait("sync", sDVE, d_after)
                p.dma("sync", sums16[h:h + 1, 512 * t5:512 * (t5 + 1)],
                      srow[0:1, 512 * slot:512 * (slot + 1)], sST)
                srow_dma[slot] = sST.n
                return d_after, a_after

            def emit_stage2(info):
                zp, h, t5, d_after, a_after = info
                c, qrow = h // 2, (h % 2) * 64
                p.wait("tensor", sDVE, d_after)
                p.wait("tensor", sACT, a_after)
                p.op("tensor", lambda e, zp=zp, h=h: e.matmul(
                    psZ[zp][0:64, :], wv_uT[:, 64 * h:64 * (h + 1)], zsbs[zp][:, :],
                    start=True, stop=True), incs=(sPE, 1))
                pe_after = sPE.n
                p.wait("vector", sPE, pe_after)
                p.op("vector", lambda e, zp=zp, c=c, qrow=qrow, t5=t5: e.tensor_copy(
                    yallT[qrow:qrow + 64, 2048 * c + 512 * t5: 2048 * c + 512 * (t5 + 1)],
                    psZ[zp][0:64, :]), incs=(sDVE, 1))
                yraw_done[zp] = sDVE.n

            for h in range(16):
                for t5 in range(4):
                    it = 4 * h + t5
                    zp = it % 2
                    emit_scores(h, t5, 0)
                    for gi in range(NG):
                        if gi + 1 < NG:
                            emit_scores(h, t5, gi + 1)
                        if gi == 0 and prev_iter[0] is not None:
                            emit_stage2(prev_iter[0])
                            prev_iter[0] = None
                        emit_exp(h, t5, gi)
                        emit_stage1(h, t5, gi, zp)
                    prev_iter[0] = (zp, h, t5) + emit_tail(h, t5, zp)
            emit_stage2(prev_iter[0])

            p.wait("vector", sST)
            p.op("vector", lambda e: e.reciprocal(sums16[:, :], sums16[:, :]), incs=(sDVE, 1))
            p.wait("vector", sDVE)
            p.op("vector", lambda e: e.tensor_copy(rec_hi[:, :], sums16[:, :]), incs=(sDVE, 1))
            p.wait("vector", sDVE)
            p.op("vector", lambda e: e.tensor_tensor(sums16[:, :], sums16[:, :], rec_hi[:, :], ALU.subtract), incs=(sDVE, 1))
            p.wait("vector", sDVE)
            p.op("vector", lambda e: e.tensor_copy(rec_lo[:, :], sums16[:, :]), incs=(sDVE, 1))
            p.flush()

        with nc.sbuf_tensor("xs", [128, 16 * 1024], F32) as xs, \
             nc.sbuf_tensor("scr2", [128, 512], F32) as scr2, \
             nc.sbuf_tensor("sA2a", [128, 16], F32) as sA2a, \
             nc.sbuf_tensor("sA2b", [128, 16], F32) as sA2b, \
             nc.sbuf_tensor("sQ2a", [128, 16], F32) as sQ2a, \
             nc.sbuf_tensor("sQ2b", [128, 16], F32) as sQ2b, \
             nc.sbuf_tensor("lnx", [128, 8 * 16], F32) as lnx, \
             nc.psum_tensor("pS0", [128, 512], F32) as pS0, \
             nc.psum_tensor("pS1", [128, 512], F32) as pS1, \
             nc.psum_tensor("pP0", [128, 512], F32) as pP0, \
             nc.psum_tensor("pP1", [128, 512], F32) as pP1:
            lnsum = lnx.ap()[:, 0:16]
            lnsq = lnx.ap()[:, 16:32]
            lnmean = lnx.ap()[:, 32:48]
            lnmsq = lnx.ap()[:, 48:64]
            lnvar = lnx.ap()[:, 64:80]
            lnrstd = lnx.ap()[:, 80:96]
            lnnm = lnx.ap()[:, 96:112]

            for i in range(16):
                p.dma("sync", xs[:, 1024 * i:1024 * (i + 1)], x_own[128 * i:128 * (i + 1), :], sLD)
            xs_loaded = sLD.n

            pSs = [pS0, pS1]
            pPs = [pP0, pP1]
            p.wait("tensor", sDVE)
            scale_done = {}
            k_i = 0
            for c in range(8):
                for t5 in range(4):
                    pS = pSs[k_i % 2]
                    if k_i >= 2:
                        p.wait("tensor", sDVE, scale_done[k_i - 2])
                    p.op("tensor", lambda e, c=c, t5=t5, pS=pS: e.matmul(
                        pS[:, :], e16_bf[:, 128 * c:128 * (c + 1)], rec_hi[:, 512 * t5:512 * (t5 + 1)],
                        start=True, stop=False))
                    p.op("tensor", lambda e, c=c, t5=t5, pS=pS: e.matmul(
                        pS[:, :], e16_bf[:, 128 * c:128 * (c + 1)], rec_lo[:, 512 * t5:512 * (t5 + 1)],
                        start=False, stop=True), incs=(sPE, 1))
                    p.wait("vector", sPE)
                    p.op("vector", lambda e, c=c, t5=t5, pS=pS: e.tensor_tensor(
                        yallT[:, 2048 * c + 512 * t5: 2048 * c + 512 * (t5 + 1)],
                        yallT[:, 2048 * c + 512 * t5: 2048 * c + 512 * (t5 + 1)],
                        pS[:, :], ALU.mult), incs=(sDVE, 1))
                    scale_done[k_i] = sDVE.n
                    k_i += 1

            pr_i = 0
            prj_done = {}
            for ti in range(16):
                for cc in range(2):
                    pP = pPs[pr_i % 2]
                    if pr_i >= 2:
                        p.wait("tensor", sDVE, prj_done[pr_i - 2])
                    if pr_i == 0:
                        p.wait("tensor", sDVE)
                    for k in range(8):
                        p.op("tensor", lambda e, ti=ti, cc=cc, k=k, pP=pP: e.matmul(
                            pP[:, :], yallT[:, 2048 * k + 128 * ti: 2048 * k + 128 * (ti + 1)],
                            w_projT[:, 1024 * k + 512 * cc: 1024 * k + 512 * (cc + 1)],
                            start=(k == 0), stop=False))
                    p.op("tensor", lambda e, cc=cc, pP=pP: e.matmul(
                        pP[:, :], onesr_bf[:, :], bprj_hi[:, 512 * cc:512 * (cc + 1)],
                        start=False, stop=False))
                    p.op("tensor", lambda e, cc=cc, pP=pP: e.matmul(
                        pP[:, :], onesr_bf[:, :], bprj_lo[:, 512 * cc:512 * (cc + 1)],
                        start=False, stop=True), incs=(sPE, 1))
                    p.wait("vector", sPE)
                    if pr_i == 0:
                        p.wait("vector", sLD, xs_loaded)
                    sAd = sA2a if cc == 0 else sA2b
                    sQd = sQ2a if cc == 0 else sQ2b
                    p.op("vector", lambda e, ti=ti, cc=cc, pP=pP: e.tensor_tensor(
                        xs[:, 1024 * ti + 512 * cc: 1024 * ti + 512 * (cc + 1)],
                        pP[:, :], xs[:, 1024 * ti + 512 * cc: 1024 * ti + 512 * (cc + 1)],
                        ALU.add), incs=(sDVE, 1))
                    prj_done[pr_i] = sDVE.n
                    pr_i += 1
                    p.wait("scalar", sDVE, prj_done[pr_i - 1])
                    p.op("scalar", lambda e, ti=ti, cc=cc, sAd=sAd: e.activation(
                        scr2[:, :], xs[:, 1024 * ti + 512 * cc: 1024 * ti + 512 * (cc + 1)],
                        AF.Copy, bias=0.0, scale=1.0,
                        accum_out=sAd[:, ti:ti + 1]), incs=(sACT, 1))
                    p.op("scalar", lambda e, ti=ti, cc=cc, sQd=sQd: e.activation(
                        scr2[:, :], xs[:, 1024 * ti + 512 * cc: 1024 * ti + 512 * (cc + 1)],
                        AF.Square, bias=0.0, scale=1.0,
                        accum_out=sQd[:, ti:ti + 1]), incs=(sACT, 1))

            p.wait("vector", sDVE)
            p.wait("vector", sACT)
            p.op("vector", lambda e: e.tensor_tensor(lnsum, sA2a[:, :], sA2b[:, :], ALU.add), incs=(sDVE, 1))
            p.op("vector", lambda e: e.tensor_tensor(lnsq, sQ2a[:, :], sQ2b[:, :], ALU.add), incs=(sDVE, 1))
            p.wait("vector", sDVE)
            p.op("vector", lambda e: e.tensor_scalar(lnmean, lnsum, 1.0 / D, None, ALU.mult), incs=(sDVE, 1))
            p.wait("vector", sDVE)
            p.op("vector", lambda e: e.tensor_tensor(lnmsq, lnmean, lnmean, ALU.mult), incs=(sDVE, 1))
            p.op("vector", lambda e: e.tensor_scalar(lnvar, lnsq, 1.0 / D, None, ALU.mult), incs=(sDVE, 1))
            p.wait("vector", sDVE)
            p.op("vector", lambda e: e.tensor_tensor(lnvar, lnvar, lnmsq, ALU.subtract), incs=(sDVE, 1))
            p.wait("scalar", sDVE)
            p.op("scalar", lambda e: e.activation(lnvar, lnvar, AF.Ln, bias=epsc[:, 1:2], scale=1.0), incs=(sACT, 1))
            p.wait("scalar", sACT)
            p.op("scalar", lambda e: e.activation(lnrstd, lnvar, AF.Exp, bias=0.0, scale=-0.5), incs=(sACT, 1))
            p.wait("vector", sACT)
            p.op("vector", lambda e: e.tensor_scalar(lnnm, lnmean, -1.0, None, ALU.mult), incs=(sDVE, 1))
            p.wait("vector", sDVE)
            for ti in range(16):
                p.op("vector", lambda e, ti=ti: e.tensor_scalar(
                    xs[:, 1024 * ti:1024 * (ti + 1)], xs[:, 1024 * ti:1024 * (ti + 1)],
                    lnnm[:, ti:ti + 1], lnrstd[:, ti:ti + 1], ALU.add, ALU.mult), incs=(sDVE, 1))
                dv = sDVE.n
                p.wait("sync", sDVE, dv)
                p.dma("sync", x1_out[128 * ti:128 * (ti + 1), :], xs[:, 1024 * ti:1024 * (ti + 1)], sST)
            p.wait("sync", sST)
            p.flush()

    return nc



def build_launch2():
    nc = bass.Bass("TRN2", target_bir_lowering=False, detect_race_conditions=False)
    dt = nc.dram_tensor
    x1_in = dt("x1", [TC, D], F32, kind="ExternalInput").ap()
    wg_st = dt("wg_st", [128, 64], F32, kind="ExternalInput").ap()
    bg_bc = dt("bg_bc", [128, 8], F32, kind="ExternalInput").ap()
    w1 = dt("w1", [8, D, D], F32, kind="ExternalInput").ap()
    w2 = dt("w2", [8, D, D], F32, kind="ExternalInput").ap()
    b1cols = dt("b1cols", [128, 64], F32, kind="ExternalInput").ap()
    b2m = dt("b2m", [8, D], F32, kind="ExternalInput").ap()
    ones_row = dt("ones_row", [1, 128], F32, kind="ExternalInput").ap()
    ident_in = dt("ident_in", [128, 128], F32, kind="ExternalInput").ap()
    x2_out = dt("x2", [TC, D], F32, kind="ExternalOutput").ap()

    with contextlib.ExitStack() as ctx:
        p = Prog(nc, ctx)
        sLD = p.dsem("sLD")
        sPE = p.sem("sPE")
        sACT = p.sem("sACT")
        sDVE = p.sem("sDVE")
        sST = p.dsem("sST")

        def sb(name, shape, dtyp):
            return ctx.enter_context(nc.sbuf_tensor(name, shape, dtyp))

        g = "gpsimd"
        onesr_bf = sb("onesr_bf", [1, 128], BF16)
        ident_bf = sb("ident_bf", [128, 128], BF16)
        epsc = sb("epsc", [128, 2], F32)
        bg_sb = sb("bg_sb", [128, 8], F32)
        b1_sb = sb("b1_sb", [128, 64], F32)
        b2f = sb("b2f", [8, D], F32)
        b2hi = sb("b2hi", [8, D], BF16)
        b2lo = sb("b2lo", [8, D], BF16)
        wgf = sb("wgf", [128, 64], F32)
        wghi = sb("wghi", [128, 64], BF16)
        wglo = sb("wglo", [128, 64], BF16)
        x1T = sb("x1T", [128, 8 * TC], BF16)
        maskF = sb("maskF", [128, 16 * 8], F32)
        m8 = sb("m8", [8, TC], BF16)
        acc = sb("acc", [128, 16 * 1024], F32)

        p.dma(g, onesr_bf[:, :], ones_row, sLD)
        p.dma(g, ident_bf[:, :], ident_in, sLD)
        p.dma("sync", bg_sb[:, :], bg_bc, sLD)
        p.dma("sync", b1_sb[:, :], b1cols, sLD)
        p.dma("sync", b2f[:, :], b2m, sLD)
        p.dma("sync", wgf[:, :], wg_st, sLD)
        n_pre = sLD.n

        p.wait("scalar", sLD, n_pre)
        p.op("vector", lambda e: e.memset(epsc[:, 0:1], 1e-20), incs=(sDVE, 1))
        p.op("vector", lambda e: e.memset(epsc[:, 1:2], 1e-5), incs=(sDVE, 1))
        p.wait("vector", sLD, n_pre)
        p.op("vector", lambda e: e.tensor_copy(b2hi[:, :], b2f[:, :]), incs=(sDVE, 1))
        p.op("vector", lambda e: e.tensor_copy(wghi[:, :], wgf[:, :]), incs=(sDVE, 1))
        p.wait("vector", sDVE)
        p.op("vector", lambda e: e.tensor_tensor(b2f[:, :], b2f[:, :], b2hi[:, :], ALU.subtract), incs=(sDVE, 1))
        p.op("vector", lambda e: e.tensor_tensor(wgf[:, :], wgf[:, :], wghi[:, :], ALU.subtract), incs=(sDVE, 1))
        p.wait("vector", sDVE)
        p.op("vector", lambda e: e.tensor_copy(b2lo[:, :], b2f[:, :]), incs=(sDVE, 1))
        p.op("vector", lambda e: e.tensor_copy(wglo[:, :], wgf[:, :]), incs=(sDVE, 1))

        with nc.sbuf_tensor("xloT", [128, 8 * TC], BF16) as xloT, \
             nc.sbuf_tensor("stgA", [128, 1024], BF16) as stgA, \
             nc.sbuf_tensor("stgB", [128, 1024], BF16) as stgB, \
             nc.sbuf_tensor("stgF", [128, 1024], F32) as stgF, \
             nc.sbuf_tensor("rt0", [128, 8], F32) as rt0, \
             nc.sbuf_tensor("rt1", [128, 8], F32) as rt1, \
             nc.sbuf_tensor("rt2", [128, 8], F32) as rt2, \
             nc.sbuf_tensor("rmx", [128, 1], F32) as rmx, \
             nc.sbuf_tensor("mbf", [128, 8], BF16) as mbf, \
             nc.psum_tensor("pT0", [128, 1024], BF16) as pT0, \
             nc.psum_tensor("pT1", [128, 1024], BF16) as pT1, \
             nc.psum_tensor("pL", [128, 8], F32) as pL, \
             nc.psum_tensor("pM8", [8, 128], BF16) as pM8:
            pTs = [pT0, pT1]
            x1T_r = x1T.ap().rearrange("p (c n) -> p c n", c=8)
            xloT_r = xloT.ap().rearrange("p (c n) -> p c n", c=8)
            for i in range(16):
                stg = stgA if i % 2 == 0 else stgB
                pt = pTs[i % 2]
                p.wait(g, sPE)
                p.wait(g, sDVE)
                p.dma(g, stg[:, :], x1_in[128 * i:128 * (i + 1), :], sLD)
                p.wait("sync", sDVE)
                p.dma("sync", stgF[:, :], x1_in[128 * i:128 * (i + 1), :], sLD)
                p.wait("tensor", sLD)
                p.wait("tensor", sDVE)
                for c in range(8):
                    p.op("tensor", lambda e, stg=stg, pt=pt, c=c: e.transpose(
                        pt[:, 128 * c:128 * (c + 1)], stg[:, 128 * c:128 * (c + 1)],
                        ident_bf[:, :]), incs=(sPE, 1) if c == 7 else ())
                p.wait("vector", sPE)
                p.op("vector", lambda e, i=i, pt=pt: e.tensor_copy(
                    x1T_r[:, :, 128 * i:128 * (i + 1)],
                    pt.ap().rearrange("p (c n) -> p c n", c=8)), incs=(sDVE, 1))
                p.wait("vector", sLD)
                p.op("vector", lambda e, stg=stg: e.tensor_tensor(
                    stgF[:, :], stgF[:, :], stg[:, :], ALU.subtract), incs=(sDVE, 1))
                p.wait("vector", sDVE)
                p.wait("vector", sPE)
                p.op("vector", lambda e, stg=stg: e.tensor_copy(stg[:, :], stgF[:, :]), incs=(sDVE, 1))
                dv_lo = sDVE.n
                p.wait("tensor", sDVE, dv_lo)
                for c in range(8):
                    p.op("tensor", lambda e, stg=stg, pt=pt, c=c: e.transpose(
                        pt[:, 128 * c:128 * (c + 1)], stg[:, 128 * c:128 * (c + 1)],
                        ident_bf[:, :]), incs=(sPE, 1) if c == 7 else ())
                p.wait("vector", sPE)
                p.op("vector", lambda e, i=i, pt=pt: e.tensor_copy(
                    xloT_r[:, :, 128 * i:128 * (i + 1)],
                    pt.ap().rearrange("p (c n) -> p c n", c=8)), incs=(sDVE, 1))

            wg_r = wghi.ap().rearrange("p (c n) -> p c n", c=8)
            wgl_r = wglo.ap().rearrange("p (c n) -> p c n", c=8)
            p.wait("tensor", sDVE)
            for ti in range(16):
                nmm = [0]
                p.wait("tensor", sDVE)
                for k in range(8):
                    for lh, rh in ((x1T_r[:, k, 128 * ti:128 * (ti + 1)], wg_r[:, k, :]),
                                   (x1T_r[:, k, 128 * ti:128 * (ti + 1)], wgl_r[:, k, :]),
                                   (xloT_r[:, k, 128 * ti:128 * (ti + 1)], wg_r[:, k, :])):
                        first, last = nmm[0] == 0, nmm[0] == 23
                        p.op("tensor", lambda e, lh=lh, rh=rh, first=first, last=last: e.matmul(
                            pL[:, :], lh, rh, start=first, stop=last),
                            incs=(sPE, 1) if last else ())
                        nmm[0] += 1
                p.wait("vector", sPE)
                p.op("vector", lambda e: e.tensor_tensor(rt0[:, :], pL[:, :], bg_sb[:, :], ALU.add), incs=(sDVE, 1))
                p.wait("vector", sDVE)
                p.op("vector", lambda e: e.tensor_reduce(rmx[:, :], rt0[:, :], AX.X, ALU.max), incs=(sDVE, 1))
                p.wait("vector", sDVE)
                p.op("vector", lambda e: e.tensor_scalar(rt1[:, :], rt0[:, :], rmx[:, :], None, ALU.is_ge), incs=(sDVE, 1))
                p.wait("vector", sDVE)
                p.op("vector", lambda e: e.tensor_copy(rt2[:, 0:1], rt1[:, 0:1]), incs=(sDVE, 1))
                p.op("vector", lambda e: e.tensor_tensor(rt2[:, 1:8], rt1[:, 1:8], rt1[:, 0:7], ALU.add), incs=(sDVE, 1))
                p.wait("vector", sDVE)
                p.op("vector", lambda e: e.tensor_copy(rt0[:, 0:2], rt2[:, 0:2]), incs=(sDVE, 1))
                p.op("vector", lambda e: e.tensor_tensor(rt0[:, 2:8], rt2[:, 2:8], rt2[:, 0:6], ALU.add), incs=(sDVE, 1))
                p.wait("vector", sDVE)
                p.op("vector", lambda e: e.tensor_copy(rt2[:, 0:4], rt0[:, 0:4]), incs=(sDVE, 1))
                p.op("vector", lambda e: e.tensor_tensor(rt2[:, 4:8], rt0[:, 4:8], rt0[:, 0:4], ALU.add), incs=(sDVE, 1))
                p.wait("vector", sDVE)
                p.op("vector", lambda e: e.tensor_scalar(rt0[:, :], rt2[:, :], 1.5, None, ALU.is_le), incs=(sDVE, 1))
                p.wait("vector", sDVE)
                p.op("vector", lambda e, ti=ti: e.tensor_tensor(
                    maskF[:, 8 * ti:8 * (ti + 1)], rt1[:, :], rt0[:, :], ALU.mult), incs=(sDVE, 1))
                p.wait("vector", sDVE)
                p.op("vector", lambda e, ti=ti: e.tensor_copy(mbf[:, :], maskF[:, 8 * ti:8 * (ti + 1)]), incs=(sDVE, 1))
                dvm = sDVE.n
                p.wait("tensor", sDVE, dvm)
                p.wait("tensor", sDVE, dvm)
                p.op("tensor", lambda e: e.transpose(pM8[:, :], mbf[:, :], ident_bf[:, :]), incs=(sPE, 1))
                p.wait("vector", sPE)
                p.op("vector", lambda e, ti=ti: e.tensor_copy(m8[:, 128 * ti:128 * (ti + 1)], pM8[:, :]), incs=(sDVE, 1))
            p.flush()

        with nc.sbuf_tensor("w1sb", [128, 8 * D], BF16) as w1sb, \
             nc.sbuf_tensor("w2sb", [128, 8 * D], BF16) as w2sb, \
             nc.sbuf_tensor("h_e", [128, 8 * TC], BF16) as h_e, \
             nc.psum_tensor("pMa", [128, 1024], F32) as pMa, \
             nc.psum_tensor("pMb", [128, 1024], F32) as pMb, \
             nc.psum_tensor("pY0", [128, 1024], F32) as pY0, \
             nc.psum_tensor("pY1", [128, 1024], F32) as pY1:
            pYs = [pY0, pY1]
            pMs = [pMa, pMb]

            p.wait("tensor", sDVE)
            py_free = [0, 0]
            for ti in range(16):
                pY = pYs[ti % 2]
                p.wait("tensor", sDVE, py_free[ti % 2])
                for cc in range(2):
                    p.op("tensor", lambda e, ti=ti, cc=cc, pY=pY: e.matmul(
                        pY[:, 512 * cc:512 * (cc + 1)], m8[:, 128 * ti:128 * (ti + 1)],
                        b2hi[:, 512 * cc:512 * (cc + 1)], start=True, stop=False))
                    p.op("tensor", lambda e, ti=ti, cc=cc, pY=pY: e.matmul(
                        pY[:, 512 * cc:512 * (cc + 1)], m8[:, 128 * ti:128 * (ti + 1)],
                        b2lo[:, 512 * cc:512 * (cc + 1)], start=False, stop=True),
                        incs=(sPE, 1) if cc == 1 else ())
                p.wait("vector", sPE)
                p.op("vector", lambda e, ti=ti, pY=pY: e.tensor_copy(
                    acc[:, 1024 * ti:1024 * (ti + 1)], pY[:, :]), incs=(sDVE, 1))
                py_free[ti % 2] = sDVE.n

            gelu_done = {}
            gl_i = [0]
            pe_after_e = [0]
            for e_i in range(8):
                p.wait(g, sPE, pe_after_e[0])
                for k in range(8):
                    p.dma(g, w1sb[:, 1024 * k:1024 * (k + 1)], w1[e_i, 128 * k:128 * (k + 1), :], sLD)
                for k in range(8):
                    p.dma(g, w2sb[:, 1024 * k:1024 * (k + 1)], w2[e_i, 128 * k:128 * (k + 1), :], sLD)
                wld = sLD.n

                for ic in range(8):
                    for th in range(2):
                        u = gl_i[0]
                        pM = pMs[u % 2]
                        p.wait("tensor", sACT, gelu_done.get(u - 2, 0))
                        if ic == 0 and th == 0:
                            p.wait("tensor", sLD, wld)
                            p.wait("tensor", sPE, pe_after_e[0])
                        for t5 in range(2):
                            for k in range(8):
                                p.op("tensor", lambda e, ic=ic, th=th, t5=t5, k=k, pM=pM: e.matmul(
                                    pM[:, 512 * t5:512 * (t5 + 1)],
                                    w1sb[:, 1024 * k + 128 * ic:1024 * k + 128 * (ic + 1)],
                                    x1T[:, 2048 * k + 1024 * th + 512 * t5:2048 * k + 1024 * th + 512 * (t5 + 1)],
                                    start=(k == 0), stop=(k == 7)),
                                    incs=(sPE, 1) if (t5 == 1 and k == 7) else ())
                        p.wait("scalar", sPE)
                        p.op("scalar", lambda e, e_i=e_i, ic=ic, th=th, pM=pM: e.activation(
                            h_e[:, 2048 * ic + 1024 * th:2048 * ic + 1024 * (th + 1)],
                            pM[:, :], AF.Gelu, bias=b1_sb[:, 8 * e_i + ic:8 * e_i + ic + 1],
                            scale=1.0), incs=(sACT, 1))
                        gelu_done[u] = sACT.n
                        gl_i[0] += 1

                a_need = sACT.n
                for ti in range(16):
                    pY = pYs[ti % 2]
                    p.wait("tensor", sDVE, py_free[ti % 2])
                    if ti == 0:
                        p.wait("tensor", sACT, a_need)
                    for ic in range(8):
                        for cc in range(2):
                            p.op("tensor", lambda e, ti=ti, ic=ic, cc=cc, pY=pY: e.matmul(
                                pY[:, 512 * cc:512 * (cc + 1)],
                                h_e[:, 2048 * ic + 128 * ti:2048 * ic + 128 * (ti + 1)],
                                w2sb[:, 1024 * ic + 512 * cc:1024 * ic + 512 * (cc + 1)],
                                start=(ic == 0), stop=(ic == 7)),
                                incs=(sPE, 1) if (ic == 7 and cc == 1) else ())
                    p.wait("vector", sPE)
                    p.op("vector", lambda e, ti=ti, e_i=e_i, pY=pY: e.scalar_tensor_tensor(
                        acc[:, 1024 * ti:1024 * (ti + 1)], pY[:, :],
                        maskF[:, 8 * ti + e_i:8 * ti + e_i + 1],
                        acc[:, 1024 * ti:1024 * (ti + 1)],
                        ALU.mult, ALU.add), incs=(sDVE, 1))
                    py_free[ti % 2] = sDVE.n
                pe_after_e[0] = sPE.n
            p.flush()

        with nc.sbuf_tensor("xs", [128, 16 * 1024], F32) as xs, \
             nc.sbuf_tensor("scr2", [128, 1024], F32) as scr2, \
             nc.sbuf_tensor("sA2a", [128, 16], F32) as sA2a, \
             nc.sbuf_tensor("sQ2a", [128, 16], F32) as sQ2a, \
             nc.sbuf_tensor("lnx", [128, 8 * 16], F32) as lnx:
            lnsum = lnx.ap()[:, 0:16]
            lnsq = lnx.ap()[:, 16:32]
            lnmean = lnx.ap()[:, 32:48]
            lnmsq = lnx.ap()[:, 48:64]
            lnvar = lnx.ap()[:, 64:80]
            lnrstd = lnx.ap()[:, 80:96]
            lnnm = lnx.ap()[:, 96:112]

            for i in range(16):
                p.dma("sync", xs[:, 1024 * i:1024 * (i + 1)], x1_in[128 * i:128 * (i + 1), :], sLD)
            xs_loaded = sLD.n
            p.wait("vector", sLD, xs_loaded)
            p.wait("vector", sDVE)
            for ti in range(16):
                p.op("vector", lambda e, ti=ti: e.tensor_tensor(
                    xs[:, 1024 * ti:1024 * (ti + 1)], acc[:, 1024 * ti:1024 * (ti + 1)],
                    xs[:, 1024 * ti:1024 * (ti + 1)], ALU.add), incs=(sDVE, 1))
                dvr = sDVE.n
                p.wait("scalar", sDVE, dvr)
                p.op("scalar", lambda e, ti=ti: e.activation(
                    scr2[:, :], xs[:, 1024 * ti:1024 * (ti + 1)], AF.Copy, bias=0.0,
                    scale=1.0, accum_out=sA2a[:, ti:ti + 1]), incs=(sACT, 1))
                p.op("scalar", lambda e, ti=ti: e.activation(
                    scr2[:, :], xs[:, 1024 * ti:1024 * (ti + 1)], AF.Square, bias=0.0,
                    scale=1.0, accum_out=sQ2a[:, ti:ti + 1]), incs=(sACT, 1))
            p.wait("vector", sACT)
            p.op("vector", lambda e: e.tensor_scalar(lnmean, sA2a[:, :], 1.0 / D, None, ALU.mult), incs=(sDVE, 1))
            p.wait("vector", sDVE)
            p.op("vector", lambda e: e.tensor_tensor(lnmsq, lnmean, lnmean, ALU.mult), incs=(sDVE, 1))
            p.op("vector", lambda e: e.tensor_scalar(lnvar, sQ2a[:, :], 1.0 / D, None, ALU.mult), incs=(sDVE, 1))
            p.wait("vector", sDVE)
            p.op("vector", lambda e: e.tensor_tensor(lnvar, lnvar, lnmsq, ALU.subtract), incs=(sDVE, 1))
            p.wait("scalar", sDVE)
            p.op("scalar", lambda e: e.activation(lnvar, lnvar, AF.Ln, bias=epsc[:, 1:2], scale=1.0), incs=(sACT, 1))
            p.wait("scalar", sACT)
            p.op("scalar", lambda e: e.activation(lnrstd, lnvar, AF.Exp, bias=0.0, scale=-0.5), incs=(sACT, 1))
            p.wait("vector", sACT)
            p.op("vector", lambda e: e.tensor_scalar(lnnm, lnmean, -1.0, None, ALU.mult), incs=(sDVE, 1))
            p.wait("vector", sDVE)
            for ti in range(16):
                p.op("vector", lambda e, ti=ti: e.tensor_scalar(
                    xs[:, 1024 * ti:1024 * (ti + 1)], xs[:, 1024 * ti:1024 * (ti + 1)],
                    lnnm[:, ti:ti + 1], lnrstd[:, ti:ti + 1], ALU.add, ALU.mult), incs=(sDVE, 1))
                dv = sDVE.n
                p.wait("sync", sDVE, dv)
                p.dma("sync", x2_out[128 * ti:128 * (ti + 1), :], xs[:, 1024 * ti:1024 * (ti + 1)], sST)
            p.wait("sync", sST)
            p.flush()

    return nc



def build_launch3():
    nc = bass.Bass("TRN2", target_bir_lowering=False, detect_race_conditions=False)
    dt = nc.dram_tensor
    x2s = dt("x2s", [T, 512], F32, kind="ExternalInput").ap()
    w_in_s = dt("w_in_s", [4, 128, 128], F32, kind="ExternalInput").ap()
    w_a_s = dt("w_a_s", [4, 128, 128], F32, kind="ExternalInput").ap()
    b_in_c = dt("b_in_c", [128, 4], F32, kind="ExternalInput").ap()
    b_a_c = dt("b_a_c", [128, 4], F32, kind="ExternalInput").ap()
    ap_c = dt("ap_c", [128, 4], F32, kind="ExternalInput").ap()
    ident_in = dt("ident_in", [128, 128], F32, kind="ExternalInput").ap()
    h_out = dt("h", [T, 512], F32, kind="ExternalOutput").ap()

    with contextlib.ExitStack() as ctx:
        p = Prog(nc, ctx)
        sLD = p.dsem("sLD")
        sPE = p.sem("sPE")
        sACT = p.sem("sACT")
        sDVE = p.sem("sDVE")
        sST = p.dsem("sST")

        def sb(name, shape, dtyp):
            return ctx.enter_context(nc.sbuf_tensor(name, shape, dtyp))

        g = "gpsimd"
        ident_f = sb("ident_f", [128, 128], F32)
        b_in_sb = sb("b_in_sb", [128, 4], F32)
        b_a_sb = sb("b_a_sb", [128, 4], F32)
        ap_sb = sb("ap_sb", [128, 4], F32)
        onec = sb("onec", [128, 1], F32)
        wi_sb = sb("wi_sb", [128, 4 * 128], BF16)
        wa_sb = sb("wa_sb", [128, 4 * 128], BF16)
        zeros = sb("zeros", [128, T], F32)
        xF = sb("xF", [128, T], F32)
        xFb = sb("xFb", [128, T], BF16)
        gx = sb("gx", [128, T], F32)
        ga = sb("ga", [128, T], F32)
        av = sb("av", [128, T], F32)
        mlt = sb("mlt", [128, T], F32)
        nx = sb("nx", [128, T], F32)
        cpb = sb("cpb", [128, T + 8], F32)
        pcl = sb("pcl", [128, T], F32)
        rp = sb("rp", [128, T], F32)
        Sb = sb("Sb", [128, T], F32)
        dcol = sb("dcol", [128, 4], F32)
        stgF = sb("stgF", [128, 512], F32)
        stg2 = sb("stg2", [128, 512], F32)

        p.dma("sync", ident_f[:, :], ident_in, sLD)
        p.dma("sync", b_in_sb[:, :], b_in_c, sLD)
        p.dma("sync", b_a_sb[:, :], b_a_c, sLD)
        p.dma("sync", ap_sb[:, :], ap_c, sLD)
        for hb in range(4):
            p.dma(g, wi_sb[:, 128 * hb:128 * (hb + 1)], w_in_s[hb], sLD)
            p.dma(g, wa_sb[:, 128 * hb:128 * (hb + 1)], w_a_s[hb], sLD)
        n_pre = sLD.n
        p.op("vector", lambda e: e.memset(zeros[:, :], 0.0), incs=(sDVE, 1))
        p.op("vector", lambda e: e.memset(onec[:, :], 1.0), incs=(sDVE, 1))
        p.op("vector", lambda e: e.memset(cpb[:, 0:1], 1.0), incs=(sDVE, 1))

        p.wait("vector", sLD, n_pre)
        p.wait("scalar", sLD, n_pre)
        p.op("vector", lambda e: e.tensor_scalar(dcol[:, :], ap_sb[:, :], -1.0, None, ALU.mult), incs=(sDVE, 1))
        p.wait("vector", sDVE)
        p.op("vector", lambda e: e.tensor_tensor(dcol[:, :], dcol[:, :], ap_sb[:, :], ALU.max), incs=(sDVE, 1))
        dv = sDVE.n
        p.wait("scalar", sDVE, dv)
        p.op("scalar", lambda e: e.activation(dcol[:, :], dcol[:, :], AF.Exp, bias=0.0, scale=-1.0), incs=(sACT, 1))
        p.wait("scalar", sACT)
        p.op("scalar", lambda e: e.activation(dcol[:, :], dcol[:, :], AF.Ln, bias=onec[:, 0:1], scale=1.0), incs=(sACT, 1))
        p.wait("vector", sACT)
        p.op("vector", lambda e: e.tensor_scalar(stgF[:, 0:4], ap_sb[:, :], 0.0, None, ALU.max), incs=(sDVE, 1))
        p.wait("vector", sDVE)
        p.op("vector", lambda e: e.tensor_tensor(dcol[:, :], dcol[:, :], stgF[:, 0:4], ALU.add), incs=(sDVE, 1))
        p.wait("vector", sDVE)
        p.op("vector", lambda e: e.tensor_scalar(dcol[:, :], dcol[:, :], 1e-3, None, ALU.add), incs=(sDVE, 1))

        with nc.psum_tensor("pT0", [128, 2048], F32) as pT0, \
             nc.psum_tensor("pG1", [128, 2048], F32) as pG1:
            pGs = [pT0, pG1]
            for cc in range(4):
                p.wait("tensor", sACT)
                for tt in range(32):
                    par = tt % 2
                    stg = stgF if par == 0 else stg2
                    p.wait(g, sPE)
                    p.wait(g, sDVE)
                    p.dma(g, stg[:, 0:128], x2s[128 * tt:128 * (tt + 1), 128 * cc:128 * (cc + 1)], sLD)
                    p.wait("tensor", sLD)
                    if tt % 4 == 0 and tt > 0:
                        p.wait("tensor", sDVE)
                    p.op("tensor", lambda e, stg=stg, tt=tt: e.transpose(
                        pT0[:, 512 * (tt % 4):512 * (tt % 4) + 128],
                        stg[:, 0:128], ident_f[:, :]), incs=(sPE, 1))
                    if tt % 4 == 3:
                        p.wait("vector", sPE)
                        p.op("vector", lambda e, tt=tt: e.tensor_copy(
                            xF[:, 128 * (tt - 3):128 * (tt + 1)],
                            pT0.ap().rearrange("p (c n) -> p c n", c=4)[:, :, 0:128]),
                            incs=(sDVE, 1))
                p.wait("vector", sDVE)
                p.op("vector", lambda e: e.tensor_copy(xFb[:, :], xF[:, :]), incs=(sDVE, 1))
                dvb = sDVE.n

                for gi, (wsb, bsb, dst) in enumerate(((wi_sb, b_in_sb, gx), (wa_sb, b_a_sb, ga))):
                    for grp in range(2):
                        pG = pGs[grp]
                        p.wait("tensor", sDVE)
                        p.wait("tensor", sACT)
                        for t5 in range(4):
                            p.op("tensor", lambda e, cc=cc, wsb=wsb, t5=t5, grp=grp, pG=pG: e.matmul(
                                pG[:, 512 * t5:512 * (t5 + 1)],
                                wsb[:, 128 * cc:128 * (cc + 1)],
                                xFb[:, 2048 * grp + 512 * t5:2048 * grp + 512 * (t5 + 1)],
                                start=True, stop=True),
                                incs=(sPE, 1) if t5 == 3 else ())
                        p.wait("scalar", sPE)
                        p.op("scalar", lambda e, cc=cc, bsb=bsb, dst=dst, grp=grp, pG=pG: e.activation(
                            dst[:, 2048 * grp:2048 * (grp + 1)], pG[:, :], AF.Sigmoid,
                            bias=bsb[:, cc:cc + 1], scale=1.0), incs=(sACT, 1))

                a_done = sACT.n
                p.wait("vector", sACT, a_done)
                p.op("vector", lambda e, cc=cc: e.tensor_scalar(
                    ga[:, :], ga[:, :], dcol[:, cc:cc + 1], -1.0, ALU.mult, ALU.mult), incs=(sDVE, 1))
                dv = sDVE.n
                p.wait("scalar", sDVE, dv)
                p.op("scalar", lambda e: e.activation(av[:, :], ga[:, :], AF.Exp, bias=0.0, scale=1.0), incs=(sACT, 1))
                p.wait("vector", sACT)
                p.op("vector", lambda e: e.tensor_scalar(av[:, :], av[:, :], 0.9999899864196777, None, ALU.min), incs=(sDVE, 1))
                p.wait("vector", sDVE)
                p.op("vector", lambda e: e.tensor_tensor(mlt[:, :], av[:, :], av[:, :], ALU.mult), incs=(sDVE, 1))
                p.wait("vector", sDVE)
                p.op("vector", lambda e: e.tensor_scalar(mlt[:, :], mlt[:, :], -1.0, 1.0, ALU.mult, ALU.add), incs=(sDVE, 1))
                p.wait("vector", sDVE)
                p.op("vector", lambda e: e.tensor_scalar(mlt[:, :], mlt[:, :], 1e-5, None, ALU.max), incs=(sDVE, 1))
                dv = sDVE.n
                p.wait("scalar", sDVE, dv)
                p.op("scalar", lambda e: e.activation(mlt[:, :], mlt[:, :], AF.Ln, bias=0.0, scale=1.0), incs=(sACT, 1))
                p.wait("scalar", sACT)
                p.op("scalar", lambda e: e.activation(mlt[:, :], mlt[:, :], AF.Exp, bias=0.0, scale=0.5), incs=(sACT, 1))
                p.wait("vector", sACT)
                p.op("vector", lambda e: e.tensor_tensor(nx[:, :], gx[:, :], xF[:, :], ALU.mult), incs=(sDVE, 1))
                p.wait("vector", sDVE)
                p.op("vector", lambda e: e.tensor_tensor(nx[:, :], nx[:, :], mlt[:, :], ALU.mult), incs=(sDVE, 1))
                p.wait("vector", sDVE)
                p.op("vector", lambda e: e.tensor_tensor_scan(
                    cpb[:, 1:T + 1], av[:, :], zeros[:, :], 1.0, ALU.mult, ALU.add), incs=(sDVE, 1))
                p.wait("vector", sDVE)
                p.op("vector", lambda e: e.tensor_scalar(pcl[:, :], cpb[:, 0:T], 1e-7, None, ALU.max), incs=(sDVE, 1))
                dv = sDVE.n
                p.wait("scalar", sDVE, dv)
                p.op("scalar", lambda e: e.activation(rp[:, :], pcl[:, :], AF.Ln, bias=0.0, scale=1.0), incs=(sACT, 1))
                p.wait("scalar", sACT)
                p.op("scalar", lambda e: e.activation(rp[:, :], rp[:, :], AF.Exp, bias=0.0, scale=-1.0), incs=(sACT, 1))
                p.wait("vector", sACT)
                p.op("vector", lambda e: e.tensor_scalar(nx[:, :], nx[:, :], 1e-7, None, ALU.add), incs=(sDVE, 1))
                p.wait("vector", sDVE)
                p.op("vector", lambda e: e.tensor_tensor(nx[:, :], nx[:, :], rp[:, :], ALU.mult), incs=(sDVE, 1))
                p.wait("vector", sDVE)
                p.op("vector", lambda e: e.tensor_tensor_scan(
                    Sb[:, :], nx[:, :], zeros[:, :], 0.0, ALU.add, ALU.add), incs=(sDVE, 1))
                p.wait("vector", sDVE)
                p.op("vector", lambda e: e.tensor_tensor(Sb[:, :], Sb[:, :], pcl[:, :], ALU.mult), incs=(sDVE, 1))
                hv_done = sDVE.n

                for tt in range(32):
                    par = tt % 2
                    stg = stgF if par == 0 else stg2
                    slot = tt % 4
                    p.wait("tensor", sDVE, hv_done)
                    if tt >= 4:
                        p.wait("tensor", sDVE)
                    p.op("tensor", lambda e, tt=tt, slot=slot: e.transpose(
                        pG1[:, 512 * slot:512 * slot + 128],
                        Sb[:, 128 * tt:128 * (tt + 1)], ident_f[:, :]), incs=(sPE, 1))
                    pe = sPE.n
                    p.wait("vector", sPE, pe)
                    p.wait("vector", sST)
                    p.op("vector", lambda e, stg=stg, slot=slot: e.tensor_copy(
                        stg[:, 0:128], pG1[:, 512 * slot:512 * slot + 128]), incs=(sDVE, 1))
                    dvc = sDVE.n
                    p.wait("sync", sDVE, dvc)
                    p.dma("sync", h_out[128 * tt:128 * (tt + 1), 128 * cc:128 * (cc + 1)], stg[:, 0:128], sST)
            p.wait("sync", sST)
            p.flush()
    return nc



def build_launch4():
    nc = bass.Bass("TRN2", target_bir_lowering=False, detect_race_conditions=False)
    dt = nc.dram_tensor
    x2_in = dt("x2", [TC, D], F32, kind="ExternalInput").ap()
    h_in = dt("h", [TC, D], F32, kind="ExternalInput").ap()
    win = dt("win", [D, 64], F32, kind="ExternalInput").ap()
    bin_c = dt("bin_c", [64, 1], F32, kind="ExternalInput").ap()
    w1c = dt("w1c", [64, 16], F32, kind="ExternalInput").ap()
    b1c = dt("b1c", [64, 16], F32, kind="ExternalInput").ap()
    w2c = dt("w2c", [64, 16], F32, kind="ExternalInput").ap()
    b2c = dt("b2c", [64, 1], F32, kind="ExternalInput").ap()
    wout = dt("wout", [64, D], F32, kind="ExternalInput").ap()
    bout_row = dt("bout_row", [1, D], F32, kind="ExternalInput").ap()
    ones_row = dt("ones_row", [1, 128], F32, kind="ExternalInput").ap()
    ident_in = dt("ident_in", [128, 128], F32, kind="ExternalInput").ap()
    out_t = dt("out", [TC, D], F32, kind="ExternalOutput").ap()

    with contextlib.ExitStack() as ctx:
        p = Prog(nc, ctx)
        sLD = p.dsem("sLD")
        sPE = p.sem("sPE")
        sACT = p.sem("sACT")
        sDVE = p.sem("sDVE")
        sST = p.dsem("sST")

        def sb(name, shape, dtyp):
            return ctx.enter_context(nc.sbuf_tensor(name, shape, dtyp))

        g = "gpsimd"
        onesr_bf = sb("onesr_bf", [1, 128], BF16)
        ident_bf = sb("ident_bf", [128, 128], BF16)
        epsc = sb("epsc", [128, 2], F32)
        bin_sb = sb("bin_sb", [64, 1], F32)
        w1_sb = sb("w1_sb", [64, 16], F32)
        b1_sb = sb("b1_sb", [64, 16], F32)
        w2_sb = sb("w2_sb", [64, 16], F32)
        b2_sb = sb("b2_sb", [64, 1], F32)
        win_sb = sb("win_sb", [128, 8 * 64], BF16)
        wout_sb = sb("wout_sb", [64, D], BF16)
        bout_hi = sb("bout_hi", [1, D], BF16)
        bout_lo = sb("bout_lo", [1, D], BF16)
        brow = sb("brow", [1, D], F32)
        xs = sb("xs", [128, 16 * 1024], F32)
        x3T = sb("x3T", [128, 8 * TC], BF16)
        scr2 = sb("scr2", [128, 1024], F32)
        sA2a = sb("sA2a", [128, 16], F32)
        sQ2a = sb("sQ2a", [128, 16], F32)
        lnx = sb("lnx", [128, 8 * 16], F32)
        lnmean = lnx.ap()[:, 32:48]
        lnmsq = lnx.ap()[:, 48:64]
        lnvar = lnx.ap()[:, 64:80]
        lnrstd = lnx.ap()[:, 80:96]
        lnnm = lnx.ap()[:, 96:112]

        p.dma(g, onesr_bf[:, :], ones_row, sLD)
        p.dma(g, ident_bf[:, :], ident_in, sLD)
        p.dma("sync", bin_sb[:, :], bin_c, sLD)
        p.dma("sync", w1_sb[:, :], w1c, sLD)
        p.dma("sync", b1_sb[:, :], b1c, sLD)
        p.dma("sync", w2_sb[:, :], w2c, sLD)
        p.dma("sync", b2_sb[:, :], b2c, sLD)
        p.dma("sync", brow[:, :], bout_row, sLD)
        for k in range(8):
            p.dma(g, win_sb[:, 64 * k:64 * (k + 1)], win[128 * k:128 * (k + 1), :], sLD)
        p.dma(g, wout_sb[:, :], wout, sLD)
        n_pre = sLD.n

        p.op("vector", lambda e: e.memset(epsc[:, 0:1], 1e-20), incs=(sDVE, 1))
        p.op("vector", lambda e: e.memset(epsc[:, 1:2], 1e-5), incs=(sDVE, 1))
        p.wait("vector", sLD, n_pre)
        p.wait("scalar", sLD, n_pre)
        p.op("vector", lambda e: e.tensor_copy(bout_hi[:, :], brow[:, :]), incs=(sDVE, 1))
        p.wait("vector", sDVE)
        p.op("vector", lambda e: e.tensor_tensor(brow[:, :], brow[:, :], bout_hi[:, :], ALU.subtract), incs=(sDVE, 1))
        p.wait("vector", sDVE)
        p.op("vector", lambda e: e.tensor_copy(bout_lo[:, :], brow[:, :]), incs=(sDVE, 1))

        def emit_ln(src_wait_fn, out_idx_iter):
            for ti in out_idx_iter:
                src_wait_fn(ti)
                p.op("scalar", lambda e, ti=ti: e.activation(
                    scr2[:, :], xs[:, 1024 * ti:1024 * (ti + 1)], AF.Copy, bias=0.0,
                    scale=1.0, accum_out=sA2a[:, ti:ti + 1]), incs=(sACT, 1))
                p.op("scalar", lambda e, ti=ti: e.activation(
                    scr2[:, :], xs[:, 1024 * ti:1024 * (ti + 1)], AF.Square, bias=0.0,
                    scale=1.0, accum_out=sQ2a[:, ti:ti + 1]), incs=(sACT, 1))
            p.wait("vector", sACT)
            p.op("vector", lambda e: e.tensor_scalar(lnmean, sA2a[:, :], 1.0 / D, None, ALU.mult), incs=(sDVE, 1))
            p.wait("vector", sDVE)
            p.op("vector", lambda e: e.tensor_tensor(lnmsq, lnmean, lnmean, ALU.mult), incs=(sDVE, 1))
            p.op("vector", lambda e: e.tensor_scalar(lnvar, sQ2a[:, :], 1.0 / D, None, ALU.mult), incs=(sDVE, 1))
            p.wait("vector", sDVE)
            p.op("vector", lambda e: e.tensor_tensor(lnvar, lnvar, lnmsq, ALU.subtract), incs=(sDVE, 1))
            p.wait("scalar", sDVE)
            p.op("scalar", lambda e: e.activation(lnvar, lnvar, AF.Ln, bias=epsc[:, 1:2], scale=1.0), incs=(sACT, 1))
            p.wait("scalar", sACT)
            p.op("scalar", lambda e: e.activation(lnrstd, lnvar, AF.Exp, bias=0.0, scale=-0.5), incs=(sACT, 1))
            p.wait("vector", sACT)
            p.op("vector", lambda e: e.tensor_scalar(lnnm, lnmean, -1.0, None, ALU.mult), incs=(sDVE, 1))
            p.wait("vector", sDVE)
            for ti in range(16):
                p.op("vector", lambda e, ti=ti: e.tensor_scalar(
                    xs[:, 1024 * ti:1024 * (ti + 1)], xs[:, 1024 * ti:1024 * (ti + 1)],
                    lnnm[:, ti:ti + 1], lnrstd[:, ti:ti + 1], ALU.add, ALU.mult), incs=(sDVE, 1))

        with nc.sbuf_tensor("hs", [128, 16 * 1024], F32) as hs, \
             nc.sbuf_tensor("stgC", [128, 1024], BF16) as stgC, \
             nc.sbuf_tensor("stgD", [128, 1024], BF16) as stgD, \
             nc.psum_tensor("pT0", [128, 1024], BF16) as pT0, \
             nc.psum_tensor("pT1", [128, 1024], BF16) as pT1:
            for i in range(16):
                p.dma("sync", xs[:, 1024 * i:1024 * (i + 1)], x2_in[128 * i:128 * (i + 1), :], sLD)
                p.dma(g, hs[:, 1024 * i:1024 * (i + 1)], h_in[128 * i:128 * (i + 1), :], sLD)
            all_ld = sLD.n
            p.wait("vector", sLD, all_ld)
            res_done = {}
            for ti in range(16):
                p.op("vector", lambda e, ti=ti: e.tensor_tensor(
                    xs[:, 1024 * ti:1024 * (ti + 1)], hs[:, 1024 * ti:1024 * (ti + 1)],
                    xs[:, 1024 * ti:1024 * (ti + 1)], ALU.add), incs=(sDVE, 1))
                res_done[ti] = sDVE.n
            emit_ln(lambda ti: p.wait("scalar", sDVE, res_done[ti]), range(16))
            ln3_done = sDVE.n

            pTs = [pT0, pT1]
            for i in range(16):
                stg = stgC if i % 2 == 0 else stgD
                pt = pTs[i % 2]
                p.wait("vector", sDVE, ln3_done)
                p.wait("vector", sPE)
                p.op("vector", lambda e, i=i, stg=stg: e.tensor_copy(
                    stg[:, :], xs[:, 1024 * i:1024 * (i + 1)]), incs=(sDVE, 1))
                dvs = sDVE.n
                p.wait("tensor", sDVE, dvs)
                for c in range(8):
                    p.op("tensor", lambda e, stg=stg, pt=pt, c=c: e.transpose(
                        pt[:, 128 * c:128 * (c + 1)], stg[:, 128 * c:128 * (c + 1)],
                        ident_bf[:, :]), incs=(sPE, 1) if c == 7 else ())
                p.wait("vector", sPE)
                p.op("vector", lambda e, i=i, pt=pt: e.tensor_copy(
                    x3T.ap().rearrange("p (c n) -> p c n", c=8)[:, :, 128 * i:128 * (i + 1)],
                    pt.ap().rearrange("p (c n) -> p c n", c=8)), incs=(sDVE, 1))
            p.flush()

        with nc.sbuf_tensor("s_sb", [64, TC], F32) as s_sb, \
             nc.sbuf_tensor("hb0", [64, TC], F32) as hb0, \
             nc.sbuf_tensor("hb1", [64, TC], F32) as hb1, \
             nc.sbuf_tensor("ga0", [64, TC], F32) as ga0, \
             nc.sbuf_tensor("ga1", [64, TC], F32) as ga1, \
             nc.sbuf_tensor("gbf", [64, TC], BF16) as gbf, \
             nc.psum_tensor("pS0", [64, 512], F32) as pS0, \
             nc.psum_tensor("pS1", [64, 512], F32) as pS1, \
             nc.psum_tensor("pO0", [128, 1024], F32) as pO0, \
             nc.psum_tensor("pO1", [128, 1024], F32) as pO1:
            pSs = [pS0, pS1]
            pOs = [pO0, pO1]
            p.wait("tensor", sDVE)
            s_done = {}
            for ts in range(4):
                pS = pSs[ts % 2]
                if ts >= 2:
                    p.wait("tensor", sDVE, s_done[ts - 2])
                for k in range(8):
                    p.op("tensor", lambda e, ts=ts, k=k, pS=pS: e.matmul(
                        pS[:, :], win_sb[:, 64 * k:64 * (k + 1)],
                        x3T[:, 2048 * k + 512 * ts:2048 * k + 512 * (ts + 1)],
                        start=(k == 0), stop=(k == 7)),
                        incs=(sPE, 1) if k == 7 else ())
                p.wait("vector", sPE)
                p.op("vector", lambda e, ts=ts, pS=pS: e.tensor_scalar(
                    s_sb[:, 512 * ts:512 * (ts + 1)], pS[:, :], bin_sb[:, 0:1], None,
                    ALU.add), incs=(sDVE, 1))
                s_done[ts] = sDVE.n

            p.wait("scalar", sDVE)
            gas = [ga0, ga1]
            hbs = [hb0, hb1]
            for hh in range(16):
                hbuf = hbs[hh % 2]
                p.wait("scalar", sDVE)
                p.op("scalar", lambda e, hh=hh, hbuf=hbuf: e.activation(
                    hbuf[:, :], s_sb[:, :], AF.Relu, bias=b1_sb[:, hh:hh + 1],
                    scale=w1_sb[:, hh:hh + 1]), incs=(sACT, 1))
                av = sACT.n
                p.wait("vector", sACT, av)
                if hh == 0:
                    p.op("vector", lambda e, hbuf=hbuf: e.tensor_scalar(
                        gas[0][:, :], hbuf[:, :], w2_sb[:, 0:1], None, ALU.mult), incs=(sDVE, 1))
                else:
                    src_g, dst_g = gas[(hh + 1) % 2], gas[hh % 2]
                    p.op("vector", lambda e, hh=hh, hbuf=hbuf, src_g=src_g, dst_g=dst_g: e.scalar_tensor_tensor(
                        dst_g[:, :], hbuf[:, :], w2_sb[:, hh:hh + 1], src_g[:, :],
                        ALU.mult, ALU.add), incs=(sDVE, 1))
                p.wait("vector", sDVE)
            gfin = gas[15 % 2]
            p.op("vector", lambda e, gfin=gfin: e.tensor_scalar(
                gfin[:, :], gfin[:, :], b2_sb[:, 0:1], None, ALU.add), incs=(sDVE, 1))
            p.wait("vector", sDVE)
            p.op("vector", lambda e, gfin=gfin: e.tensor_copy(gbf[:, :], gfin[:, :]), incs=(sDVE, 1))
            g_done = sDVE.n

            p.wait("tensor", sDVE, g_done)
            o_done = {}
            for ti in range(16):
                pO = pOs[ti % 2]
                if ti >= 2:
                    p.wait("tensor", sDVE, o_done[ti - 2])
                for cc in range(2):
                    p.op("tensor", lambda e, ti=ti, cc=cc, pO=pO: e.matmul(
                        pO[:, 512 * cc:512 * (cc + 1)], gbf[:, 128 * ti:128 * (ti + 1)],
                        wout_sb[:, 512 * cc:512 * (cc + 1)], start=True, stop=False))
                    p.op("tensor", lambda e, cc=cc, pO=pO: e.matmul(
                        pO[:, 512 * cc:512 * (cc + 1)], onesr_bf[:, :],
                        bout_hi[:, 512 * cc:512 * (cc + 1)], start=False, stop=False))
                    p.op("tensor", lambda e, cc=cc, pO=pO: e.matmul(
                        pO[:, 512 * cc:512 * (cc + 1)], onesr_bf[:, :],
                        bout_lo[:, 512 * cc:512 * (cc + 1)], start=False, stop=True),
                        incs=(sPE, 1) if cc == 1 else ())
                p.wait("vector", sPE)
                p.op("vector", lambda e, ti=ti, pO=pO: e.tensor_tensor(
                    xs[:, 1024 * ti:1024 * (ti + 1)], pO[:, :],
                    xs[:, 1024 * ti:1024 * (ti + 1)], ALU.add), incs=(sDVE, 1))
                o_done[ti] = sDVE.n
            emit_ln(lambda ti: p.wait("scalar", sDVE, o_done[ti]), range(16))
            for ti in range(16):
                dv = sDVE.n
                p.wait("sync", sDVE, dv)
                p.dma("sync", out_t[128 * ti:128 * (ti + 1), :], xs[:, 1024 * ti:1024 * (ti + 1)], sST)
            p.wait("sync", sST)
            p.flush()
    return nc



def build_fused():
    nc = bass.Bass("TRN2", target_bir_lowering=False, detect_race_conditions=False)
    dt = nc.dram_tensor
    x_own = dt("x_own", [TC, D], F32, kind="ExternalInput").ap()
    x_oth = dt("x_oth", [TC, D], F32, kind="ExternalInput").ap()
    wq_d = dt("wq_d", [QL, D], F32, kind="ExternalInput").ap()
    w_qk = dt("w_qk", [D, QL], F32, kind="ExternalInput").ap()
    wkv_d = dt("wkv_d", [KV, D], F32, kind="ExternalInput").ap()
    wv_u = dt("wv_u", [D, KV], F32, kind="ExternalInput").ap()
    w_proj = dt("w_proj", [D, D], F32, kind="ExternalInput").ap()
    bq_cols = dt("bq_cols", [128, 4], F32, kind="ExternalInput").ap()
    bqk_cols = dt("bqk_cols", [64, 16], F32, kind="ExternalInput").ap()
    bkv_col = dt("bkv_col", [64, 1], F32, kind="ExternalInput").ap()
    bvu_row = dt("bvu_row", [1, D], F32, kind="ExternalInput").ap()
    bprj_row = dt("bprj_row", [1, D], F32, kind="ExternalInput").ap()
    ones_row = dt("ones_row", [1, 128], F32, kind="ExternalInput").ap()
    ident_in = dt("ident_in", [128, 128], F32, kind="ExternalInput").ap()
    halves = dt("halves", [128, 2], F32, kind="ExternalInput").ap()
    e16 = dt("e16", [8, 16, 128], F32, kind="ExternalInput").ap()
    wg_st = dt("wg_st", [128, 64], F32, kind="ExternalInput").ap()
    bg_bc = dt("bg_bc", [128, 8], F32, kind="ExternalInput").ap()
    w1 = dt("w1", [8, D, D], F32, kind="ExternalInput").ap()
    w2 = dt("w2", [8, D, D], F32, kind="ExternalInput").ap()
    b1cols = dt("b1cols", [128, 64], F32, kind="ExternalInput").ap()
    b2m = dt("b2m", [8, D], F32, kind="ExternalInput").ap()
    w_in_a = dt("w_in_a", [8, 128, 128], F32, kind="ExternalInput").ap()
    w_a_a = dt("w_a_a", [8, 128, 128], F32, kind="ExternalInput").ap()
    b_in_c8 = dt("b_in_c8", [128, 8], F32, kind="ExternalInput").ap()
    b_a_c8 = dt("b_a_c8", [128, 8], F32, kind="ExternalInput").ap()
    ap_c8 = dt("ap_c8", [128, 8], F32, kind="ExternalInput").ap()
    cmask_in = dt("cmask_in", [128, 16], F32, kind="ExternalInput").ap()
    cneut_in = dt("cneut_in", [128, 16], F32, kind="ExternalInput").ap()
    win = dt("win", [D, 64], F32, kind="ExternalInput").ap()
    bin_c = dt("bin_c", [64, 1], F32, kind="ExternalInput").ap()
    w1c = dt("w1c", [64, 16], F32, kind="ExternalInput").ap()
    b1c = dt("b1c", [64, 16], F32, kind="ExternalInput").ap()
    w2c = dt("w2c", [64, 16], F32, kind="ExternalInput").ap()
    b2c = dt("b2c", [64, 1], F32, kind="ExternalInput").ap()
    wout = dt("wout", [64, D], F32, kind="ExternalInput").ap()
    bout_row = dt("bout_row", [1, D], F32, kind="ExternalInput").ap()
    out_t = dt("out", [TC, D], F32, kind="ExternalOutput").ap()
    cc_in = dt("cc_in", [128, 16], F32, kind="Internal")
    cc_out = dt("cc_out", [256, 16], F32, kind="Internal")

    with contextlib.ExitStack() as ctx:
        p = Prog(nc, ctx)
        sLD = p.dsem("sLD")
        sPE = p.sem("sPE")
        sACT = p.sem("sACT")
        sDVE = p.sem("sDVE")
        sST = p.dsem("sST")
        sCC = p.sem("sCC")
        allsems = [sLD, sPE, sACT, sDVE, sST]

        def sb(name, shape, dtyp):
            return ctx.enter_context(nc.sbuf_tensor(name, shape, dtyp))

        g = "gpsimd"
        onesr_bf = sb("onesr_bf", [1, 128], BF16)
        ident_bf = sb("ident_bf", [128, 128], BF16)
        ident_f = sb("ident_f", [128, 128], F32)
        halves_bf = sb("halves_bf", [128, 2], BF16)
        e16_bf = sb("e16_bf", [16, 8 * 128], BF16)
        epsc = sb("epsc", [128, 3], F32)
        scr2 = sb("scr2", [128, 1024], F32)
        sA2a = sb("sA2a", [128, 16], F32)
        sQ2a = sb("sQ2a", [128, 16], F32)
        lnx = sb("lnx", [128, 8 * 16], F32)
        lnmean = lnx.ap()[:, 32:48]
        lnmsq = lnx.ap()[:, 48:64]
        lnvar = lnx.ap()[:, 64:80]
        lnrstd = lnx.ap()[:, 80:96]
        lnnm = lnx.ap()[:, 96:112]
        srow = sb("srow", [64, 1024], F32)

        p.dma(g, onesr_bf[:, :], ones_row, sLD)
        p.dma(g, ident_bf[:, :], ident_in, sLD)
        p.dma("sync", ident_f[:, :], ident_in, sLD)
        p.dma(g, halves_bf[:, :], halves, sLD)
        for c in range(8):
            p.dma(g, e16_bf[:, 128 * c:128 * (c + 1)], e16[c], sLD)
        p.op("vector", lambda e: e.memset(epsc[:, 0:1], 1e-20), incs=(sDVE, 1))
        p.op("vector", lambda e: e.memset(epsc[:, 1:2], 1e-5), incs=(sDVE, 1))
        p.op("vector", lambda e: e.memset(epsc[:, 2:3], 1.0), incs=(sDVE, 1))

        def emit_ln(wait_fn):
            for ti in range(16):
                wait_fn(ti)
                p.op("scalar", lambda e, ti=ti: e.activation(
                    scr2[:, :], xs[:, 1024 * ti:1024 * (ti + 1)], AF.Copy, bias=0.0,
                    scale=1.0, accum_out=sA2a[:, ti:ti + 1]), incs=(sACT, 1))
                p.op("scalar", lambda e, ti=ti: e.activation(
                    scr2[:, :], xs[:, 1024 * ti:1024 * (ti + 1)], AF.Square, bias=0.0,
                    scale=1.0, accum_out=sQ2a[:, ti:ti + 1]), incs=(sACT, 1))
            p.wait("vector", sACT)
            p.op("vector", lambda e: e.tensor_scalar(lnmean, sA2a[:, :], 1.0 / D, None, ALU.mult), incs=(sDVE, 1))
            p.wait("vector", sDVE)
            p.op("vector", lambda e: e.tensor_tensor(lnmsq, lnmean, lnmean, ALU.mult), incs=(sDVE, 1))
            p.op("vector", lambda e: e.tensor_scalar(lnvar, sQ2a[:, :], 1.0 / D, None, ALU.mult), incs=(sDVE, 1))
            p.wait("vector", sDVE)
            p.op("vector", lambda e: e.tensor_tensor(lnvar, lnvar, lnmsq, ALU.subtract), incs=(sDVE, 1))
            p.wait("scalar", sDVE)
            p.op("scalar", lambda e: e.activation(lnvar, lnvar, AF.Ln, bias=epsc[:, 1:2], scale=1.0), incs=(sACT, 1))
            p.wait("scalar", sACT)
            p.op("scalar", lambda e: e.activation(lnrstd, lnvar, AF.Exp, bias=0.0, scale=-0.5), incs=(sACT, 1))
            p.wait("vector", sACT)
            p.op("vector", lambda e: e.tensor_scalar(lnnm, lnmean, -1.0, None, ALU.mult), incs=(sDVE, 1))
            p.wait("vector", sDVE)
            for ti in range(16):
                p.op("vector", lambda e, ti=ti: e.tensor_scalar(
                    xs[:, 1024 * ti:1024 * (ti + 1)], xs[:, 1024 * ti:1024 * (ti + 1)],
                    lnnm[:, ti:ti + 1], lnrstd[:, ti:ti + 1], ALU.add, ALU.mult), incs=(sDVE, 1))

        bprj_hi = sb("bprj_hi", [1, D], BF16)
        bprj_lo = sb("bprj_lo", [1, D], BF16)

        p.dma("sync", srow[0:1, 0:D], bprj_row, sLD)
        n_pre0 = sLD.n
        p.wait("vector", sLD, n_pre0)
        p.op("vector", lambda e: e.tensor_copy(bprj_hi[:, :], srow[0:1, 0:D]), incs=(sDVE, 1))
        p.wait("vector", sDVE)
        p.op("vector", lambda e: e.tensor_tensor(srow[0:1, 0:D], srow[0:1, 0:D], bprj_hi[:, :], ALU.subtract), incs=(sDVE, 1))
        p.wait("vector", sDVE)
        p.op("vector", lambda e: e.tensor_copy(bprj_lo[:, :], srow[0:1, 0:D]), incs=(sDVE, 1))

        spanB = ctx.enter_context(contextlib.ExitStack())

        def sbB(name, shape, dtyp):
            return spanB.enter_context(nc.sbuf_tensor(name, shape, dtyp))

        maxq = sbB("maxq", [1, 16], F32)
        maxk = sbB("maxk", [1, 2], F32)
        bndrow = sbB("bndrow", [1, 16], F32)
        bnd_bf = sbB("bnd_bf", [1, 16], BF16)
        negbnd = sbB("negbnd", [128, 16], F32)
        wv_uT = sbB("wv_uT", [65, D], BF16)
        ckvT = sbB("ckvT", [64, T], BF16)
        ckv_tok = sbB("ckv_tok", [128, 32 * 65], BF16)
        qT = sbB("qT", [64, 16 * TC], BF16)
        p.dma(g, wv_uT[64:65, :], bvu_row, sLD)

        with nc.psum_tensor("pT0", [128, 1024], BF16) as pT0, \
             nc.psum_tensor("pT1", [128, 1024], BF16) as pT1, \
             nc.sbuf_tensor("xTo", [128, 8 * TC], BF16) as xTo, \
             nc.sbuf_tensor("c_qT", [128, 4 * TC], BF16) as c_qT, \
             nc.sbuf_tensor("qsq", [128, TC], BF16) as qsq, \
             nc.sbuf_tensor("ksq", [64, T], BF16) as ksq, \
             nc.sbuf_tensor("wq_dT", [128, 8 * QL], BF16) as wq_dT, \
             nc.sbuf_tensor("w_qkT", [128, 4 * D], BF16) as w_qkT, \
             nc.sbuf_tensor("wkv_dT", [128, 8 * KV], BF16) as wkv_dT, \
             nc.sbuf_tensor("bq_sb", [128, 4], F32) as bq_sb, \
             nc.sbuf_tensor("bqk_sb", [64, 16], F32) as bqk_sb, \
             nc.sbuf_tensor("bkv_sb", [64, 1], F32) as bkv_sb, \
             nc.sbuf_tensor("stgA", [128, 1024], BF16) as stgA, \
             nc.sbuf_tensor("stgB", [128, 1024], BF16) as stgB:
            p.dma("sync", bq_sb[:, :], bq_cols, sLD)
            p.dma("sync", bqk_sb[:, :], bqk_cols, sLD)
            p.dma("sync", bkv_sb[:, :], bkv_col, sLD)
            n_pre = sLD.n
            pTs = [pT0, pT1]
            tcnt = [0]
            tl_tp_done = [0, 0]
            tl_cp_done = [0, 0]
            tl_ld_done = [None, None]

            def tload(dram, R, C, dest_fn, dest_war=0):
                nR = (R + 127) // 128
                nCc = (C + 127) // 128
                cw = min(128, C)
                assert C % 128 == 0 or nCc == 1
                for r in range(nR):
                    Rw = min(128, R - 128 * r)
                    sl = tcnt[0] % 2
                    stg = stgA if sl == 0 else stgB
                    pt = pTs[sl]
                    tcnt[0] += 1
                    p.wait(g, sPE, tl_tp_done[sl])
                    p.dma(g, stg[0:Rw, 0:C], dram[128 * r:128 * r + Rw, :], sLD)
                    tl_ld_done[sl] = sLD.n
                    p.wait("tensor", sLD, tl_ld_done[sl])
                    p.wait("tensor", sDVE, tl_cp_done[sl])
                    for c in range(nCc):
                        ccw = min(128, C - 128 * c)
                        p.op("tensor", lambda e, stg=stg, pt=pt, r=r, c=c, Rw=Rw, ccw=ccw: e.transpose(
                            pt[0:ccw, 128 * c:128 * c + Rw],
                            stg[0:Rw, 128 * c:128 * c + ccw],
                            ident_bf[0:Rw, 0:Rw]),
                            incs=(sPE, 1) if c == nCc - 1 else ())
                    tl_tp_done[sl] = sPE.n
                    p.wait("vector", sPE, tl_tp_done[sl])
                    if dest_war:
                        p.wait("vector", sPE, dest_war)
                    if nCc == 1:
                        src_v = pt[0:cw, 0:Rw]
                    else:
                        src_v = pt.ap().rearrange("p (c n) -> p c n", c=8)[:, 0:nCc, 0:Rw]
                    p.op("vector", lambda e, d=dest_fn(r), s=src_v: e.tensor_copy(d, s),
                         incs=(sDVE, 1))
                    tl_cp_done[sl] = sDVE.n

            xT_r = xTo.ap().rearrange("p (c n) -> p c n", c=8)
            tload(x_own, TC, D, lambda r: xT_r[:, :, 128 * r:128 * r + 128])
            wqT_r = wq_dT.ap().rearrange("p (c n) -> p c n", c=8)
            tload(wq_d, QL, D, lambda r: wqT_r[:, :, 128 * r:128 * r + 128])
            wqkT_r = w_qkT.ap().rearrange("p (c n) -> p c n", c=4)
            tload(w_qk, D, QL, lambda r: wqkT_r[:, 0:4, 128 * r:128 * r + 128])
            wkvT_r = wkv_dT.ap().rearrange("p (c n) -> p c n", c=8)
            tload(wkv_d, KV, D, lambda r: wkvT_r[:, :, 0:64])
            tload(wv_u, D, KV, lambda r: wv_uT[0:64, 128 * r:128 * r + 128])

            with nc.psum_tensor("pmA", [128, 512], F32) as pmA, \
                 nc.psum_tensor("pmB", [128, 512], F32) as pmB, \
                 nc.psum_tensor("pq", [1, 2048], F32) as pq:
                pms = [pmA, pmB]
                pm_i = [0]
                pm_done = [0, 0]
                p.wait("tensor", sDVE)
                p.wait("scalar", sLD, n_pre)

                def mm_chain(out_view, lhsTs, rhss, bias_ap, raw=0):
                    sl = pm_i[0] % 2
                    pm = pms[sl]
                    pm_i[0] += 1
                    p.wait("tensor", sDVE, max(pm_done[sl], raw))
                    nk = len(lhsTs)
                    M = out_view.shape[0]
                    N = out_view.shape[-1]
                    for k in range(nk):
                        p.op("tensor", lambda e, k=k, pm=pm, M=M, N=N, lh=lhsTs[k], rh=rhss[k], nk=nk: e.matmul(
                            pm[0:M, 0:N], lh, rh, start=(k == 0), stop=(k == nk - 1)),
                            incs=(sPE, 1) if k == nk - 1 else ())
                    p.wait("vector", sPE)
                    p.op("vector", lambda e, pm=pm, M=M, N=N, ov=out_view, b=bias_ap: e.tensor_scalar(
                        ov, pm[0:M, 0:N], b, None, ALU.add),
                        incs=(sDVE, 1))
                    pm_done[sl] = sDVE.n
                    return pm

                def ckv_slices(lo, hi, raw=0):
                    for sl in range(lo, hi):
                        off = 512 * (sl % 4)
                        mm_chain(ckvT[0:64, 512 * sl:512 * (sl + 1)],
                                 [wkv_dT[:, 64 * k:64 * (k + 1)] for k in range(8)],
                                 [xTo[:, 2048 * k + off: 2048 * k + off + 512] for k in range(8)],
                                 bkv_sb[0:64, 0:1], raw=raw)

                tl_all = sDVE.n
                ckv_slices(0, 4, raw=tl_all)
                for oc in range(4):
                    for ts in range(4):
                        mm_chain(c_qT[:, 2048 * oc + 512 * ts: 2048 * oc + 512 * (ts + 1)],
                                 [wq_dT[:, 512 * k + 128 * oc: 512 * k + 128 * (oc + 1)] for k in range(8)],
                                 [xTo[:, 2048 * k + 512 * ts: 2048 * k + 512 * (ts + 1)] for k in range(8)],
                                 bq_sb[:, oc:oc + 1], raw=tl_all)
                cq_all = sDVE.n
                for h in range(16):
                    for ts in range(4):
                        mm_chain(qT[0:64, 2048 * h + 512 * ts: 2048 * h + 512 * (ts + 1)],
                                 [w_qkT[:, 1024 * k + 64 * h: 1024 * k + 64 * (h + 1)] for k in range(4)],
                                 [c_qT[:, 2048 * k + 512 * ts: 2048 * k + 512 * (ts + 1)] for k in range(4)],
                                 bqk_sb[:, h:h + 1], raw=cq_all)

                tload(x_oth, TC, D, lambda r: xT_r[:, :, 128 * r:128 * r + 128],
                      dest_war=sPE.n)
                ckv_slices(4, 8, raw=sDVE.n)

                tok_r = ckv_tok.ap().rearrange("p (s n) -> p s n", s=32)
                p.wait("tensor", sDVE)
                for gq in range(8):
                    pt = pTs[gq % 2]
                    p.wait("tensor", sDVE)
                    for j in range(4):
                        st = 4 * gq + j
                        p.op("tensor", lambda e, st=st, j=j, pt=pt: e.transpose(
                            pt[:, 64 * j:64 * j + 64],
                            ckvT[0:64, 128 * st:128 * (st + 1)],
                            ident_bf[0:64, 0:64]),
                            incs=(sPE, 1) if j == 3 else ())
                    p.wait("vector", sPE)
                    p.op("vector", lambda e, gq=gq, pt=pt: e.tensor_copy(
                        tok_r[:, 4 * gq:4 * (gq + 1), 0:64],
                        pt.ap().rearrange("p (c n) -> p c n", c=16)[:, 0:4, 0:64]),
                        incs=(sDVE, 1))
                p.op("vector", lambda e: e.memset(tok_r[:, :, 64:65], 1.0), incs=(sDVE, 1))

                p.wait("vector", sDVE)
                p.op("vector", lambda e: e.tensor_tensor(ksq[:, :], ckvT[0:64, :], ckvT[0:64, :], ALU.mult), incs=(sDVE, 1))
                p.wait("tensor", sDVE)
                for rnd in range(2):
                    for sl in range(4):
                        p.op("tensor", lambda e, rnd=rnd, sl=sl: e.matmul(
                            pq[0:1, 512 * sl:512 * (sl + 1)], halves_bf[0:64, 0:1],
                            ksq[0:64, 512 * (4 * rnd + sl):512 * (4 * rnd + sl + 1)],
                            start=True, stop=True),
                            incs=(sPE, 1) if sl == 3 else ())
                    p.wait("vector", sPE)
                    p.op("vector", lambda e, rnd=rnd: e.tensor_reduce(
                        maxk[0:1, rnd:rnd + 1], pq[0:1, :], AX.X, ALU.max), incs=(sDVE, 1))
                    p.wait("tensor", sDVE)
                p.wait("vector", sDVE)
                p.op("vector", lambda e: e.tensor_tensor(
                    maxk[0:1, 0:1], maxk[0:1, 0:1], maxk[0:1, 1:2], ALU.max), incs=(sDVE, 1))
                for h in range(16):
                    p.wait("vector", sDVE)
                    p.op("vector", lambda e, h=h: e.tensor_tensor(
                        qsq[0:64, :], qT[0:64, 2048 * h:2048 * (h + 1)], qT[0:64, 2048 * h:2048 * (h + 1)], ALU.mult),
                        incs=(sDVE, 1))
                    p.wait("tensor", sDVE)
                    for ts in range(4):
                        p.op("tensor", lambda e, ts=ts: e.matmul(
                            pq[0:1, 512 * ts:512 * (ts + 1)],
                            halves_bf[0:64, 0:1], qsq[0:64, 512 * ts:512 * (ts + 1)],
                            start=True, stop=True),
                            incs=(sPE, 1) if ts == 3 else ())
                    p.wait("vector", sPE)
                    p.op("vector", lambda e, h=h: e.tensor_reduce(
                        maxq[0:1, h:h + 1], pq[0:1, :], AX.X, ALU.max), incs=(sDVE, 1))
                    p.wait("tensor", sDVE)
                p.wait("vector", sDVE)
                p.op("vector", lambda e: e.tensor_scalar(
                    bndrow[:, :], maxq[:, :], maxk[0:1, 0:1], None, ALU.mult), incs=(sDVE, 1))
                p.wait("scalar", sDVE)
                p.op("scalar", lambda e: e.activation(bndrow[:, :], bndrow[:, :], AF.Ln, bias=epsc[0:1, 0:1], scale=1.0), incs=(sACT, 1))
                p.wait("scalar", sACT)
                p.op("scalar", lambda e: e.activation(bndrow[:, :], bndrow[:, :], AF.Exp, bias=0.0, scale=0.5), incs=(sACT, 1))
                p.wait("vector", sACT)
                p.op("vector", lambda e: e.tensor_scalar(
                    bndrow[:, :], bndrow[:, :], -0.13125, -0.5, ALU.mult, ALU.add), incs=(sDVE, 1))
                p.wait("vector", sDVE)
                p.op("vector", lambda e: e.tensor_copy(bnd_bf[:, :], bndrow[:, :]), incs=(sDVE, 1))
                p.wait("tensor", sDVE)
                p.op("tensor", lambda e: e.matmul(
                    pmA[0:128, 0:16], onesr_bf[:, :], bnd_bf[:, :], start=True, stop=True), incs=(sPE, 1))
                p.wait("vector", sPE)
                p.op("vector", lambda e: e.tensor_copy(negbnd[:, :], pmA[0:128, 0:16]), incs=(sDVE, 1))
                p.flush()

        groups = [(0, 4), (4, 2), (6, 4), (10, 2), (12, 4), (16, 2), (18, 4),
                  (22, 2), (24, 4), (28, 2), (30, 2)]
        NG = len(groups)

        yall_stack = ctx.enter_context(contextlib.ExitStack())
        yallT = yall_stack.enter_context(nc.sbuf_tensor("yallT", [128, 8 * TC], BF16, side="right"))
        with nc.sbuf_tensor("sums16", [16, TC], F32) as sums16, \
             nc.sbuf_tensor("rec_hi", [16, TC], BF16) as rec_hi, \
             nc.sbuf_tensor("rec_lo", [16, TC], BF16) as rec_lo:
            with nc.sbuf_tensor("stripA", [128, 2048], BF16) as stripA, \
                 nc.sbuf_tensor("stripB", [128, 1024], BF16) as stripB, \
                 nc.sbuf_tensor("zsb0", [65, 512], BF16) as zsb0, \
                 nc.sbuf_tensor("zsb1", [65, 512], BF16) as zsb1, \
                 nc.psum_tensor("psA", [128, 2048], F32) as psA, \
                 nc.psum_tensor("psB", [128, 1024], F32) as psB, \
                 nc.psum_tensor("psZ0", [65, 512], F32) as psZ0, \
                 nc.psum_tensor("psZ1", [65, 512], F32) as psZ1:
                zsbs = [zsb0, zsb1]
                psG = [psA, psB]
                psZ = [psZ0, psZ1]
                tok_r = ckv_tok.ap().rearrange("p (s n) -> p s n", s=32)
                last_exp = [0, 0]
                scores_done = {}
                exp_done = {}
                last_strip_reader = {}
                yraw_done = [0, 0]
                srow_dma = [0, 0]
                prev_iter = [None]
                p.wait("tensor", sACT)
                p.wait("tensor", sDVE)
                p.wait("scalar", sDVE)

                def emit_scores(h, t5, gi):
                    s0, glen = groups[gi]
                    par = gi % 2
                    pg = psG[par]
                    p.wait("tensor", sACT, last_exp[par])
                    for j in range(glen):
                        st = s0 + j
                        p.op("tensor", lambda e, st=st, j=j, pg=pg, h=h, t5=t5: e.matmul(
                            pg[:, 512 * j:512 * (j + 1)],
                            ckvT[0:64, 128 * st:128 * (st + 1)],
                            qT[0:64, 2048 * h + 512 * t5: 2048 * h + 512 * (t5 + 1)],
                            start=True, stop=True),
                            incs=(sPE, 1) if j == glen - 1 else ())
                    scores_done[gi] = sPE.n

                def emit_exp(h, t5, gi):
                    s0, glen = groups[gi]
                    par = gi % 2
                    pg = psG[par]
                    strip = stripA if par == 0 else stripB
                    p.wait("scalar", sPE, scores_done[gi])
                    lsr = last_strip_reader.get(par)
                    if lsr:
                        p.wait("scalar", sPE, lsr)
                    p.op("scalar", lambda e, pg=pg, strip=strip, glen=glen, h=h: e.activation(
                        strip[:, 0:512 * glen], pg[:, 0:512 * glen], AF.Exp,
                        bias=negbnd[:, h:h + 1], scale=0.125),
                        incs=(sACT, 1))
                    exp_done[gi] = sACT.n
                    last_exp[par] = sACT.n

                def emit_stage1(h, t5, gi, zp):
                    s0, glen = groups[gi]
                    par = gi % 2
                    strip = stripA if par == 0 else stripB
                    p.wait("tensor", sACT, exp_done[gi])
                    if gi == 0:
                        p.wait("tensor", sDVE, yraw_done[zp])
                    for j in range(glen):
                        st = s0 + j
                        p.op("tensor", lambda e, st=st, j=j, strip=strip, zp=zp,
                             first=(gi == 0 and j == 0), last=(gi == NG - 1 and j == glen - 1): e.matmul(
                            psZ[zp][:, :], tok_r[:, st, :], strip[:, 512 * j:512 * (j + 1)],
                            start=first, stop=last),
                            incs=(sPE, 1) if j == glen - 1 else ())
                    last_strip_reader[par] = sPE.n

                def emit_tail(h, t5, zp):
                    p.wait("vector", sPE)
                    slot = (4 * h + t5) % 2
                    p.wait("vector", sST, srow_dma[slot])
                    p.op("vector", lambda e, zp=zp: e.tensor_copy(
                        zsbs[zp][0:64, :], psZ[zp][0:64, :]), incs=(sDVE, 1))
                    p.op("vector", lambda e, zp=zp: e.tensor_copy(
                        zsbs[zp][64:65, :], psZ[zp][64:65, :]), incs=(sDVE, 1))
                    p.op("vector", lambda e, zp=zp, slot=slot: e.tensor_copy(
                        srow[0:1, 512 * slot:512 * (slot + 1)], psZ[zp][64:65, :]),
                        incs=(sDVE, 1))
                    d_after = sDVE.n
                    p.wait("sync", sDVE, d_after)
                    p.dma("sync", sums16[h:h + 1, 512 * t5:512 * (t5 + 1)],
                          srow[0:1, 512 * slot:512 * (slot + 1)], sST)
                    srow_dma[slot] = sST.n
                    return d_after, d_after

                def emit_stage2(info):
                    zp, h, t5, d_after, a_after = info
                    c, qrow = h // 2, (h % 2) * 64
                    p.wait("tensor", sDVE, d_after)
                    p.op("tensor", lambda e, zp=zp, h=h: e.matmul(
                        psZ[zp][0:64, :], wv_uT[:, 64 * h:64 * (h + 1)], zsbs[zp][:, :],
                        start=True, stop=True), incs=(sPE, 1))
                    pe_after = sPE.n
                    p.wait("vector", sPE, pe_after)
                    p.op("vector", lambda e, zp=zp, c=c, qrow=qrow, t5=t5: e.tensor_copy(
                        yallT[qrow:qrow + 64, 2048 * c + 512 * t5: 2048 * c + 512 * (t5 + 1)],
                        psZ[zp][0:64, :]), incs=(sDVE, 1))
                    yraw_done[zp] = sDVE.n

                for h in range(16):
                    for t5 in range(4):
                        it = 4 * h + t5
                        zp = it % 2
                        emit_scores(h, t5, 0)
                        for gi in range(NG):
                            if gi + 1 < NG:
                                emit_scores(h, t5, gi + 1)
                            if gi == 0 and prev_iter[0] is not None:
                                emit_stage2(prev_iter[0])
                                prev_iter[0] = None
                            if gi >= 1:
                                emit_stage1(h, t5, gi - 1, zp)
                            emit_exp(h, t5, gi)
                        emit_stage1(h, t5, NG - 1, zp)
                        prev_iter[0] = (zp, h, t5) + emit_tail(h, t5, zp)
                emit_stage2(prev_iter[0])

                p.wait("vector", sST)
                p.op("vector", lambda e: e.reciprocal(sums16[:, :], sums16[:, :]), incs=(sDVE, 1))
                p.wait("vector", sDVE)
                p.op("vector", lambda e: e.tensor_copy(rec_hi[:, :], sums16[:, :]), incs=(sDVE, 1))
                p.wait("vector", sDVE)
                p.op("vector", lambda e: e.tensor_tensor(sums16[:, :], sums16[:, :], rec_hi[:, :], ALU.subtract), incs=(sDVE, 1))
                p.wait("vector", sDVE)
                p.op("vector", lambda e: e.tensor_copy(rec_lo[:, :], sums16[:, :]), incs=(sDVE, 1))
                p.flush()

            with nc.psum_tensor("pS0", [128, 512], F32) as pS0, \
                 nc.psum_tensor("pS1", [128, 512], F32) as pS1:
                pSs = [pS0, pS1]
                p.barrier_all(allsems)
                scale_done = {}
                k_i = 0
                for c in range(8):
                    for t5 in range(4):
                        pS = pSs[k_i % 2]
                        if k_i >= 2:
                            p.wait("tensor", sDVE, scale_done[k_i - 2])
                        p.op("tensor", lambda e, c=c, t5=t5, pS=pS: e.matmul(
                            pS[:, :], e16_bf[:, 128 * c:128 * (c + 1)], rec_hi[:, 512 * t5:512 * (t5 + 1)],
                            start=True, stop=False))
                        p.op("tensor", lambda e, c=c, t5=t5, pS=pS: e.matmul(
                            pS[:, :], e16_bf[:, 128 * c:128 * (c + 1)], rec_lo[:, 512 * t5:512 * (t5 + 1)],
                            start=False, stop=True), incs=(sPE, 1))
                        p.wait("vector", sPE)
                        p.op("vector", lambda e, c=c, t5=t5, pS=pS: e.tensor_tensor(
                            yallT[:, 2048 * c + 512 * t5: 2048 * c + 512 * (t5 + 1)],
                            yallT[:, 2048 * c + 512 * t5: 2048 * c + 512 * (t5 + 1)],
                            pS[:, :], ALU.mult), incs=(sDVE, 1))
                        scale_done[k_i] = sDVE.n
                        k_i += 1
                p.flush()
        spanB.close()

        xs = ctx.enter_context(nc.sbuf_tensor("xs", [128, 16 * 1024], F32))
        for i in range(16):
            p.dma("sync", xs[:, 1024 * i:1024 * (i + 1)], x_own[128 * i:128 * (i + 1), :], sLD)
        xs_loaded = sLD.n

        with nc.sbuf_tensor("w_projT", [128, 8 * D], BF16) as w_projT, \
             nc.sbuf_tensor("stgP", [128, 1024], BF16) as stgP, \
             nc.psum_tensor("pTP", [128, 1024], BF16) as pTP, \
             nc.psum_tensor("pP0", [128, 512], F32) as pP0, \
             nc.psum_tensor("pP1", [128, 512], F32) as pP1:
            p.barrier_all(allsems)
            wpT_r = w_projT.ap().rearrange("p (c n) -> p c n", c=8)
            for r in range(8):
                p.wait(g, sPE)
                p.wait(g, sDVE)
                p.dma(g, stgP[:, :], w_proj[128 * r:128 * (r + 1), :], sLD)
                p.wait("tensor", sLD)
                p.wait("tensor", sDVE)
                for c in range(8):
                    p.op("tensor", lambda e, pt=pTP, c=c, r=r: e.transpose(
                        pTP[:, 128 * c:128 * (c + 1)], stgP[:, 128 * c:128 * (c + 1)],
                        ident_bf[:, :]), incs=(sPE, 1) if c == 7 else ())
                p.wait("vector", sPE)
                p.op("vector", lambda e, r=r: e.tensor_copy(
                    wpT_r[:, :, 128 * r:128 * r + 128],
                    pTP.ap().rearrange("p (c n) -> p c n", c=8)), incs=(sDVE, 1))

            pPs = [pP0, pP1]
            p.wait("tensor", sDVE)
            pr_i = 0
            prj_done = {}
            res_done = {}
            for ti in range(16):
                for cc in range(2):
                    pP = pPs[pr_i % 2]
                    if pr_i >= 2:
                        p.wait("tensor", sDVE, prj_done[pr_i - 2])
                    for k in range(8):
                        p.op("tensor", lambda e, ti=ti, cc=cc, k=k, pP=pP: e.matmul(
                            pP[:, :], yallT[:, 2048 * k + 128 * ti: 2048 * k + 128 * (ti + 1)],
                            w_projT[:, 1024 * k + 512 * cc: 1024 * k + 512 * (cc + 1)],
                            start=(k == 0), stop=False))
                    p.op("tensor", lambda e, cc=cc, pP=pP: e.matmul(
                        pP[:, :], onesr_bf[:, :], bprj_hi[:, 512 * cc:512 * (cc + 1)],
                        start=False, stop=False))
                    p.op("tensor", lambda e, cc=cc, pP=pP: e.matmul(
                        pP[:, :], onesr_bf[:, :], bprj_lo[:, 512 * cc:512 * (cc + 1)],
                        start=False, stop=True), incs=(sPE, 1))
                    p.wait("vector", sPE)
                    if pr_i == 0:
                        p.wait("vector", sLD, xs_loaded)
                    p.op("vector", lambda e, ti=ti, cc=cc, pP=pP: e.tensor_tensor(
                        xs[:, 1024 * ti + 512 * cc: 1024 * ti + 512 * (cc + 1)],
                        pP[:, :], xs[:, 1024 * ti + 512 * cc: 1024 * ti + 512 * (cc + 1)],
                        ALU.add), incs=(sDVE, 1))
                    prj_done[pr_i] = sDVE.n
                    pr_i += 1
                res_done[ti] = sDVE.n
            emit_ln(lambda ti: p.wait("scalar", sDVE, res_done[ti]))
            p.flush()
        yall_stack.close()

        moe_stack = ctx.enter_context(contextlib.ExitStack())

        def sbM(name, shape, dtyp):
            return moe_stack.enter_context(nc.sbuf_tensor(name, shape, dtyp))

        bg_sb = sbM("bg_sb", [128, 8], F32)
        b1_sb = sbM("b1_sb", [128, 64], F32)
        b2f = sbM("b2f", [8, D], F32)
        b2hi = sbM("b2hi", [8, D], BF16)
        b2lo = sbM("b2lo", [8, D], BF16)
        wgf = sbM("wgf", [128, 64], F32)
        wghi = sbM("wghi", [128, 64], BF16)
        wglo = sbM("wglo", [128, 64], BF16)
        maskF = sbM("maskF", [128, 16 * 8], F32)
        m8 = sbM("m8", [8, TC], BF16)
        x1T = sbM("x1T", [128, 8 * TC], BF16)

        p.dma("sync", bg_sb[:, :], bg_bc, sLD)
        p.dma("sync", b1_sb[:, :], b1cols, sLD)
        p.dma("sync", b2f[:, :], b2m, sLD)
        p.dma("sync", wgf[:, :], wg_st, sLD)
        n_pre2 = sLD.n
        p.wait("vector", sLD, n_pre2)
        p.wait("scalar", sLD, n_pre2)
        p.op("vector", lambda e: e.tensor_copy(b2hi[:, :], b2f[:, :]), incs=(sDVE, 1))
        p.op("vector", lambda e: e.tensor_copy(wghi[:, :], wgf[:, :]), incs=(sDVE, 1))
        p.wait("vector", sDVE)
        p.op("vector", lambda e: e.tensor_tensor(b2f[:, :], b2f[:, :], b2hi[:, :], ALU.subtract), incs=(sDVE, 1))
        p.op("vector", lambda e: e.tensor_tensor(wgf[:, :], wgf[:, :], wghi[:, :], ALU.subtract), incs=(sDVE, 1))
        p.wait("vector", sDVE)
        p.op("vector", lambda e: e.tensor_copy(b2lo[:, :], b2f[:, :]), incs=(sDVE, 1))
        p.op("vector", lambda e: e.tensor_copy(wglo[:, :], wgf[:, :]), incs=(sDVE, 1))

        with nc.sbuf_tensor("xloT", [128, 8 * TC], BF16) as xloT, \
             nc.sbuf_tensor("stgA2", [128, 1024], BF16) as stgA2, \
             nc.sbuf_tensor("stgB2", [128, 1024], BF16) as stgB2, \
             nc.sbuf_tensor("rt0", [128, 8], F32) as rt0, \
             nc.sbuf_tensor("rt1", [128, 8], F32) as rt1, \
             nc.sbuf_tensor("rt2", [128, 8], F32) as rt2, \
             nc.sbuf_tensor("rmx", [128, 1], F32) as rmx, \
             nc.sbuf_tensor("mbf", [128, 8], BF16) as mbf, \
             nc.psum_tensor("pT0b", [128, 1024], BF16) as pT0b, \
             nc.psum_tensor("pT1b", [128, 1024], BF16) as pT1b, \
             nc.psum_tensor("pL", [128, 8], F32) as pL, \
             nc.psum_tensor("pM8", [8, 128], BF16) as pM8:
            p.barrier_all(allsems)
            pTs = [pT0b, pT1b]
            stgs = [stgA2, stgB2]
            x1T_r = x1T.ap().rearrange("p (c n) -> p c n", c=8)
            xloT_r = xloT.ap().rearrange("p (c n) -> p c n", c=8)
            for i in range(16):
                stg = stgs[i % 2]
                pt = pTs[i % 2]
                p.wait("vector", sPE)
                p.wait("vector", sDVE)
                p.op("vector", lambda e, i=i, stg=stg: e.tensor_copy(
                    stg[:, :], xs[:, 1024 * i:1024 * (i + 1)]), incs=(sDVE, 1))
                dv_hi = sDVE.n
                p.wait("tensor", sDVE, dv_hi)
                p.wait("tensor", sDVE)
                for c in range(8):
                    p.op("tensor", lambda e, stg=stg, pt=pt, c=c: e.transpose(
                        pt[:, 128 * c:128 * (c + 1)], stg[:, 128 * c:128 * (c + 1)],
                        ident_bf[:, :]), incs=(sPE, 1) if c == 7 else ())
                p.wait("vector", sPE)
                p.op("vector", lambda e, i=i, pt=pt: e.tensor_copy(
                    x1T_r[:, :, 128 * i:128 * (i + 1)],
                    pt.ap().rearrange("p (c n) -> p c n", c=8)), incs=(sDVE, 1))
                p.wait("vector", sPE)
                p.op("vector", lambda e, i=i, stg=stg: e.tensor_tensor(
                    stg[:, :], xs[:, 1024 * i:1024 * (i + 1)], stg[:, :], ALU.subtract),
                    incs=(sDVE, 1))
                dv_lo = sDVE.n
                p.wait("tensor", sDVE, dv_lo)
                for c in range(8):
                    p.op("tensor", lambda e, stg=stg, pt=pt, c=c: e.transpose(
                        pt[:, 128 * c:128 * (c + 1)], stg[:, 128 * c:128 * (c + 1)],
                        ident_bf[:, :]), incs=(sPE, 1) if c == 7 else ())
                p.wait("vector", sPE)
                p.op("vector", lambda e, i=i, pt=pt: e.tensor_copy(
                    xloT_r[:, :, 128 * i:128 * (i + 1)],
                    pt.ap().rearrange("p (c n) -> p c n", c=8)), incs=(sDVE, 1))

            wg_r = wghi.ap().rearrange("p (c n) -> p c n", c=8)
            wgl_r = wglo.ap().rearrange("p (c n) -> p c n", c=8)
            p.wait("tensor", sDVE)
            for ti in range(16):
                nmm = [0]
                p.wait("tensor", sDVE)
                for k in range(8):
                    for lh, rh in ((x1T_r[:, k, 128 * ti:128 * (ti + 1)], wg_r[:, k, :]),
                                   (x1T_r[:, k, 128 * ti:128 * (ti + 1)], wgl_r[:, k, :]),
                                   (xloT_r[:, k, 128 * ti:128 * (ti + 1)], wg_r[:, k, :])):
                        first, last = nmm[0] == 0, nmm[0] == 23
                        p.op("tensor", lambda e, lh=lh, rh=rh, first=first, last=last: e.matmul(
                            pL[:, :], lh, rh, start=first, stop=last),
                            incs=(sPE, 1) if last else ())
                        nmm[0] += 1
                p.wait("vector", sPE)
                p.op("vector", lambda e: e.tensor_tensor(rt0[:, :], pL[:, :], bg_sb[:, :], ALU.add), incs=(sDVE, 1))
                p.wait("vector", sDVE)
                p.op("vector", lambda e: e.tensor_reduce(rmx[:, :], rt0[:, :], AX.X, ALU.max), incs=(sDVE, 1))
                p.wait("vector", sDVE)
                p.op("vector", lambda e: e.tensor_scalar(rt1[:, :], rt0[:, :], rmx[:, :], None, ALU.is_ge), incs=(sDVE, 1))
                p.wait("vector", sDVE)
                p.op("vector", lambda e: e.tensor_copy(rt2[:, 0:1], rt1[:, 0:1]), incs=(sDVE, 1))
                p.op("vector", lambda e: e.tensor_tensor(rt2[:, 1:8], rt1[:, 1:8], rt1[:, 0:7], ALU.add), incs=(sDVE, 1))
                p.wait("vector", sDVE)
                p.op("vector", lambda e: e.tensor_copy(rt0[:, 0:2], rt2[:, 0:2]), incs=(sDVE, 1))
                p.op("vector", lambda e: e.tensor_tensor(rt0[:, 2:8], rt2[:, 2:8], rt2[:, 0:6], ALU.add), incs=(sDVE, 1))
                p.wait("vector", sDVE)
                p.op("vector", lambda e: e.tensor_copy(rt2[:, 0:4], rt0[:, 0:4]), incs=(sDVE, 1))
                p.op("vector", lambda e: e.tensor_tensor(rt2[:, 4:8], rt0[:, 4:8], rt0[:, 0:4], ALU.add), incs=(sDVE, 1))
                p.wait("vector", sDVE)
                p.op("vector", lambda e: e.tensor_scalar(rt0[:, :], rt2[:, :], 1.5, None, ALU.is_le), incs=(sDVE, 1))
                p.wait("vector", sDVE)
                p.op("vector", lambda e, ti=ti: e.tensor_tensor(
                    maskF[:, 8 * ti:8 * (ti + 1)], rt1[:, :], rt0[:, :], ALU.mult), incs=(sDVE, 1))
                p.wait("vector", sDVE)
                p.op("vector", lambda e, ti=ti: e.tensor_copy(mbf[:, :], maskF[:, 8 * ti:8 * (ti + 1)]), incs=(sDVE, 1))
                dvm = sDVE.n
                p.wait("tensor", sDVE, dvm)
                p.op("tensor", lambda e: e.transpose(pM8[:, :], mbf[:, :], ident_bf[:, :]), incs=(sPE, 1))
                p.wait("vector", sPE)
                p.op("vector", lambda e, ti=ti: e.tensor_copy(m8[:, 128 * ti:128 * (ti + 1)], pM8[:, :]), incs=(sDVE, 1))
            p.flush()

        with nc.sbuf_tensor("w1sb", [128, 8 * D], BF16) as w1sb, \
             nc.sbuf_tensor("w2sb", [128, 8 * D], BF16) as w2sb, \
             nc.sbuf_tensor("h_e", [128, 8 * TC], BF16) as h_e, \
             nc.psum_tensor("pMa", [128, 1024], F32) as pMa, \
             nc.psum_tensor("pMb", [128, 1024], F32) as pMb, \
             nc.psum_tensor("pY0", [128, 1024], F32) as pY0, \
             nc.psum_tensor("pY1", [128, 1024], F32) as pY1:
            p.barrier_all(allsems)
            pYs = [pY0, pY1]
            pMs = [pMa, pMb]
            p.wait("tensor", sDVE)
            py_free = [0, 0]
            for ti in range(16):
                pY = pYs[ti % 2]
                p.wait("tensor", sDVE, py_free[ti % 2])
                for cc in range(2):
                    p.op("tensor", lambda e, ti=ti, cc=cc, pY=pY: e.matmul(
                        pY[:, 512 * cc:512 * (cc + 1)], m8[:, 128 * ti:128 * (ti + 1)],
                        b2hi[:, 512 * cc:512 * (cc + 1)], start=True, stop=False))
                    p.op("tensor", lambda e, ti=ti, cc=cc, pY=pY: e.matmul(
                        pY[:, 512 * cc:512 * (cc + 1)], m8[:, 128 * ti:128 * (ti + 1)],
                        b2lo[:, 512 * cc:512 * (cc + 1)], start=False, stop=True),
                        incs=(sPE, 1) if cc == 1 else ())
                p.wait("vector", sPE)
                p.op("vector", lambda e, ti=ti, pY=pY: e.tensor_tensor(
                    xs[:, 1024 * ti:1024 * (ti + 1)], pY[:, :],
                    xs[:, 1024 * ti:1024 * (ti + 1)], ALU.add), incs=(sDVE, 1))
                py_free[ti % 2] = sDVE.n

            gelu_done = {}
            gl_i = [0]

            def load_w1(e_i, wait_pe):
                p.wait(g, sPE, wait_pe)
                for k in range(8):
                    p.dma(g, w1sb[:, 1024 * k:1024 * (k + 1)], w1[e_i, 128 * k:128 * (k + 1), :], sLD)
                return sLD.n

            def load_w2(e_i, wait_pe):
                p.wait(g, sPE, wait_pe)
                for k in range(8):
                    p.dma(g, w2sb[:, 1024 * k:1024 * (k + 1)], w2[e_i, 128 * k:128 * (k + 1), :], sLD)
                return sLD.n

            w1_ld = load_w1(0, 0)
            w2_ld = load_w2(0, 0)
            for e_i in range(8):
                for ic in range(8):
                    for th in range(2):
                        u = gl_i[0]
                        pM = pMs[u % 2]
                        p.wait("tensor", sACT, gelu_done.get(u - 2, 0))
                        if ic == 0 and th == 0:
                            p.wait("tensor", sLD, w1_ld)
                        for t5 in range(2):
                            for k in range(8):
                                p.op("tensor", lambda e, ic=ic, th=th, t5=t5, k=k, pM=pM: e.matmul(
                                    pM[:, 512 * t5:512 * (t5 + 1)],
                                    w1sb[:, 1024 * k + 128 * ic:1024 * k + 128 * (ic + 1)],
                                    x1T[:, 2048 * k + 1024 * th + 512 * t5:2048 * k + 1024 * th + 512 * (t5 + 1)],
                                    start=(k == 0), stop=(k == 7)),
                                    incs=(sPE, 1) if (t5 == 1 and k == 7) else ())
                        p.wait("scalar", sPE)
                        p.op("scalar", lambda e, e_i=e_i, ic=ic, th=th, pM=pM: e.activation(
                            h_e[:, 2048 * ic + 1024 * th:2048 * ic + 1024 * (th + 1)],
                            pM[:, :], AF.Gelu, bias=b1_sb[:, 8 * e_i + ic:8 * e_i + ic + 1],
                            scale=1.0), incs=(sACT, 1))
                        gelu_done[u] = sACT.n
                        gl_i[0] += 1
                a_need = sACT.n
                w1_free = sPE.n
                if e_i + 1 < 8:
                    w1_ld = load_w1(e_i + 1, w1_free)
                for ti in range(16):
                    pY = pYs[ti % 2]
                    p.wait("tensor", sDVE, py_free[ti % 2])
                    if ti == 0:
                        p.wait("tensor", sACT, a_need)
                        p.wait("tensor", sLD, w2_ld)
                    for ic in range(8):
                        for cc in range(2):
                            p.op("tensor", lambda e, ti=ti, ic=ic, cc=cc, pY=pY: e.matmul(
                                pY[:, 512 * cc:512 * (cc + 1)],
                                h_e[:, 2048 * ic + 128 * ti:2048 * ic + 128 * (ti + 1)],
                                w2sb[:, 1024 * ic + 512 * cc:1024 * ic + 512 * (cc + 1)],
                                start=(ic == 0), stop=(ic == 7)),
                                incs=(sPE, 1) if (ic == 7 and cc == 1) else ())
                    p.wait("vector", sPE)
                    p.op("vector", lambda e, ti=ti, e_i=e_i, pY=pY: e.scalar_tensor_tensor(
                        xs[:, 1024 * ti:1024 * (ti + 1)], pY[:, :],
                        maskF[:, 8 * ti + e_i:8 * ti + e_i + 1],
                        xs[:, 1024 * ti:1024 * (ti + 1)],
                        ALU.mult, ALU.add), incs=(sDVE, 1))
                    py_free[ti % 2] = sDVE.n
                w2_free = sPE.n
                if e_i + 1 < 8:
                    w2_ld = load_w2(e_i + 1, w2_free)
            moe_done = dict(py_free=list(py_free))
            final_res = sDVE.n
            emit_ln(lambda ti: p.wait("scalar", sDVE, final_res))
            p.flush()
        moe_stack.close()

        scan_stack = ctx.enter_context(contextlib.ExitStack())

        def sbS(name, shape, dtyp):
            return scan_stack.enter_context(nc.sbuf_tensor(name, shape, dtyp))

        wi_sb = sbS("wi_sb", [128, 8 * 128], BF16)
        wa_sb = sbS("wa_sb", [128, 8 * 128], BF16)
        bi8_sb = sbS("bi8_sb", [128, 8], F32)
        ba8_sb = sbS("ba8_sb", [128, 8], F32)
        dc8 = sbS("dc8", [128, 8], F32)
        ctab = sbS("ctab", [128, 16], F32)
        gath = sbS("gath", [128, 16], F32)
        carry = sbS("carry", [128, 16], F32)
        cmask_sb = sbS("cmask_sb", [128, 16], F32)
        cneut_sb = sbS("cneut_sb", [128, 16], F32)

        for hb in range(8):
            p.dma(g, wi_sb[:, 128 * hb:128 * (hb + 1)], w_in_a[hb], sLD)
            p.dma(g, wa_sb[:, 128 * hb:128 * (hb + 1)], w_a_a[hb], sLD)
        p.dma("sync", bi8_sb[:, :], b_in_c8, sLD)
        p.dma("sync", ba8_sb[:, :], b_a_c8, sLD)
        p.dma("sync", dc8[:, :], ap_c8, sLD)
        p.dma("sync", cmask_sb[:, :], cmask_in, sLD)
        p.dma("sync", cneut_sb[:, :], cneut_in, sLD)
        n_pre3 = sLD.n
        p.wait("vector", sLD, n_pre3)
        p.wait("scalar", sLD, n_pre3)
        p.op("vector", lambda e: e.tensor_scalar(ctab[:, 0:8], dc8[:, :], -1.0, None, ALU.mult), incs=(sDVE, 1))
        p.wait("vector", sDVE)
        p.op("vector", lambda e: e.tensor_tensor(ctab[:, 0:8], ctab[:, 0:8], dc8[:, :], ALU.max), incs=(sDVE, 1))
        p.op("vector", lambda e: e.tensor_scalar(ctab[:, 8:16], dc8[:, :], 0.0, None, ALU.max), incs=(sDVE, 1))
        dv = sDVE.n
        p.wait("scalar", sDVE, dv)
        p.op("scalar", lambda e: e.activation(ctab[:, 0:8], ctab[:, 0:8], AF.Exp, bias=0.0, scale=-1.0), incs=(sACT, 1))
        p.wait("scalar", sACT)
        p.op("scalar", lambda e: e.activation(ctab[:, 0:8], ctab[:, 0:8], AF.Ln, bias=epsc[:, 2:3], scale=1.0), incs=(sACT, 1))
        p.wait("vector", sACT)
        p.op("vector", lambda e: e.tensor_tensor(dc8[:, :], ctab[:, 0:8], ctab[:, 8:16], ALU.add), incs=(sDVE, 1))
        p.wait("vector", sDVE)
        p.op("vector", lambda e: e.tensor_scalar(dc8[:, :], dc8[:, :], 1e-3, None, ALU.add), incs=(sDVE, 1))

        xF2 = sbS("xF2", [128, TC], F32)
        gx2 = sbS("gx2", [128, TC], F32)
        ga2 = sbS("ga2", [128, TC], F32)
        with nc.sbuf_tensor("xF", [128, TC], F32) as xF, \
             nc.sbuf_tensor("xFb", [128, TC], BF16) as xFb, \
             nc.sbuf_tensor("gx", [128, TC], F32) as gx, \
             nc.sbuf_tensor("ga", [128, TC], F32) as ga, \
             nc.sbuf_tensor("av", [128, TC], F32) as av, \
             nc.sbuf_tensor("mlt", [128, TC], F32) as mlt, \
             nc.sbuf_tensor("nx", [128, TC], F32) as nx, \
             nc.sbuf_tensor("cpb", [128, TC + 8], F32) as cpb, \
             nc.sbuf_tensor("Sb", [128, TC], F32) as Sb, \
             nc.sbuf_tensor("zeros", [128, TC], F32) as zeros, \
             nc.psum_tensor("pT3", [128, 2048], F32) as pT3, \
             nc.psum_tensor("pG1", [128, 2048], F32) as pG1:
            p.barrier_all(allsems)
            p.op("vector", lambda e: e.memset(zeros[:, :], 0.0), incs=(sDVE, 1))

            xFs = [xF, xF2]
            gxs = [gx, gx2]
            gas = [ga, ga2]
            sig_done = {}
            xfcopy_done = {}
            gatemm_done = {}
            bk_xf_free = {}
            bk_gx_free = {}
            bk_ga_free = {}
            bk_hadd = {}

            def scan_front(i, cc, pass2):
                par = i % 2
                xFp, gxp, gap = xFs[par], gxs[par], gas[par]
                p.wait("tensor", sACT, sig_done.get(i - 1, 0))
                p.wait("tensor", sDVE, max(bk_hadd.get(i - 2, 0), bk_xf_free.get(i - 2, 0)))
                for tt in range(16):
                    slot = tt % 4
                    if slot == 0 and tt > 0:
                        p.wait("tensor", sDVE)
                    p.op("tensor", lambda e, tt=tt, cc=cc, slot=slot: e.transpose(
                        pT3[:, 512 * slot:512 * slot + 128],
                        xs[:, 1024 * tt + 128 * cc:1024 * tt + 128 * (cc + 1)],
                        ident_f[:, :]), incs=(sPE, 1))
                    if slot == 3:
                        p.wait("vector", sPE)
                        p.op("vector", lambda e, tt=tt, xFp=xFp: e.tensor_copy(
                            xFp[:, 128 * (tt - 3):128 * (tt + 1)],
                            pT3.ap().rearrange("p (c n) -> p c n", c=4)[:, :, 0:128]),
                            incs=(sDVE, 1))
                p.wait("vector", sPE, gatemm_done.get(i - 1, 0))
                p.wait("vector", sDVE)
                p.op("vector", lambda e, xFp=xFp: e.tensor_copy(xFb[:, :], xFp[:, :]), incs=(sDVE, 1))
                xfcopy_done[i] = dvb = sDVE.n
                for wsb, bsb, dst, pG in ((wi_sb, bi8_sb, gxp, pT3), (wa_sb, ba8_sb, gap, pG1)):
                    p.wait("tensor", sDVE, dvb)
                    for t5 in range(4):
                        p.op("tensor", lambda e, cc=cc, wsb=wsb, t5=t5, pG=pG: e.matmul(
                            pG[:, 512 * t5:512 * (t5 + 1)],
                            wsb[:, 128 * cc:128 * (cc + 1)],
                            xFb[:, 512 * t5:512 * (t5 + 1)],
                            start=True, stop=True),
                            incs=(sPE, 1) if t5 == 3 else ())
                    p.wait("scalar", sPE)
                    if dst is gxp:
                        p.wait("scalar", sDVE, bk_gx_free.get(i - 2, 0))
                    else:
                        p.wait("scalar", sACT, bk_ga_free.get(i - 2, 0))
                    p.op("scalar", lambda e, cc=cc, bsb=bsb, dst=dst, pG=pG: e.activation(
                        dst[:, :], pG[:, :], AF.Sigmoid,
                        bias=bsb[:, cc:cc + 1], scale=1.0), incs=(sACT, 1))
                gatemm_done[i] = sPE.n
                sig_done[i] = sACT.n

            def scan_back(i, cc, pass2):
                par = i % 2
                xFp, gxp, gap = xFs[par], gxs[par], gas[par]
                p.wait("vector", sACT, sig_done[i])
                p.op("vector", lambda e, cc=cc, gap=gap: e.tensor_scalar(
                    gap[:, :], gap[:, :], dc8[:, cc:cc + 1], -1.0, ALU.mult, ALU.mult), incs=(sDVE, 1))
                dv = sDVE.n
                p.wait("scalar", sDVE, dv)
                p.op("scalar", lambda e, gap=gap: e.activation(av[:, :], gap[:, :], AF.Exp, bias=0.0, scale=1.0), incs=(sACT, 1))
                bk_ga_free[i] = sACT.n
                p.wait("vector", sACT)
                p.op("vector", lambda e: e.tensor_scalar(av[:, :], av[:, :], 0.9999899864196777, None, ALU.min), incs=(sDVE, 1))
                p.wait("vector", sDVE)
                p.op("vector", lambda e: e.scalar_tensor_tensor(
                    mlt[:, :], av[:, :], -1.0, av[:, :], ALU.mult, ALU.mult), incs=(sDVE, 1))
                p.wait("vector", sDVE)
                p.op("vector", lambda e: e.tensor_scalar(mlt[:, :], mlt[:, :], 1.0, 1e-5, ALU.add, ALU.max), incs=(sDVE, 1))
                dv = sDVE.n
                p.wait("scalar", sDVE, dv)
                p.op("scalar", lambda e: e.activation(mlt[:, :], mlt[:, :], AF.Ln, bias=0.0, scale=1.0), incs=(sACT, 1))
                p.wait("scalar", sACT)
                p.op("scalar", lambda e: e.activation(mlt[:, :], mlt[:, :], AF.Exp, bias=0.0, scale=0.5), incs=(sACT, 1))
                p.wait("vector", sACT)
                p.op("vector", lambda e, gxp=gxp, xFp=xFp: e.tensor_tensor(nx[:, :], gxp[:, :], xFp[:, :], ALU.mult), incs=(sDVE, 1))
                bk_xf_free[i] = sDVE.n
                p.wait("vector", sDVE)
                p.op("vector", lambda e: e.tensor_tensor(nx[:, :], nx[:, :], mlt[:, :], ALU.mult), incs=(sDVE, 1))
                p.wait("vector", sDVE)
                if pass2:
                    p.op("vector", lambda e, cc=cc: e.tensor_copy(cpb[:, 0:1], carry[:, cc:cc + 1]), incs=(sDVE, 1))
                    p.wait("vector", sDVE)
                    p.op("vector", lambda e, cc=cc: e.tensor_tensor_scan(
                        cpb[:, 1:TC + 1], av[:, :], zeros[:, :], carry[:, cc:cc + 1],
                        ALU.mult, ALU.add), incs=(sDVE, 1))
                else:
                    p.op("vector", lambda e: e.memset(cpb[:, 0:1], 1.0), incs=(sDVE, 1))
                    p.wait("vector", sDVE)
                    p.op("vector", lambda e: e.tensor_tensor_scan(
                        cpb[:, 1:TC + 1], av[:, :], zeros[:, :], 1.0,
                        ALU.mult, ALU.add), incs=(sDVE, 1))
                p.wait("vector", sDVE)
                p.op("vector", lambda e: e.tensor_scalar(mlt[:, :], cpb[:, 0:TC], 1e-7, None, ALU.max), incs=(sDVE, 1))
                dv = sDVE.n
                p.wait("scalar", sDVE, dv)
                p.op("scalar", lambda e, gxp=gxp: e.activation(gxp[:, :], mlt[:, :], AF.Ln, bias=0.0, scale=1.0), incs=(sACT, 1))
                p.wait("scalar", sACT)
                p.op("scalar", lambda e, gxp=gxp: e.activation(gxp[:, :], gxp[:, :], AF.Exp, bias=0.0, scale=-1.0), incs=(sACT, 1))
                p.wait("vector", sACT)
                p.op("vector", lambda e, gxp=gxp: e.scalar_tensor_tensor(
                    nx[:, :], nx[:, :], 1e-7, gxp[:, :], ALU.add, ALU.mult), incs=(sDVE, 1))
                bk_gx_free[i] = sDVE.n
                p.wait("vector", sDVE)
                if pass2:
                    p.op("vector", lambda e, cc=cc: e.tensor_tensor_scan(
                        Sb[:, :], nx[:, :], zeros[:, :], carry[:, 8 + cc:8 + cc + 1],
                        ALU.add, ALU.add), incs=(sDVE, 1))
                else:
                    p.op("vector", lambda e: e.tensor_tensor_scan(
                        Sb[:, :], nx[:, :], zeros[:, :], 0.0,
                        ALU.add, ALU.add), incs=(sDVE, 1))
                p.wait("vector", sDVE)
                if not pass2:
                    p.op("vector", lambda e, cc=cc: e.tensor_copy(ctab[:, cc:cc + 1], cpb[:, TC:TC + 1]), incs=(sDVE, 1))
                    p.op("vector", lambda e, cc=cc: e.tensor_copy(ctab[:, 8 + cc:8 + cc + 1], Sb[:, TC - 1:TC]), incs=(sDVE, 1))
                    bk_hadd[i] = sDVE.n
                else:
                    p.op("vector", lambda e: e.tensor_tensor(Sb[:, :], Sb[:, :], mlt[:, :], ALU.mult), incs=(sDVE, 1))
                    hv_done = sDVE.n
                    p.wait("tensor", sDVE, hv_done)
                    p.wait("tensor", sACT, sig_done.get(i + 1, sig_done[i]))
                    for tt in range(16):
                        slot = tt % 4
                        if slot == 0:
                            p.wait("tensor", sDVE)
                        p.op("tensor", lambda e, tt=tt, slot=slot: e.transpose(
                            pG1[:, 512 * slot:512 * slot + 128],
                            Sb[:, 128 * tt:128 * (tt + 1)], ident_f[:, :]), incs=(sPE, 1))
                        pe = sPE.n
                        p.wait("vector", sPE, pe)
                        p.op("vector", lambda e, tt=tt, cc=cc, slot=slot: e.tensor_tensor(
                            xs[:, 1024 * tt + 128 * cc:1024 * tt + 128 * (cc + 1)],
                            pG1[:, 512 * slot:512 * slot + 128],
                            xs[:, 1024 * tt + 128 * cc:1024 * tt + 128 * (cc + 1)],
                            ALU.add), incs=(sDVE, 1))
                    bk_hadd[i] = sDVE.n

            seq = [(cc, False) for cc in range(8)] + [(cc, True) for cc in range(8)]
            scan_front(0, *seq[0])
            for i in range(16):
                if i + 1 < 16:
                    scan_front(i + 1, *seq[i + 1])
                if i == 8:
                    dvt = bk_hadd[7]
                    p.wait("sync", sDVE, dvt)
                    p.dma("sync", cc_in[:, :], ctab[:, :], sST)
                    cin_done = sST.n
                    p.wait(g, sST, cin_done)
                    p.op(g, lambda e: e.collective_compute(
                        "AllGather", ALU.bypass,
                        replica_groups=[[0, 1], [2, 3], [4, 5], [6, 7]],
                        ins=[cc_in.ap().opt()], outs=[cc_out.ap().opt()]), incs=(sCC, 1))
                    p.wait(g, sCC, 1)
                    p.dma(g, gath[:, :], cc_out.ap()[0:128, :], sLD)
                    gth_done = sLD.n
                    p.wait("vector", sLD, gth_done)
                    p.op("vector", lambda e: e.tensor_tensor(carry[:, :], gath[:, :], cmask_sb[:, :], ALU.mult), incs=(sDVE, 1))
                    p.wait("vector", sDVE)
                    p.op("vector", lambda e: e.tensor_tensor(carry[:, :], carry[:, :], cneut_sb[:, :], ALU.add), incs=(sDVE, 1))
                    p.wait("vector", sDVE)
                scan_back(i, *seq[i])
            scan_res = sDVE.n
            emit_ln(lambda ti: p.wait("scalar", sDVE, scan_res))
            p.flush()
        scan_stack.close()

        bin_sb = sb("bin_sb", [64, 1], F32)
        w1_sbf = sb("w1_sbf", [64, 16], F32)
        b1_sbf = sb("b1_sbf", [64, 16], F32)
        w2_sbf = sb("w2_sbf", [64, 16], F32)
        b2_sbf = sb("b2_sbf", [64, 1], F32)
        win_sb = sb("win_sb", [128, 8 * 64], BF16)
        wout_sb = sb("wout_sb", [64, D], BF16)
        bout_hi = sb("bout_hi", [1, D], BF16)
        bout_lo = sb("bout_lo", [1, D], BF16)
        x3T = sb("x3T", [128, 8 * TC], BF16)
        hb2 = sb("hb2", [64, TC], F32)
        hb3 = sb("hb3", [64, TC], F32)

        p.dma("sync", bin_sb[:, :], bin_c, sLD)
        p.dma("sync", w1_sbf[:, :], w1c, sLD)
        p.dma("sync", b1_sbf[:, :], b1c, sLD)
        p.dma("sync", w2_sbf[:, :], w2c, sLD)
        p.dma("sync", b2_sbf[:, :], b2c, sLD)
        p.dma("sync", srow[0:1, 0:D], bout_row, sLD)
        for k in range(8):
            p.dma(g, win_sb[:, 64 * k:64 * (k + 1)], win[128 * k:128 * (k + 1), :], sLD)
        p.dma(g, wout_sb[:, :], wout, sLD)
        n_pre4 = sLD.n
        p.wait("vector", sLD, n_pre4)
        p.wait("scalar", sLD, n_pre4)
        p.op("vector", lambda e: e.tensor_copy(bout_hi[:, :], srow[0:1, 0:D]), incs=(sDVE, 1))
        p.wait("vector", sDVE)
        p.op("vector", lambda e: e.tensor_tensor(srow[0:1, 0:D], srow[0:1, 0:D], bout_hi[:, :], ALU.subtract), incs=(sDVE, 1))
        p.wait("vector", sDVE)
        p.op("vector", lambda e: e.tensor_copy(bout_lo[:, :], srow[0:1, 0:D]), incs=(sDVE, 1))

        with nc.sbuf_tensor("stgC", [128, 1024], BF16) as stgC, \
             nc.sbuf_tensor("stgD", [128, 1024], BF16) as stgD, \
             nc.sbuf_tensor("s_sb", [64, TC], F32) as s_sb, \
             nc.sbuf_tensor("hb0", [64, TC], F32) as hb0, \
             nc.sbuf_tensor("hb1", [64, TC], F32) as hb1, \
             nc.sbuf_tensor("ga0", [64, TC], F32) as ga0, \
             nc.sbuf_tensor("ga1", [64, TC], F32) as ga1, \
             nc.sbuf_tensor("gbf", [64, TC], BF16) as gbf, \
             nc.psum_tensor("pT0c", [128, 1024], BF16) as pT0c, \
             nc.psum_tensor("pT1c", [128, 1024], BF16) as pT1c, \
             nc.psum_tensor("pS0c", [64, 512], F32) as pS0c, \
             nc.psum_tensor("pS1c", [64, 512], F32) as pS1c, \
             nc.psum_tensor("pO0", [128, 1024], F32) as pO0, \
             nc.psum_tensor("pO1", [128, 1024], F32) as pO1:
            p.barrier_all(allsems)
            pTs = [pT0c, pT1c]
            stgs = [stgC, stgD]
            tp3_done = [0, 0]
            cp3_done = [0, 0]
            for i in range(16):
                sl = i % 2
                stg = stgs[sl]
                pt = pTs[sl]
                p.wait("vector", sPE, tp3_done[sl])
                p.op("vector", lambda e, i=i, stg=stg: e.tensor_copy(
                    stg[:, :], xs[:, 1024 * i:1024 * (i + 1)]), incs=(sDVE, 1))
                dvs = sDVE.n
                p.wait("tensor", sDVE, max(dvs, cp3_done[sl]))
                for c in range(8):
                    p.op("tensor", lambda e, stg=stg, pt=pt, c=c: e.transpose(
                        pt[:, 128 * c:128 * (c + 1)], stg[:, 128 * c:128 * (c + 1)],
                        ident_bf[:, :]), incs=(sPE, 1) if c == 7 else ())
                tp3_done[sl] = sPE.n
                p.wait("vector", sPE, tp3_done[sl])
                p.op("vector", lambda e, i=i, pt=pt: e.tensor_copy(
                    x3T.ap().rearrange("p (c n) -> p c n", c=8)[:, :, 128 * i:128 * (i + 1)],
                    pt.ap().rearrange("p (c n) -> p c n", c=8)), incs=(sDVE, 1))
                cp3_done[sl] = sDVE.n

            pSs = [pS0c, pS1c]
            pOs = [pO0, pO1]
            p.wait("tensor", sDVE)
            s_done = {}
            for ts in range(4):
                pS = pSs[ts % 2]
                if ts >= 2:
                    p.wait("tensor", sDVE, s_done[ts - 2])
                for k in range(8):
                    p.op("tensor", lambda e, ts=ts, k=k, pS=pS: e.matmul(
                        pS[:, :], win_sb[:, 64 * k:64 * (k + 1)],
                        x3T[:, 2048 * k + 512 * ts:2048 * k + 512 * (ts + 1)],
                        start=(k == 0), stop=(k == 7)),
                        incs=(sPE, 1) if k == 7 else ())
                p.wait("vector", sPE)
                p.op("vector", lambda e, ts=ts, pS=pS: e.tensor_scalar(
                    s_sb[:, 512 * ts:512 * (ts + 1)], pS[:, :], bin_sb[:, 0:1], None,
                    ALU.add), incs=(sDVE, 1))
                s_done[ts] = sDVE.n

            all_s = sDVE.n
            p.wait("scalar", sDVE, all_s)
            gas = [ga0, ga1]
            hbs = [hb0, hb1, hb2, hb3]
            stt_done = {}
            for hh in range(16):
                hbuf = hbs[hh % 4]
                p.wait("scalar", sDVE, stt_done.get(hh - 4, 0))
                p.op("scalar", lambda e, hh=hh, hbuf=hbuf: e.activation(
                    hbuf[:, :], s_sb[:, :], AF.Relu, bias=b1_sbf[:, hh:hh + 1],
                    scale=w1_sbf[:, hh:hh + 1]), incs=(sACT, 1))
                av_ = sACT.n
                p.wait("vector", sACT, av_)
                if hh == 0:
                    p.op("vector", lambda e, hbuf=hbuf: e.tensor_scalar(
                        gas[0][:, :], hbuf[:, :], w2_sbf[:, 0:1], None, ALU.mult), incs=(sDVE, 1))
                else:
                    src_g, dst_g = gas[(hh + 1) % 2], gas[hh % 2]
                    p.op("vector", lambda e, hh=hh, hbuf=hbuf, src_g=src_g, dst_g=dst_g: e.scalar_tensor_tensor(
                        dst_g[:, :], hbuf[:, :], w2_sbf[:, hh:hh + 1], src_g[:, :],
                        ALU.mult, ALU.add), incs=(sDVE, 1))
                stt_done[hh] = sDVE.n
            gfin = gas[15 % 2]
            p.op("vector", lambda e, gfin=gfin: e.tensor_scalar(
                gfin[:, :], gfin[:, :], b2_sbf[:, 0:1], None, ALU.add), incs=(sDVE, 1))
            p.wait("vector", sDVE)
            p.op("vector", lambda e, gfin=gfin: e.tensor_copy(gbf[:, :], gfin[:, :]), incs=(sDVE, 1))
            g_done = sDVE.n

            p.wait("tensor", sDVE, g_done)
            o_done = {}
            for ti in range(16):
                pO = pOs[ti % 2]
                if ti >= 2:
                    p.wait("tensor", sDVE, o_done[ti - 2])
                for cc in range(2):
                    p.op("tensor", lambda e, ti=ti, cc=cc, pO=pO: e.matmul(
                        pO[:, 512 * cc:512 * (cc + 1)], gbf[:, 128 * ti:128 * (ti + 1)],
                        wout_sb[:, 512 * cc:512 * (cc + 1)], start=True, stop=False))
                    p.op("tensor", lambda e, cc=cc, pO=pO: e.matmul(
                        pO[:, 512 * cc:512 * (cc + 1)], onesr_bf[:, :],
                        bout_hi[:, 512 * cc:512 * (cc + 1)], start=False, stop=False))
                    p.op("tensor", lambda e, cc=cc, pO=pO: e.matmul(
                        pO[:, 512 * cc:512 * (cc + 1)], onesr_bf[:, :],
                        bout_lo[:, 512 * cc:512 * (cc + 1)], start=False, stop=True),
                        incs=(sPE, 1) if cc == 1 else ())
                p.wait("vector", sPE)
                p.op("vector", lambda e, ti=ti, pO=pO: e.tensor_tensor(
                    xs[:, 1024 * ti:1024 * (ti + 1)], pO[:, :],
                    xs[:, 1024 * ti:1024 * (ti + 1)], ALU.add), incs=(sDVE, 1))
                o_done[ti] = sDVE.n
            emit_ln(lambda ti: p.wait("scalar", sDVE, o_done[ti]))
            ln4_final = sDVE.n
            for ti in range(16):
                p.wait("sync", sDVE, ln4_final - (15 - ti))
                p.dma("sync", out_t[128 * ti:128 * (ti + 1), :], xs[:, 1024 * ti:1024 * (ti + 1)], sST)
            p.wait("sync", sST)
            p.flush()
    return nc


def run_fused(inputs):
    ones_row, ident, halves, e16 = _common_consts()
    ncF = _get("fused", build_fused)
    X = np.asarray(inputs["L0_attn_x"], np.float32)
    Wg = np.asarray(inputs["L0_ffn_Wg"], np.float32)
    bg = np.asarray(inputs["L0_ffn_bg"], np.float32)
    b1 = np.asarray(inputs["L0_ffn_b1"], np.float32)
    b2 = np.asarray(inputs["L0_ffn_b2"], np.float32)
    wg_st = np.ascontiguousarray(Wg.reshape(8, 128, 8).transpose(1, 0, 2).reshape(128, 64))
    bg_bc = np.ascontiguousarray(np.tile(bg.reshape(1, 8), (128, 1)))
    b1cols = np.ascontiguousarray(b1.reshape(8, 8, 128).transpose(2, 0, 1).reshape(128, 64))
    apv = np.asarray(inputs["L1_attn_a_param"], np.float32).reshape(8, 128)
    b_in = np.asarray(inputs["L1_attn_b_in"], np.float32)
    b_a = np.asarray(inputs["L1_attn_b_a"], np.float32)
    cm1 = np.ones((128, 16), np.float32)
    cm0 = np.zeros((128, 16), np.float32)
    cn_first = np.zeros((128, 16), np.float32)
    cn_first[:, 0:8] = 1.0
    in_maps = []
    for c in range(NCORES):
        b, hf = c // 2, c % 2
        in_maps.append(dict(
            x_own=np.ascontiguousarray(X[b, TC * hf:TC * (hf + 1)]),
            x_oth=np.ascontiguousarray(X[b, TC * (1 - hf):TC * (2 - hf)]),
            wq_d=np.asarray(inputs["L0_attn_wq_d"], np.float32),
            w_qk=np.asarray(inputs["L0_attn_w_qk"], np.float32),
            wkv_d=np.asarray(inputs["L0_attn_wkv_d"], np.float32),
            wv_u=np.asarray(inputs["L0_attn_wv_u"], np.float32),
            w_proj=np.asarray(inputs["L0_attn_w_proj"], np.float32),
            bq_cols=np.ascontiguousarray(np.asarray(inputs["L0_attn_bq_d"], np.float32).reshape(4, 128).T),
            bqk_cols=np.ascontiguousarray(np.asarray(inputs["L0_attn_b_qk"], np.float32).reshape(16, 64).T),
            bkv_col=np.asarray(inputs["L0_attn_bkv_d"], np.float32).reshape(64, 1),
            bvu_row=np.asarray(inputs["L0_attn_bv_u"], np.float32).reshape(1, D),
            bprj_row=np.asarray(inputs["L0_attn_b_proj"], np.float32).reshape(1, D),
            ones_row=ones_row, ident_in=ident, halves=halves, e16=e16,
            wg_st=wg_st, bg_bc=bg_bc,
            w1=np.asarray(inputs["L0_ffn_W1"], np.float32),
            w2=np.asarray(inputs["L0_ffn_W2"], np.float32),
            b1cols=b1cols, b2m=b2,
            w_in_a=np.asarray(inputs["L1_attn_w_in"], np.float32),
            w_a_a=np.asarray(inputs["L1_attn_w_a"], np.float32),
            b_in_c8=np.ascontiguousarray(b_in.T),
            b_a_c8=np.ascontiguousarray(b_a.T),
            ap_c8=np.ascontiguousarray(apv.T),
            cmask_in=cm1 if hf == 1 else cm0,
            cneut_in=cm0 if hf == 1 else cn_first,
            win=np.asarray(inputs["L1_ffn_Win"], np.float32),
            bin_c=np.asarray(inputs["L1_ffn_bin_"], np.float32).reshape(64, 1),
            w1c=np.asarray(inputs["L1_ffn_W1"], np.float32),
            b1c=np.asarray(inputs["L1_ffn_b1"], np.float32),
            w2c=np.asarray(inputs["L1_ffn_W2"], np.float32),
            b2c=np.asarray(inputs["L1_ffn_b2"], np.float32).reshape(64, 1),
            wout=np.asarray(inputs["L1_ffn_Wout"], np.float32),
            bout_row=np.asarray(inputs["L1_ffn_bout"], np.float32).reshape(1, D),
        ))
    res = run_bass_kernel_spmd(ncF, in_maps, core_ids=list(range(NCORES)))
    out = np.zeros((B, T, D), np.float32)
    for c in range(NCORES):
        b, hf = c // 2, c % 2
        out[b, TC * hf:TC * (hf + 1)] = res.results[c]["out"]
    return out



_cache = {}


def _get(name, builder):
    if name not in _cache:
        _cache[name] = builder()
    return _cache[name]


def _common_consts():
    ones_row = np.ones((1, 128), np.float32)
    ident = np.eye(128, dtype=np.float32)
    halves = np.zeros((128, 2), np.float32)
    halves[0:64, 0] = 1.0
    halves[64:128, 1] = 1.0
    e16 = np.zeros((8, 16, 128), np.float32)
    for c in range(8):
        for r in range(128):
            e16[c, 2 * c + r // 64, r] = 1.0
    return ones_row, ident, halves, e16


def run_l1(inputs, X):
    ones_row, ident, halves, e16 = _common_consts()
    nc1 = _get("l1", build_launch1)
    in_maps = []
    for c in range(NCORES):
        b, hf = c // 2, c % 2
        m = dict(
            x_own=np.ascontiguousarray(X[b, TC * hf:TC * (hf + 1)]),
            x_oth=np.ascontiguousarray(X[b, TC * (1 - hf):TC * (2 - hf)]),
            wq_d=np.asarray(inputs["L0_attn_wq_d"], np.float32),
            w_qk=np.asarray(inputs["L0_attn_w_qk"], np.float32),
            wkv_d=np.asarray(inputs["L0_attn_wkv_d"], np.float32),
            wv_u=np.asarray(inputs["L0_attn_wv_u"], np.float32),
            w_proj=np.asarray(inputs["L0_attn_w_proj"], np.float32),
            bq_cols=np.ascontiguousarray(np.asarray(inputs["L0_attn_bq_d"], np.float32).reshape(4, 128).T),
            bqk_cols=np.ascontiguousarray(np.asarray(inputs["L0_attn_b_qk"], np.float32).reshape(8, 128).T),
            bkv_col=np.asarray(inputs["L0_attn_bkv_d"], np.float32).reshape(64, 1),
            bvu_row=np.asarray(inputs["L0_attn_bv_u"], np.float32).reshape(1, D),
            bprj_row=np.asarray(inputs["L0_attn_b_proj"], np.float32).reshape(1, D),
            ones_row=ones_row, ident_in=ident, halves=halves, e16=e16,
        )
        in_maps.append(m)
    res = run_bass_kernel_spmd(nc1, in_maps, core_ids=list(range(NCORES)))
    x1 = np.zeros((B, T, D), np.float32)
    for c in range(NCORES):
        b, hf = c // 2, c % 2
        x1[b, TC * hf:TC * (hf + 1)] = res.results[c]["x1"]
    return x1


def run_l2(inputs, x1_full):
    ones_row, ident, halves, e16 = _common_consts()
    nc2 = _get("l2", build_launch2)
    Wg = np.asarray(inputs["L0_ffn_Wg"], np.float32)
    bg = np.asarray(inputs["L0_ffn_bg"], np.float32)
    b1 = np.asarray(inputs["L0_ffn_b1"], np.float32)
    b2 = np.asarray(inputs["L0_ffn_b2"], np.float32)
    wg_st = np.ascontiguousarray(Wg.reshape(8, 128, 8).transpose(1, 0, 2).reshape(128, 64))
    bg_bc = np.ascontiguousarray(np.tile(bg.reshape(1, 8), (128, 1)))
    b1cols = np.ascontiguousarray(b1.reshape(8, 8, 128).transpose(2, 0, 1).reshape(128, 64))
    w1 = np.asarray(inputs["L0_ffn_W1"], np.float32)
    w2 = np.asarray(inputs["L0_ffn_W2"], np.float32)
    in_maps = []
    for c in range(NCORES):
        b, hf = c // 2, c % 2
        in_maps.append(dict(
            x1=np.ascontiguousarray(x1_full[b, TC * hf:TC * (hf + 1)]),
            wg_st=wg_st, bg_bc=bg_bc, w1=w1, w2=w2, b1cols=b1cols, b2m=b2,
            ones_row=ones_row, ident_in=ident,
        ))
    res = run_bass_kernel_spmd(nc2, in_maps, core_ids=list(range(NCORES)))
    x2 = np.zeros((B, T, D), np.float32)
    for c in range(NCORES):
        b, hf = c // 2, c % 2
        x2[b, TC * hf:TC * (hf + 1)] = res.results[c]["x2"]
    return x2


def run_l3(inputs, x2_full):
    ones_row, ident, halves, e16 = _common_consts()
    nc3 = _get("l3", build_launch3)
    w_in = np.asarray(inputs["L1_attn_w_in"], np.float32)
    w_a = np.asarray(inputs["L1_attn_w_a"], np.float32)
    b_in = np.asarray(inputs["L1_attn_b_in"], np.float32)
    b_a = np.asarray(inputs["L1_attn_b_a"], np.float32)
    ap = np.asarray(inputs["L1_attn_a_param"], np.float32).reshape(8, 128)
    in_maps = []
    for c in range(NCORES):
        b, lh = c // 2, c % 2
        in_maps.append(dict(
            x2s=np.ascontiguousarray(x2_full[b][:, 512 * lh:512 * (lh + 1)]),
            w_in_s=np.ascontiguousarray(w_in[4 * lh:4 * (lh + 1)]),
            w_a_s=np.ascontiguousarray(w_a[4 * lh:4 * (lh + 1)]),
            b_in_c=np.ascontiguousarray(b_in[4 * lh:4 * (lh + 1)].T),
            b_a_c=np.ascontiguousarray(b_a[4 * lh:4 * (lh + 1)].T),
            ap_c=np.ascontiguousarray(ap[4 * lh:4 * (lh + 1)].T),
            ident_in=ident,
        ))
    res = run_bass_kernel_spmd(nc3, in_maps, core_ids=list(range(NCORES)))
    h = np.zeros((B, T, D), np.float32)
    for c in range(NCORES):
        b, lh = c // 2, c % 2
        h[b][:, 512 * lh:512 * (lh + 1)] = res.results[c]["h"]
    return h


def run_l4(inputs, x2_full, h_full):
    ones_row, ident, halves, e16 = _common_consts()
    nc4 = _get("l4", build_launch4)
    in_maps = []
    for c in range(NCORES):
        b, hf = c // 2, c % 2
        in_maps.append(dict(
            x2=np.ascontiguousarray(x2_full[b, TC * hf:TC * (hf + 1)]),
            h=np.ascontiguousarray(h_full[b, TC * hf:TC * (hf + 1)]),
            win=np.asarray(inputs["L1_ffn_Win"], np.float32),
            bin_c=np.asarray(inputs["L1_ffn_bin_"], np.float32).reshape(64, 1),
            w1c=np.asarray(inputs["L1_ffn_W1"], np.float32),
            b1c=np.asarray(inputs["L1_ffn_b1"], np.float32),
            w2c=np.asarray(inputs["L1_ffn_W2"], np.float32),
            b2c=np.asarray(inputs["L1_ffn_b2"], np.float32).reshape(64, 1),
            wout=np.asarray(inputs["L1_ffn_Wout"], np.float32),
            bout_row=np.asarray(inputs["L1_ffn_bout"], np.float32).reshape(1, D),
            ones_row=ones_row, ident_in=ident,
        ))
    res = run_bass_kernel_spmd(nc4, in_maps, core_ids=list(range(NCORES)))
    out = np.zeros((B, T, D), np.float32)
    for c in range(NCORES):
        b, hf = c // 2, c % 2
        out[b, TC * hf:TC * (hf + 1)] = res.results[c]["out"]
    return out


def kernel(**inputs):
    return run_fused(inputs)


# version 18
# speedup vs baseline: 1.0147x; 1.0147x over previous
import contextlib
import numpy as np

import concourse.bass as bass
import concourse.mybir as mybir
from concourse.bass_utils import run_bass_kernel_spmd

F32 = mybir.dt.float32
BF16 = mybir.dt.bfloat16
AF = mybir.ActivationFunctionType
ALU = mybir.AluOpType
AX = mybir.AxisListType

B, T, D = 4, 4096, 1024
TC = 2048
H, HD, KV, QL = 16, 64, 64, 512
NCORES = 8

ENGINES = ("tensor", "vector", "scalar", "gpsimd", "sync")


class Sem:
    __slots__ = ("h", "n")

    def __init__(self, h):
        self.h = h
        self.n = 0


class DSem:
    __slots__ = ("hw", "sw")

    def __init__(self, hw, sw):
        self.hw = hw
        self.sw = sw

    @property
    def n(self):
        return (self.hw.n, self.sw.n)


class Prog:

    def __init__(self, nc, ctx):
        self.nc = nc
        self.ctx = ctx
        self.ops = {e: [] for e in ENGINES}

    def sem(self, name):
        return Sem(self.ctx.enter_context(self.nc.semaphore(name)))

    def dsem(self, name):
        return DSem(self.sem(name + "h"), self.sem(name + "s"))

    def op(self, eng, fn, incs=()):
        if isinstance(incs, tuple) and incs and isinstance(incs[0], (Sem, DSem)):
            incs = [incs]
        captured = []
        for s, k in incs:
            if isinstance(s, DSem):
                s = s.sw if eng == "gpsimd" else s.hw
            s.n += k
            captured.append((s.h, k))
        self.ops[eng].append(("op", fn, captured))

    def wait(self, eng, sem, val=None):
        if isinstance(sem, DSem):
            if isinstance(val, int):
                assert val == 0, "int wait value for DSem must be 0"
                return
            vhw, vsw = sem.n if val is None else val
            if vhw:
                self.ops[eng].append(("wait", sem.hw.h, vhw))
            if vsw:
                self.ops[eng].append(("wait", sem.sw.h, vsw))
            return
        v = sem.n if val is None else val
        self.ops[eng].append(("wait", sem.h, v))

    def barrier_all(self, sems):
        for eng in ENGINES:
            for s in sems:
                self.wait(eng, s)

    def dma(self, eng, out, in_, sem):
        self.op(eng, lambda e, o=out, i=in_: e.dma_start(out=o, in_=i),
                incs=(sem, 16))

    def flush(self):
        pend = {e: lst for e, lst in self.ops.items() if lst}
        if not pend:
            return
        with self.nc.Block() as block:
            for eng, lst in pend.items():
                def make(lst=lst):
                    def body(e):
                        for item in lst:
                            if item[0] == "wait":
                                e.wait_ge(item[1], item[2])
                            else:
                                r = item[1](e)
                                for h, k in item[2]:
                                    r = r.then_inc(h, k)
                    return body
                getattr(block, eng)(make())
        self.ops = {e: [] for e in ENGINES}



def build_launch1():
    nc = bass.Bass("TRN2", target_bir_lowering=False, detect_race_conditions=False)
    dt = nc.dram_tensor
    x_own = dt("x_own", [TC, D], F32, kind="ExternalInput").ap()
    x_oth = dt("x_oth", [TC, D], F32, kind="ExternalInput").ap()
    wq_d = dt("wq_d", [QL, D], F32, kind="ExternalInput").ap()
    w_qk = dt("w_qk", [D, QL], F32, kind="ExternalInput").ap()
    wkv_d = dt("wkv_d", [KV, D], F32, kind="ExternalInput").ap()
    wv_u = dt("wv_u", [D, KV], F32, kind="ExternalInput").ap()
    w_proj = dt("w_proj", [D, D], F32, kind="ExternalInput").ap()
    bq_cols = dt("bq_cols", [128, 4], F32, kind="ExternalInput").ap()
    bqk_cols = dt("bqk_cols", [128, 8], F32, kind="ExternalInput").ap()
    bkv_col = dt("bkv_col", [64, 1], F32, kind="ExternalInput").ap()
    bvu_row = dt("bvu_row", [1, D], F32, kind="ExternalInput").ap()
    bprj_row = dt("bprj_row", [1, D], F32, kind="ExternalInput").ap()
    ones_row = dt("ones_row", [1, 128], F32, kind="ExternalInput").ap()
    ident_in = dt("ident_in", [128, 128], F32, kind="ExternalInput").ap()
    halves = dt("halves", [128, 2], F32, kind="ExternalInput").ap()
    e16 = dt("e16", [8, 16, 128], F32, kind="ExternalInput").ap()
    x1_out = dt("x1", [TC, D], F32, kind="ExternalOutput").ap()

    with contextlib.ExitStack() as ctx:
        p = Prog(nc, ctx)
        sLD = p.dsem("sLD")
        sPE = p.sem("sPE")
        sACT = p.sem("sACT")
        sDVE = p.sem("sDVE")
        sST = p.dsem("sST")

        def sb(name, shape, dtyp):
            return ctx.enter_context(nc.sbuf_tensor(name, shape, dtyp))

        onesr_bf = sb("onesr_bf", [1, 128], BF16)
        ident_bf = sb("ident_bf", [128, 128], BF16)
        halves_bf = sb("halves_bf", [128, 2], BF16)
        e16_bf = sb("e16_bf", [16, 8 * 128], BF16)
        bq_sb = sb("bq_sb", [128, 4], F32)
        bqk_sb = sb("bqk_sb", [128, 8], F32)
        bkv_sb = sb("bkv_sb", [64, 1], F32)
        bprj_hi = sb("bprj_hi", [1, D], BF16)
        bprj_lo = sb("bprj_lo", [1, D], BF16)
        maxq = sb("maxq", [1, 16], F32)
        maxk = sb("maxk", [1, 2], F32)
        bndrow = sb("bndrow", [1, 16], F32)
        bnd_bf = sb("bnd_bf", [1, 16], BF16)
        negbnd = sb("negbnd", [128, 16], F32)
        epsc = sb("epsc", [128, 2], F32)
        srow = sb("srow", [64, 1024], F32)

        wq_dT = sb("wq_dT", [128, 8 * QL], BF16)
        w_qkT = sb("w_qkT", [128, 4 * D], BF16)
        wkv_dT = sb("wkv_dT", [128, 8 * KV], BF16)
        wv_uT = sb("wv_uT", [65, D], BF16)
        w_projT = sb("w_projT", [128, 8 * D], BF16)
        ckvT = sb("ckvT", [128, T], BF16)
        ckv_tok = sb("ckv_tok", [128, 32 * 65], BF16)
        qT = sb("qT", [128, 8 * TC], BF16)

        g = "gpsimd"
        p.dma(g, onesr_bf[:, :], ones_row, sLD)
        p.dma(g, ident_bf[:, :], ident_in, sLD)
        p.dma(g, halves_bf[:, :], halves, sLD)
        for c in range(8):
            p.dma(g, e16_bf[:, 128 * c:128 * (c + 1)], e16[c], sLD)
        p.dma("sync", bq_sb[:, :], bq_cols, sLD)
        p.dma("sync", bqk_sb[:, :], bqk_cols, sLD)
        p.dma("sync", bkv_sb[:, :], bkv_col, sLD)
        p.dma("sync", srow[0:1, 0:D], bprj_row, sLD)
        p.dma(g, wv_uT[64:65, :], bvu_row, sLD)
        n_pre = sLD.n

        p.op("vector", lambda e: e.memset(epsc[:, 0:1], 1e-20), incs=(sDVE, 1))
        p.op("vector", lambda e: e.memset(epsc[:, 1:2], 1e-5), incs=(sDVE, 1))
        p.wait("vector", sLD, n_pre)
        p.op("vector", lambda e: e.tensor_copy(bprj_hi[:, :], srow[0:1, 0:D]), incs=(sDVE, 1))
        p.wait("vector", sDVE)
        p.op("vector", lambda e: e.tensor_tensor(srow[0:1, 0:D], srow[0:1, 0:D], bprj_hi[:, :], ALU.subtract), incs=(sDVE, 1))
        p.wait("vector", sDVE)
        p.op("vector", lambda e: e.tensor_copy(bprj_lo[:, :], srow[0:1, 0:D]), incs=(sDVE, 1))

        with nc.psum_tensor("pT0", [128, 1024], BF16) as pT0, \
             nc.psum_tensor("pT1", [128, 1024], BF16) as pT1, \
             nc.sbuf_tensor("xT", [128, 8 * T], BF16) as xT, \
             nc.sbuf_tensor("c_qT", [128, 4 * TC], BF16) as c_qT, \
             nc.sbuf_tensor("qsq", [128, TC], BF16) as qsq, \
             nc.sbuf_tensor("ksq", [64, T], BF16) as ksq, \
             nc.sbuf_tensor("stgA", [128, 1024], BF16) as stgA, \
             nc.sbuf_tensor("stgB", [128, 1024], BF16) as stgB:
            pTs = [pT0, pT1]
            tcnt = [0]

            def tload(dram, R, C, dest_fn):
                nR = (R + 127) // 128
                nCc = (C + 127) // 128
                cw = min(128, C)
                assert C % 128 == 0 or nCc == 1
                for r in range(nR):
                    Rw = min(128, R - 128 * r)
                    stg = stgA if tcnt[0] % 2 == 0 else stgB
                    pt = pTs[tcnt[0] % 2]
                    tcnt[0] += 1
                    p.wait(g, sPE)
                    p.dma(g, stg[0:Rw, 0:C], dram[128 * r:128 * r + Rw, :], sLD)
                    p.wait("tensor", sLD)
                    p.wait("tensor", sDVE)
                    for c in range(nCc):
                        ccw = min(128, C - 128 * c)
                        p.op("tensor", lambda e, stg=stg, pt=pt, r=r, c=c, Rw=Rw, ccw=ccw: e.transpose(
                            pt[0:ccw, 128 * c:128 * c + Rw],
                            stg[0:Rw, 128 * c:128 * c + ccw],
                            ident_bf[0:Rw, 0:Rw]),
                            incs=(sPE, 1) if c == nCc - 1 else ())
                    p.wait("vector", sPE)
                    if nCc == 1:
                        src = pt[0:cw, 0:Rw]
                    else:
                        src = pt.ap().rearrange("p (c n) -> p c n", c=8)[:, 0:nCc, 0:Rw]
                    p.op("vector", lambda e, d=dest_fn(r), s=src: e.tensor_copy(d, s),
                         incs=(sDVE, 1))

            xT_r = xT.ap().rearrange("p (c n) -> p c n", c=8)
            tload(x_own, TC, D, lambda r: xT_r[:, :, 128 * r:128 * r + 128])
            tload(x_oth, TC, D, lambda r: xT_r[:, :, TC + 128 * r:TC + 128 * r + 128])
            wqT_r = wq_dT.ap().rearrange("p (c n) -> p c n", c=8)
            tload(wq_d, QL, D, lambda r: wqT_r[:, :, 128 * r:128 * r + 128])
            wqkT_r = w_qkT.ap().rearrange("p (c n) -> p c n", c=4)
            tload(w_qk, D, QL, lambda r: wqkT_r[:, 0:4, 128 * r:128 * r + 128])
            wkvT_r = wkv_dT.ap().rearrange("p (c n) -> p c n", c=8)
            tload(wkv_d, KV, D, lambda r: wkvT_r[:, :, 0:64])
            tload(wv_u, D, KV, lambda r: wv_uT[0:64, 128 * r:128 * r + 128])
            wpT_r = w_projT.ap().rearrange("p (c n) -> p c n", c=8)
            tload(w_proj, D, D, lambda r: wpT_r[:, :, 128 * r:128 * r + 128])

            with nc.psum_tensor("pmA", [128, 512], F32) as pmA, \
                 nc.psum_tensor("pmB", [128, 512], F32) as pmB, \
                 nc.psum_tensor("pq", [1, 2048], F32) as pq:
                pms = [pmA, pmB]
                pm_i = [0]
                p.wait("tensor", sDVE)
                p.wait("scalar", sLD, n_pre)

                def mm_chain(out_view, lhsTs, rhss, bias_ap):
                    pm = pms[pm_i[0] % 2]
                    pm_i[0] += 1
                    p.wait("tensor", sDVE)
                    nk = len(lhsTs)
                    M = out_view.shape[0]
                    N = out_view.shape[-1]
                    for k in range(nk):
                        p.op("tensor", lambda e, k=k, pm=pm, M=M, N=N, lh=lhsTs[k], rh=rhss[k], nk=nk: e.matmul(
                            pm[0:M, 0:N], lh, rh, start=(k == 0), stop=(k == nk - 1)),
                            incs=(sPE, 1) if k == nk - 1 else ())
                    p.wait("vector", sPE)
                    p.op("vector", lambda e, pm=pm, M=M, N=N, ov=out_view, b=bias_ap: e.tensor_scalar(
                        ov, pm[0:M, 0:N], b, None, ALU.add),
                        incs=(sDVE, 1))

                for sl in range(8):
                    mm_chain(ckvT[0:64, 512 * sl:512 * (sl + 1)],
                             [wkv_dT[:, 64 * k:64 * (k + 1)] for k in range(8)],
                             [xT[:, 4096 * k + 512 * sl: 4096 * k + 512 * (sl + 1)] for k in range(8)],
                             bkv_sb[:, 0:1])
                    pm_last = pms[(pm_i[0] - 1) % 2]
                    p.op("vector", lambda e, sl=sl, pm=pm_last: e.tensor_scalar(
                        ckvT[64:128, 512 * sl:512 * (sl + 1)], pm[0:64, 0:512],
                        bkv_sb[:, 0:1], None, ALU.add), incs=(sDVE, 1))
                for oc in range(4):
                    for ts in range(4):
                        mm_chain(c_qT[:, 2048 * oc + 512 * ts: 2048 * oc + 512 * (ts + 1)],
                                 [wq_dT[:, 512 * k + 128 * oc: 512 * k + 128 * (oc + 1)] for k in range(8)],
                                 [xT[:, 4096 * k + 512 * ts: 4096 * k + 512 * (ts + 1)] for k in range(8)],
                                 bq_sb[:, oc:oc + 1])
                for oc in range(8):
                    for ts in range(4):
                        mm_chain(qT[:, 2048 * oc + 512 * ts: 2048 * oc + 512 * (ts + 1)],
                                 [w_qkT[:, 1024 * k + 128 * oc: 1024 * k + 128 * (oc + 1)] for k in range(4)],
                                 [c_qT[:, 2048 * k + 512 * ts: 2048 * k + 512 * (ts + 1)] for k in range(4)],
                                 bqk_sb[:, oc:oc + 1])

                tok_r = ckv_tok.ap().rearrange("p (s n) -> p s n", s=32)
                p.wait("tensor", sDVE)
                for gq in range(8):
                    pt = pTs[gq % 2]
                    p.wait("tensor", sDVE)
                    for j in range(4):
                        st = 4 * gq + j
                        p.op("tensor", lambda e, st=st, j=j, pt=pt: e.transpose(
                            pt[:, 64 * j:64 * j + 64],
                            ckvT[0:64, 128 * st:128 * (st + 1)],
                            ident_bf[0:64, 0:64]),
                            incs=(sPE, 1) if j == 3 else ())
                    p.wait("vector", sPE)
                    p.op("vector", lambda e, gq=gq, pt=pt: e.tensor_copy(
                        tok_r[:, 4 * gq:4 * (gq + 1), 0:64],
                        pt.ap().rearrange("p (c n) -> p c n", c=16)[:, 0:4, 0:64]),
                        incs=(sDVE, 1))
                p.op("vector", lambda e: e.memset(tok_r[:, :, 64:65], 1.0), incs=(sDVE, 1))

                p.wait("vector", sACT)
                p.op("vector", lambda e: e.tensor_tensor(ksq[:, :], ckvT[0:64, :], ckvT[0:64, :], ALU.mult), incs=(sDVE, 1))
                p.wait("tensor", sDVE)
                for rnd in range(2):
                    for sl in range(4):
                        p.op("tensor", lambda e, rnd=rnd, sl=sl: e.matmul(
                            pq[0:1, 512 * sl:512 * (sl + 1)], halves_bf[0:64, 0:1],
                            ksq[0:64, 512 * (4 * rnd + sl):512 * (4 * rnd + sl + 1)],
                            start=True, stop=True),
                            incs=(sPE, 1) if sl == 3 else ())
                    p.wait("vector", sPE)
                    p.op("vector", lambda e, rnd=rnd: e.tensor_reduce(
                        maxk[0:1, rnd:rnd + 1], pq[0:1, :], AX.X, ALU.max), incs=(sDVE, 1))
                    p.wait("tensor", sDVE)
                p.wait("vector", sDVE)
                p.op("vector", lambda e: e.tensor_tensor(
                    maxk[0:1, 0:1], maxk[0:1, 0:1], maxk[0:1, 1:2], ALU.max), incs=(sDVE, 1))

                for h in range(16):
                    c, half = h // 2, h % 2
                    p.wait("vector", sDVE)
                    p.op("vector", lambda e, c=c: e.tensor_tensor(
                        qsq[:, :], qT[:, 2048 * c:2048 * (c + 1)], qT[:, 2048 * c:2048 * (c + 1)], ALU.mult),
                        incs=(sDVE, 1))
                    p.wait("tensor", sDVE)
                    for ts in range(4):
                        p.op("tensor", lambda e, ts=ts, half=half: e.matmul(
                            pq[0:1, 512 * ts:512 * (ts + 1)],
                            halves_bf[:, half:half + 1], qsq[:, 512 * ts:512 * (ts + 1)],
                            start=True, stop=True),
                            incs=(sPE, 1) if ts == 3 else ())
                    p.wait("vector", sPE)
                    p.op("vector", lambda e, h=h: e.tensor_reduce(
                        maxq[0:1, h:h + 1], pq[0:1, :], AX.X, ALU.max), incs=(sDVE, 1))
                    p.wait("tensor", sDVE)

                p.wait("vector", sDVE)
                p.op("vector", lambda e: e.tensor_scalar(
                    bndrow[:, :], maxq[:, :], maxk[0:1, 0:1], None, ALU.mult), incs=(sDVE, 1))
                p.wait("scalar", sDVE)
                p.op("scalar", lambda e: e.activation(bndrow[:, :], bndrow[:, :], AF.Ln, bias=epsc[0:1, 0:1], scale=1.0), incs=(sACT, 1))
                p.wait("scalar", sACT)
                p.op("scalar", lambda e: e.activation(bndrow[:, :], bndrow[:, :], AF.Exp, bias=0.0, scale=0.5), incs=(sACT, 1))
                p.wait("vector", sACT)
                p.op("vector", lambda e: e.tensor_scalar(
                    bndrow[:, :], bndrow[:, :], -0.13125, -0.5, ALU.mult, ALU.add), incs=(sDVE, 1))
                p.wait("vector", sDVE)
                p.op("vector", lambda e: e.tensor_copy(bnd_bf[:, :], bndrow[:, :]), incs=(sDVE, 1))
                p.wait("tensor", sDVE)
                p.op("tensor", lambda e: e.matmul(
                    pmA[0:128, 0:16], onesr_bf[:, :], bnd_bf[:, :], start=True, stop=True), incs=(sPE, 1))
                p.wait("vector", sPE)
                p.op("vector", lambda e: e.tensor_copy(negbnd[:, :], pmA[0:128, 0:16]), incs=(sDVE, 1))
                p.flush()

        sums16 = sb("sums16", [16, TC], F32)
        rec_hi = sb("rec_hi", [16, TC], BF16)
        rec_lo = sb("rec_lo", [16, TC], BF16)
        yallT = sb("yallT", [128, 8 * TC], BF16)

        groups = [(0, 4), (4, 2), (6, 4), (10, 2), (12, 4), (16, 2), (18, 4),
                  (22, 2), (24, 4), (28, 2), (30, 2)]
        NG = len(groups)

        with nc.sbuf_tensor("stripA", [128, 2048], BF16) as stripA, \
             nc.sbuf_tensor("stripB", [128, 1024], BF16) as stripB, \
             nc.sbuf_tensor("zsb0", [65, 512], BF16) as zsb0, \
             nc.sbuf_tensor("zsb1", [65, 512], BF16) as zsb1, \
             nc.psum_tensor("psA", [128, 2048], F32) as psA, \
             nc.psum_tensor("psB", [128, 1024], F32) as psB, \
             nc.psum_tensor("psZ0", [65, 512], F32) as psZ0, \
             nc.psum_tensor("psZ1", [65, 512], F32) as psZ1:
            zsbs = [zsb0, zsb1]
            psG = [psA, psB]
            psZ = [psZ0, psZ1]
            tok_r = ckv_tok.ap().rearrange("p (s n) -> p s n", s=32)

            last_exp = [0, 0]
            scores_done = {}
            exp_done = {}
            yraw_done = [0, 0]
            srow_dma = [0, 0]
            prev_iter = [None]

            p.wait("tensor", sACT)
            p.wait("tensor", sDVE)
            p.wait("scalar", sDVE)

            def emit_scores(h, t5, gi):
                s0, glen = groups[gi]
                c, qrow = h // 2, (h % 2) * 64
                par = gi % 2
                pg = psG[par]
                p.wait("tensor", sACT, last_exp[par])
                for j in range(glen):
                    st = s0 + j
                    p.op("tensor", lambda e, st=st, j=j, pg=pg, c=c, qrow=qrow, t5=t5: e.matmul(
                        pg[:, 512 * j:512 * (j + 1)],
                        ckvT[qrow:qrow + 64, 128 * st:128 * (st + 1)],
                        qT[qrow:qrow + 64, 2048 * c + 512 * t5: 2048 * c + 512 * (t5 + 1)],
                        start=True, stop=True),
                        incs=(sPE, 1) if j == glen - 1 else ())
                scores_done[gi] = sPE.n

            def emit_exp(h, t5, gi):
                s0, glen = groups[gi]
                par = gi % 2
                pg = psG[par]
                strip = stripA if par == 0 else stripB
                p.wait("scalar", sPE, scores_done[gi])
                p.op("scalar", lambda e, pg=pg, strip=strip, glen=glen, h=h: e.activation(
                    strip[:, 0:512 * glen], pg[:, 0:512 * glen], AF.Exp,
                    bias=negbnd[:, h:h + 1], scale=0.125),
                    incs=(sACT, 1))
                exp_done[gi] = sACT.n
                last_exp[par] = sACT.n

            def emit_stage1(h, t5, gi, zp):
                s0, glen = groups[gi]
                par = gi % 2
                strip = stripA if par == 0 else stripB
                p.wait("tensor", sACT, exp_done[gi])
                if gi == 0:
                    p.wait("tensor", sDVE, yraw_done[zp])
                for j in range(glen):
                    st = s0 + j
                    p.op("tensor", lambda e, st=st, j=j, strip=strip, zp=zp,
                         first=(gi == 0 and j == 0), last=(gi == NG - 1 and j == glen - 1): e.matmul(
                        psZ[zp][:, :], tok_r[:, st, :], strip[:, 512 * j:512 * (j + 1)],
                        start=first, stop=last),
                        incs=(sPE, 1) if j == glen - 1 else ())

            def emit_tail(h, t5, zp):
                p.wait("scalar", sPE)
                p.op("scalar", lambda e, zp=zp: e.activation(
                    zsbs[zp][:, :], psZ[zp][:, :], AF.Copy, bias=0.0, scale=1.0),
                    incs=(sACT, 1))
                a_after = sACT.n
                slot = (4 * h + t5) % 2
                p.wait("vector", sACT, a_after)
                p.wait("vector", sST, srow_dma[slot])
                p.op("vector", lambda e, zp=zp, slot=slot: e.tensor_copy(
                    srow[0:1, 512 * slot:512 * (slot + 1)], psZ[zp][64:65, :]),
                    incs=(sDVE, 1))
                d_after = sDVE.n
                p.wait("sync", sDVE, d_after)
                p.dma("sync", sums16[h:h + 1, 512 * t5:512 * (t5 + 1)],
                      srow[0:1, 512 * slot:512 * (slot + 1)], sST)
                srow_dma[slot] = sST.n
                return d_after, a_after

            def emit_stage2(info):
                zp, h, t5, d_after, a_after = info
                c, qrow = h // 2, (h % 2) * 64
                p.wait("tensor", sDVE, d_after)
                p.wait("tensor", sACT, a_after)
                p.op("tensor", lambda e, zp=zp, h=h: e.matmul(
                    psZ[zp][0:64, :], wv_uT[:, 64 * h:64 * (h + 1)], zsbs[zp][:, :],
                    start=True, stop=True), incs=(sPE, 1))
                pe_after = sPE.n
                p.wait("vector", sPE, pe_after)
                p.op("vector", lambda e, zp=zp, c=c, qrow=qrow, t5=t5: e.tensor_copy(
                    yallT[qrow:qrow + 64, 2048 * c + 512 * t5: 2048 * c + 512 * (t5 + 1)],
                    psZ[zp][0:64, :]), incs=(sDVE, 1))
                yraw_done[zp] = sDVE.n

            for h in range(16):
                for t5 in range(4):
                    it = 4 * h + t5
                    zp = it % 2
                    emit_scores(h, t5, 0)
                    for gi in range(NG):
                        if gi + 1 < NG:
                            emit_scores(h, t5, gi + 1)
                        if gi == 0 and prev_iter[0] is not None:
                            emit_stage2(prev_iter[0])
                            prev_iter[0] = None
                        emit_exp(h, t5, gi)
                        emit_stage1(h, t5, gi, zp)
                    prev_iter[0] = (zp, h, t5) + emit_tail(h, t5, zp)
            emit_stage2(prev_iter[0])

            p.wait("vector", sST)
            p.op("vector", lambda e: e.reciprocal(sums16[:, :], sums16[:, :]), incs=(sDVE, 1))
            p.wait("vector", sDVE)
            p.op("vector", lambda e: e.tensor_copy(rec_hi[:, :], sums16[:, :]), incs=(sDVE, 1))
            p.wait("vector", sDVE)
            p.op("vector", lambda e: e.tensor_tensor(sums16[:, :], sums16[:, :], rec_hi[:, :], ALU.subtract), incs=(sDVE, 1))
            p.wait("vector", sDVE)
            p.op("vector", lambda e: e.tensor_copy(rec_lo[:, :], sums16[:, :]), incs=(sDVE, 1))
            p.flush()

        with nc.sbuf_tensor("xs", [128, 16 * 1024], F32) as xs, \
             nc.sbuf_tensor("scr2", [128, 512], F32) as scr2, \
             nc.sbuf_tensor("sA2a", [128, 16], F32) as sA2a, \
             nc.sbuf_tensor("sA2b", [128, 16], F32) as sA2b, \
             nc.sbuf_tensor("sQ2a", [128, 16], F32) as sQ2a, \
             nc.sbuf_tensor("sQ2b", [128, 16], F32) as sQ2b, \
             nc.sbuf_tensor("lnx", [128, 8 * 16], F32) as lnx, \
             nc.psum_tensor("pS0", [128, 512], F32) as pS0, \
             nc.psum_tensor("pS1", [128, 512], F32) as pS1, \
             nc.psum_tensor("pP0", [128, 512], F32) as pP0, \
             nc.psum_tensor("pP1", [128, 512], F32) as pP1:
            lnsum = lnx.ap()[:, 0:16]
            lnsq = lnx.ap()[:, 16:32]
            lnmean = lnx.ap()[:, 32:48]
            lnmsq = lnx.ap()[:, 48:64]
            lnvar = lnx.ap()[:, 64:80]
            lnrstd = lnx.ap()[:, 80:96]
            lnnm = lnx.ap()[:, 96:112]

            for i in range(16):
                p.dma("sync", xs[:, 1024 * i:1024 * (i + 1)], x_own[128 * i:128 * (i + 1), :], sLD)
            xs_loaded = sLD.n

            pSs = [pS0, pS1]
            pPs = [pP0, pP1]
            p.wait("tensor", sDVE)
            scale_done = {}
            k_i = 0
            for c in range(8):
                for t5 in range(4):
                    pS = pSs[k_i % 2]
                    if k_i >= 2:
                        p.wait("tensor", sDVE, scale_done[k_i - 2])
                    p.op("tensor", lambda e, c=c, t5=t5, pS=pS: e.matmul(
                        pS[:, :], e16_bf[:, 128 * c:128 * (c + 1)], rec_hi[:, 512 * t5:512 * (t5 + 1)],
                        start=True, stop=False))
                    p.op("tensor", lambda e, c=c, t5=t5, pS=pS: e.matmul(
                        pS[:, :], e16_bf[:, 128 * c:128 * (c + 1)], rec_lo[:, 512 * t5:512 * (t5 + 1)],
                        start=False, stop=True), incs=(sPE, 1))
                    p.wait("vector", sPE)
                    p.op("vector", lambda e, c=c, t5=t5, pS=pS: e.tensor_tensor(
                        yallT[:, 2048 * c + 512 * t5: 2048 * c + 512 * (t5 + 1)],
                        yallT[:, 2048 * c + 512 * t5: 2048 * c + 512 * (t5 + 1)],
                        pS[:, :], ALU.mult), incs=(sDVE, 1))
                    scale_done[k_i] = sDVE.n
                    k_i += 1

            pr_i = 0
            prj_done = {}
            for ti in range(16):
                for cc in range(2):
                    pP = pPs[pr_i % 2]
                    if pr_i >= 2:
                        p.wait("tensor", sDVE, prj_done[pr_i - 2])
                    if pr_i == 0:
                        p.wait("tensor", sDVE)
                    for k in range(8):
                        p.op("tensor", lambda e, ti=ti, cc=cc, k=k, pP=pP: e.matmul(
                            pP[:, :], yallT[:, 2048 * k + 128 * ti: 2048 * k + 128 * (ti + 1)],
                            w_projT[:, 1024 * k + 512 * cc: 1024 * k + 512 * (cc + 1)],
                            start=(k == 0), stop=False))
                    p.op("tensor", lambda e, cc=cc, pP=pP: e.matmul(
                        pP[:, :], onesr_bf[:, :], bprj_hi[:, 512 * cc:512 * (cc + 1)],
                        start=False, stop=False))
                    p.op("tensor", lambda e, cc=cc, pP=pP: e.matmul(
                        pP[:, :], onesr_bf[:, :], bprj_lo[:, 512 * cc:512 * (cc + 1)],
                        start=False, stop=True), incs=(sPE, 1))
                    p.wait("vector", sPE)
                    if pr_i == 0:
                        p.wait("vector", sLD, xs_loaded)
                    sAd = sA2a if cc == 0 else sA2b
                    sQd = sQ2a if cc == 0 else sQ2b
                    p.op("vector", lambda e, ti=ti, cc=cc, pP=pP: e.tensor_tensor(
                        xs[:, 1024 * ti + 512 * cc: 1024 * ti + 512 * (cc + 1)],
                        pP[:, :], xs[:, 1024 * ti + 512 * cc: 1024 * ti + 512 * (cc + 1)],
                        ALU.add), incs=(sDVE, 1))
                    prj_done[pr_i] = sDVE.n
                    pr_i += 1
                    p.wait("scalar", sDVE, prj_done[pr_i - 1])
                    p.op("scalar", lambda e, ti=ti, cc=cc, sAd=sAd: e.activation(
                        scr2[:, :], xs[:, 1024 * ti + 512 * cc: 1024 * ti + 512 * (cc + 1)],
                        AF.Copy, bias=0.0, scale=1.0,
                        accum_out=sAd[:, ti:ti + 1]), incs=(sACT, 1))
                    p.op("scalar", lambda e, ti=ti, cc=cc, sQd=sQd: e.activation(
                        scr2[:, :], xs[:, 1024 * ti + 512 * cc: 1024 * ti + 512 * (cc + 1)],
                        AF.Square, bias=0.0, scale=1.0,
                        accum_out=sQd[:, ti:ti + 1]), incs=(sACT, 1))

            p.wait("vector", sDVE)
            p.wait("vector", sACT)
            p.op("vector", lambda e: e.tensor_tensor(lnsum, sA2a[:, :], sA2b[:, :], ALU.add), incs=(sDVE, 1))
            p.op("vector", lambda e: e.tensor_tensor(lnsq, sQ2a[:, :], sQ2b[:, :], ALU.add), incs=(sDVE, 1))
            p.wait("vector", sDVE)
            p.op("vector", lambda e: e.tensor_scalar(lnmean, lnsum, 1.0 / D, None, ALU.mult), incs=(sDVE, 1))
            p.wait("vector", sDVE)
            p.op("vector", lambda e: e.tensor_tensor(lnmsq, lnmean, lnmean, ALU.mult), incs=(sDVE, 1))
            p.op("vector", lambda e: e.tensor_scalar(lnvar, lnsq, 1.0 / D, None, ALU.mult), incs=(sDVE, 1))
            p.wait("vector", sDVE)
            p.op("vector", lambda e: e.tensor_tensor(lnvar, lnvar, lnmsq, ALU.subtract), incs=(sDVE, 1))
            p.wait("scalar", sDVE)
            p.op("scalar", lambda e: e.activation(lnvar, lnvar, AF.Ln, bias=epsc[:, 1:2], scale=1.0), incs=(sACT, 1))
            p.wait("scalar", sACT)
            p.op("scalar", lambda e: e.activation(lnrstd, lnvar, AF.Exp, bias=0.0, scale=-0.5), incs=(sACT, 1))
            p.wait("vector", sACT)
            p.op("vector", lambda e: e.tensor_scalar(lnnm, lnmean, -1.0, None, ALU.mult), incs=(sDVE, 1))
            p.wait("vector", sDVE)
            for ti in range(16):
                p.op("vector", lambda e, ti=ti: e.tensor_scalar(
                    xs[:, 1024 * ti:1024 * (ti + 1)], xs[:, 1024 * ti:1024 * (ti + 1)],
                    lnnm[:, ti:ti + 1], lnrstd[:, ti:ti + 1], ALU.add, ALU.mult), incs=(sDVE, 1))
                dv = sDVE.n
                p.wait("sync", sDVE, dv)
                p.dma("sync", x1_out[128 * ti:128 * (ti + 1), :], xs[:, 1024 * ti:1024 * (ti + 1)], sST)
            p.wait("sync", sST)
            p.flush()

    return nc



def build_launch2():
    nc = bass.Bass("TRN2", target_bir_lowering=False, detect_race_conditions=False)
    dt = nc.dram_tensor
    x1_in = dt("x1", [TC, D], F32, kind="ExternalInput").ap()
    wg_st = dt("wg_st", [128, 64], F32, kind="ExternalInput").ap()
    bg_bc = dt("bg_bc", [128, 8], F32, kind="ExternalInput").ap()
    w1 = dt("w1", [8, D, D], F32, kind="ExternalInput").ap()
    w2 = dt("w2", [8, D, D], F32, kind="ExternalInput").ap()
    b1cols = dt("b1cols", [128, 64], F32, kind="ExternalInput").ap()
    b2m = dt("b2m", [8, D], F32, kind="ExternalInput").ap()
    ones_row = dt("ones_row", [1, 128], F32, kind="ExternalInput").ap()
    ident_in = dt("ident_in", [128, 128], F32, kind="ExternalInput").ap()
    x2_out = dt("x2", [TC, D], F32, kind="ExternalOutput").ap()

    with contextlib.ExitStack() as ctx:
        p = Prog(nc, ctx)
        sLD = p.dsem("sLD")
        sPE = p.sem("sPE")
        sACT = p.sem("sACT")
        sDVE = p.sem("sDVE")
        sST = p.dsem("sST")

        def sb(name, shape, dtyp):
            return ctx.enter_context(nc.sbuf_tensor(name, shape, dtyp))

        g = "gpsimd"
        onesr_bf = sb("onesr_bf", [1, 128], BF16)
        ident_bf = sb("ident_bf", [128, 128], BF16)
        epsc = sb("epsc", [128, 2], F32)
        bg_sb = sb("bg_sb", [128, 8], F32)
        b1_sb = sb("b1_sb", [128, 64], F32)
        b2f = sb("b2f", [8, D], F32)
        b2hi = sb("b2hi", [8, D], BF16)
        b2lo = sb("b2lo", [8, D], BF16)
        wgf = sb("wgf", [128, 64], F32)
        wghi = sb("wghi", [128, 64], BF16)
        wglo = sb("wglo", [128, 64], BF16)
        x1T = sb("x1T", [128, 8 * TC], BF16)
        maskF = sb("maskF", [128, 16 * 8], F32)
        m8 = sb("m8", [8, TC], BF16)
        acc = sb("acc", [128, 16 * 1024], F32)

        p.dma(g, onesr_bf[:, :], ones_row, sLD)
        p.dma(g, ident_bf[:, :], ident_in, sLD)
        p.dma("sync", bg_sb[:, :], bg_bc, sLD)
        p.dma("sync", b1_sb[:, :], b1cols, sLD)
        p.dma("sync", b2f[:, :], b2m, sLD)
        p.dma("sync", wgf[:, :], wg_st, sLD)
        n_pre = sLD.n

        p.wait("scalar", sLD, n_pre)
        p.op("vector", lambda e: e.memset(epsc[:, 0:1], 1e-20), incs=(sDVE, 1))
        p.op("vector", lambda e: e.memset(epsc[:, 1:2], 1e-5), incs=(sDVE, 1))
        p.wait("vector", sLD, n_pre)
        p.op("vector", lambda e: e.tensor_copy(b2hi[:, :], b2f[:, :]), incs=(sDVE, 1))
        p.op("vector", lambda e: e.tensor_copy(wghi[:, :], wgf[:, :]), incs=(sDVE, 1))
        p.wait("vector", sDVE)
        p.op("vector", lambda e: e.tensor_tensor(b2f[:, :], b2f[:, :], b2hi[:, :], ALU.subtract), incs=(sDVE, 1))
        p.op("vector", lambda e: e.tensor_tensor(wgf[:, :], wgf[:, :], wghi[:, :], ALU.subtract), incs=(sDVE, 1))
        p.wait("vector", sDVE)
        p.op("vector", lambda e: e.tensor_copy(b2lo[:, :], b2f[:, :]), incs=(sDVE, 1))
        p.op("vector", lambda e: e.tensor_copy(wglo[:, :], wgf[:, :]), incs=(sDVE, 1))

        with nc.sbuf_tensor("xloT", [128, 8 * TC], BF16) as xloT, \
             nc.sbuf_tensor("stgA", [128, 1024], BF16) as stgA, \
             nc.sbuf_tensor("stgB", [128, 1024], BF16) as stgB, \
             nc.sbuf_tensor("stgF", [128, 1024], F32) as stgF, \
             nc.sbuf_tensor("rt0", [128, 8], F32) as rt0, \
             nc.sbuf_tensor("rt1", [128, 8], F32) as rt1, \
             nc.sbuf_tensor("rt2", [128, 8], F32) as rt2, \
             nc.sbuf_tensor("rmx", [128, 1], F32) as rmx, \
             nc.sbuf_tensor("mbf", [128, 8], BF16) as mbf, \
             nc.psum_tensor("pT0", [128, 1024], BF16) as pT0, \
             nc.psum_tensor("pT1", [128, 1024], BF16) as pT1, \
             nc.psum_tensor("pL", [128, 8], F32) as pL, \
             nc.psum_tensor("pM8", [8, 128], BF16) as pM8:
            pTs = [pT0, pT1]
            x1T_r = x1T.ap().rearrange("p (c n) -> p c n", c=8)
            xloT_r = xloT.ap().rearrange("p (c n) -> p c n", c=8)
            for i in range(16):
                stg = stgA if i % 2 == 0 else stgB
                pt = pTs[i % 2]
                p.wait(g, sPE)
                p.wait(g, sDVE)
                p.dma(g, stg[:, :], x1_in[128 * i:128 * (i + 1), :], sLD)
                p.wait("sync", sDVE)
                p.dma("sync", stgF[:, :], x1_in[128 * i:128 * (i + 1), :], sLD)
                p.wait("tensor", sLD)
                p.wait("tensor", sDVE)
                for c in range(8):
                    p.op("tensor", lambda e, stg=stg, pt=pt, c=c: e.transpose(
                        pt[:, 128 * c:128 * (c + 1)], stg[:, 128 * c:128 * (c + 1)],
                        ident_bf[:, :]), incs=(sPE, 1) if c == 7 else ())
                p.wait("vector", sPE)
                p.op("vector", lambda e, i=i, pt=pt: e.tensor_copy(
                    x1T_r[:, :, 128 * i:128 * (i + 1)],
                    pt.ap().rearrange("p (c n) -> p c n", c=8)), incs=(sDVE, 1))
                p.wait("vector", sLD)
                p.op("vector", lambda e, stg=stg: e.tensor_tensor(
                    stgF[:, :], stgF[:, :], stg[:, :], ALU.subtract), incs=(sDVE, 1))
                p.wait("vector", sDVE)
                p.wait("vector", sPE)
                p.op("vector", lambda e, stg=stg: e.tensor_copy(stg[:, :], stgF[:, :]), incs=(sDVE, 1))
                dv_lo = sDVE.n
                p.wait("tensor", sDVE, dv_lo)
                for c in range(8):
                    p.op("tensor", lambda e, stg=stg, pt=pt, c=c: e.transpose(
                        pt[:, 128 * c:128 * (c + 1)], stg[:, 128 * c:128 * (c + 1)],
                        ident_bf[:, :]), incs=(sPE, 1) if c == 7 else ())
                p.wait("vector", sPE)
                p.op("vector", lambda e, i=i, pt=pt: e.tensor_copy(
                    xloT_r[:, :, 128 * i:128 * (i + 1)],
                    pt.ap().rearrange("p (c n) -> p c n", c=8)), incs=(sDVE, 1))

            wg_r = wghi.ap().rearrange("p (c n) -> p c n", c=8)
            wgl_r = wglo.ap().rearrange("p (c n) -> p c n", c=8)
            p.wait("tensor", sDVE)
            for ti in range(16):
                nmm = [0]
                p.wait("tensor", sDVE)
                for k in range(8):
                    for lh, rh in ((x1T_r[:, k, 128 * ti:128 * (ti + 1)], wg_r[:, k, :]),
                                   (x1T_r[:, k, 128 * ti:128 * (ti + 1)], wgl_r[:, k, :]),
                                   (xloT_r[:, k, 128 * ti:128 * (ti + 1)], wg_r[:, k, :])):
                        first, last = nmm[0] == 0, nmm[0] == 23
                        p.op("tensor", lambda e, lh=lh, rh=rh, first=first, last=last: e.matmul(
                            pL[:, :], lh, rh, start=first, stop=last),
                            incs=(sPE, 1) if last else ())
                        nmm[0] += 1
                p.wait("vector", sPE)
                p.op("vector", lambda e: e.tensor_tensor(rt0[:, :], pL[:, :], bg_sb[:, :], ALU.add), incs=(sDVE, 1))
                p.wait("vector", sDVE)
                p.op("vector", lambda e: e.tensor_reduce(rmx[:, :], rt0[:, :], AX.X, ALU.max), incs=(sDVE, 1))
                p.wait("vector", sDVE)
                p.op("vector", lambda e: e.tensor_scalar(rt1[:, :], rt0[:, :], rmx[:, :], None, ALU.is_ge), incs=(sDVE, 1))
                p.wait("vector", sDVE)
                p.op("vector", lambda e: e.tensor_copy(rt2[:, 0:1], rt1[:, 0:1]), incs=(sDVE, 1))
                p.op("vector", lambda e: e.tensor_tensor(rt2[:, 1:8], rt1[:, 1:8], rt1[:, 0:7], ALU.add), incs=(sDVE, 1))
                p.wait("vector", sDVE)
                p.op("vector", lambda e: e.tensor_copy(rt0[:, 0:2], rt2[:, 0:2]), incs=(sDVE, 1))
                p.op("vector", lambda e: e.tensor_tensor(rt0[:, 2:8], rt2[:, 2:8], rt2[:, 0:6], ALU.add), incs=(sDVE, 1))
                p.wait("vector", sDVE)
                p.op("vector", lambda e: e.tensor_copy(rt2[:, 0:4], rt0[:, 0:4]), incs=(sDVE, 1))
                p.op("vector", lambda e: e.tensor_tensor(rt2[:, 4:8], rt0[:, 4:8], rt0[:, 0:4], ALU.add), incs=(sDVE, 1))
                p.wait("vector", sDVE)
                p.op("vector", lambda e: e.tensor_scalar(rt0[:, :], rt2[:, :], 1.5, None, ALU.is_le), incs=(sDVE, 1))
                p.wait("vector", sDVE)
                p.op("vector", lambda e, ti=ti: e.tensor_tensor(
                    maskF[:, 8 * ti:8 * (ti + 1)], rt1[:, :], rt0[:, :], ALU.mult), incs=(sDVE, 1))
                p.wait("vector", sDVE)
                p.op("vector", lambda e, ti=ti: e.tensor_copy(mbf[:, :], maskF[:, 8 * ti:8 * (ti + 1)]), incs=(sDVE, 1))
                dvm = sDVE.n
                p.wait("tensor", sDVE, dvm)
                p.wait("tensor", sDVE, dvm)
                p.op("tensor", lambda e: e.transpose(pM8[:, :], mbf[:, :], ident_bf[:, :]), incs=(sPE, 1))
                p.wait("vector", sPE)
                p.op("vector", lambda e, ti=ti: e.tensor_copy(m8[:, 128 * ti:128 * (ti + 1)], pM8[:, :]), incs=(sDVE, 1))
            p.flush()

        with nc.sbuf_tensor("w1sb", [128, 8 * D], BF16) as w1sb, \
             nc.sbuf_tensor("w2sb", [128, 8 * D], BF16) as w2sb, \
             nc.sbuf_tensor("h_e", [128, 8 * TC], BF16) as h_e, \
             nc.psum_tensor("pMa", [128, 1024], F32) as pMa, \
             nc.psum_tensor("pMb", [128, 1024], F32) as pMb, \
             nc.psum_tensor("pY0", [128, 1024], F32) as pY0, \
             nc.psum_tensor("pY1", [128, 1024], F32) as pY1:
            pYs = [pY0, pY1]
            pMs = [pMa, pMb]

            p.wait("tensor", sDVE)
            py_free = [0, 0]
            for ti in range(16):
                pY = pYs[ti % 2]
                p.wait("tensor", sDVE, py_free[ti % 2])
                for cc in range(2):
                    p.op("tensor", lambda e, ti=ti, cc=cc, pY=pY: e.matmul(
                        pY[:, 512 * cc:512 * (cc + 1)], m8[:, 128 * ti:128 * (ti + 1)],
                        b2hi[:, 512 * cc:512 * (cc + 1)], start=True, stop=False))
                    p.op("tensor", lambda e, ti=ti, cc=cc, pY=pY: e.matmul(
                        pY[:, 512 * cc:512 * (cc + 1)], m8[:, 128 * ti:128 * (ti + 1)],
                        b2lo[:, 512 * cc:512 * (cc + 1)], start=False, stop=True),
                        incs=(sPE, 1) if cc == 1 else ())
                p.wait("vector", sPE)
                p.op("vector", lambda e, ti=ti, pY=pY: e.tensor_copy(
                    acc[:, 1024 * ti:1024 * (ti + 1)], pY[:, :]), incs=(sDVE, 1))
                py_free[ti % 2] = sDVE.n

            gelu_done = {}
            gl_i = [0]
            pe_after_e = [0]
            for e_i in range(8):
                p.wait(g, sPE, pe_after_e[0])
                for k in range(8):
                    p.dma(g, w1sb[:, 1024 * k:1024 * (k + 1)], w1[e_i, 128 * k:128 * (k + 1), :], sLD)
                for k in range(8):
                    p.dma(g, w2sb[:, 1024 * k:1024 * (k + 1)], w2[e_i, 128 * k:128 * (k + 1), :], sLD)
                wld = sLD.n

                for ic in range(8):
                    for th in range(2):
                        u = gl_i[0]
                        pM = pMs[u % 2]
                        p.wait("tensor", sACT, gelu_done.get(u - 2, 0))
                        if ic == 0 and th == 0:
                            p.wait("tensor", sLD, wld)
                            p.wait("tensor", sPE, pe_after_e[0])
                        for t5 in range(2):
                            for k in range(8):
                                p.op("tensor", lambda e, ic=ic, th=th, t5=t5, k=k, pM=pM: e.matmul(
                                    pM[:, 512 * t5:512 * (t5 + 1)],
                                    w1sb[:, 1024 * k + 128 * ic:1024 * k + 128 * (ic + 1)],
                                    x1T[:, 2048 * k + 1024 * th + 512 * t5:2048 * k + 1024 * th + 512 * (t5 + 1)],
                                    start=(k == 0), stop=(k == 7)),
                                    incs=(sPE, 1) if (t5 == 1 and k == 7) else ())
                        p.wait("scalar", sPE)
                        p.op("scalar", lambda e, e_i=e_i, ic=ic, th=th, pM=pM: e.activation(
                            h_e[:, 2048 * ic + 1024 * th:2048 * ic + 1024 * (th + 1)],
                            pM[:, :], AF.Gelu, bias=b1_sb[:, 8 * e_i + ic:8 * e_i + ic + 1],
                            scale=1.0), incs=(sACT, 1))
                        gelu_done[u] = sACT.n
                        gl_i[0] += 1

                a_need = sACT.n
                for ti in range(16):
                    pY = pYs[ti % 2]
                    p.wait("tensor", sDVE, py_free[ti % 2])
                    if ti == 0:
                        p.wait("tensor", sACT, a_need)
                    for ic in range(8):
                        for cc in range(2):
                            p.op("tensor", lambda e, ti=ti, ic=ic, cc=cc, pY=pY: e.matmul(
                                pY[:, 512 * cc:512 * (cc + 1)],
                                h_e[:, 2048 * ic + 128 * ti:2048 * ic + 128 * (ti + 1)],
                                w2sb[:, 1024 * ic + 512 * cc:1024 * ic + 512 * (cc + 1)],
                                start=(ic == 0), stop=(ic == 7)),
                                incs=(sPE, 1) if (ic == 7 and cc == 1) else ())
                    p.wait("vector", sPE)
                    p.op("vector", lambda e, ti=ti, e_i=e_i, pY=pY: e.scalar_tensor_tensor(
                        acc[:, 1024 * ti:1024 * (ti + 1)], pY[:, :],
                        maskF[:, 8 * ti + e_i:8 * ti + e_i + 1],
                        acc[:, 1024 * ti:1024 * (ti + 1)],
                        ALU.mult, ALU.add), incs=(sDVE, 1))
                    py_free[ti % 2] = sDVE.n
                pe_after_e[0] = sPE.n
            p.flush()

        with nc.sbuf_tensor("xs", [128, 16 * 1024], F32) as xs, \
             nc.sbuf_tensor("scr2", [128, 1024], F32) as scr2, \
             nc.sbuf_tensor("sA2a", [128, 16], F32) as sA2a, \
             nc.sbuf_tensor("sQ2a", [128, 16], F32) as sQ2a, \
             nc.sbuf_tensor("lnx", [128, 8 * 16], F32) as lnx:
            lnsum = lnx.ap()[:, 0:16]
            lnsq = lnx.ap()[:, 16:32]
            lnmean = lnx.ap()[:, 32:48]
            lnmsq = lnx.ap()[:, 48:64]
            lnvar = lnx.ap()[:, 64:80]
            lnrstd = lnx.ap()[:, 80:96]
            lnnm = lnx.ap()[:, 96:112]

            for i in range(16):
                p.dma("sync", xs[:, 1024 * i:1024 * (i + 1)], x1_in[128 * i:128 * (i + 1), :], sLD)
            xs_loaded = sLD.n
            p.wait("vector", sLD, xs_loaded)
            p.wait("vector", sDVE)
            for ti in range(16):
                p.op("vector", lambda e, ti=ti: e.tensor_tensor(
                    xs[:, 1024 * ti:1024 * (ti + 1)], acc[:, 1024 * ti:1024 * (ti + 1)],
                    xs[:, 1024 * ti:1024 * (ti + 1)], ALU.add), incs=(sDVE, 1))
                dvr = sDVE.n
                p.wait("scalar", sDVE, dvr)
                p.op("scalar", lambda e, ti=ti: e.activation(
                    scr2[:, :], xs[:, 1024 * ti:1024 * (ti + 1)], AF.Copy, bias=0.0,
                    scale=1.0, accum_out=sA2a[:, ti:ti + 1]), incs=(sACT, 1))
                p.op("scalar", lambda e, ti=ti: e.activation(
                    scr2[:, :], xs[:, 1024 * ti:1024 * (ti + 1)], AF.Square, bias=0.0,
                    scale=1.0, accum_out=sQ2a[:, ti:ti + 1]), incs=(sACT, 1))
            p.wait("vector", sACT)
            p.op("vector", lambda e: e.tensor_scalar(lnmean, sA2a[:, :], 1.0 / D, None, ALU.mult), incs=(sDVE, 1))
            p.wait("vector", sDVE)
            p.op("vector", lambda e: e.tensor_tensor(lnmsq, lnmean, lnmean, ALU.mult), incs=(sDVE, 1))
            p.op("vector", lambda e: e.tensor_scalar(lnvar, sQ2a[:, :], 1.0 / D, None, ALU.mult), incs=(sDVE, 1))
            p.wait("vector", sDVE)
            p.op("vector", lambda e: e.tensor_tensor(lnvar, lnvar, lnmsq, ALU.subtract), incs=(sDVE, 1))
            p.wait("scalar", sDVE)
            p.op("scalar", lambda e: e.activation(lnvar, lnvar, AF.Ln, bias=epsc[:, 1:2], scale=1.0), incs=(sACT, 1))
            p.wait("scalar", sACT)
            p.op("scalar", lambda e: e.activation(lnrstd, lnvar, AF.Exp, bias=0.0, scale=-0.5), incs=(sACT, 1))
            p.wait("vector", sACT)
            p.op("vector", lambda e: e.tensor_scalar(lnnm, lnmean, -1.0, None, ALU.mult), incs=(sDVE, 1))
            p.wait("vector", sDVE)
            for ti in range(16):
                p.op("vector", lambda e, ti=ti: e.tensor_scalar(
                    xs[:, 1024 * ti:1024 * (ti + 1)], xs[:, 1024 * ti:1024 * (ti + 1)],
                    lnnm[:, ti:ti + 1], lnrstd[:, ti:ti + 1], ALU.add, ALU.mult), incs=(sDVE, 1))
                dv = sDVE.n
                p.wait("sync", sDVE, dv)
                p.dma("sync", x2_out[128 * ti:128 * (ti + 1), :], xs[:, 1024 * ti:1024 * (ti + 1)], sST)
            p.wait("sync", sST)
            p.flush()

    return nc



def build_launch3():
    nc = bass.Bass("TRN2", target_bir_lowering=False, detect_race_conditions=False)
    dt = nc.dram_tensor
    x2s = dt("x2s", [T, 512], F32, kind="ExternalInput").ap()
    w_in_s = dt("w_in_s", [4, 128, 128], F32, kind="ExternalInput").ap()
    w_a_s = dt("w_a_s", [4, 128, 128], F32, kind="ExternalInput").ap()
    b_in_c = dt("b_in_c", [128, 4], F32, kind="ExternalInput").ap()
    b_a_c = dt("b_a_c", [128, 4], F32, kind="ExternalInput").ap()
    ap_c = dt("ap_c", [128, 4], F32, kind="ExternalInput").ap()
    ident_in = dt("ident_in", [128, 128], F32, kind="ExternalInput").ap()
    h_out = dt("h", [T, 512], F32, kind="ExternalOutput").ap()

    with contextlib.ExitStack() as ctx:
        p = Prog(nc, ctx)
        sLD = p.dsem("sLD")
        sPE = p.sem("sPE")
        sACT = p.sem("sACT")
        sDVE = p.sem("sDVE")
        sST = p.dsem("sST")

        def sb(name, shape, dtyp):
            return ctx.enter_context(nc.sbuf_tensor(name, shape, dtyp))

        g = "gpsimd"
        ident_f = sb("ident_f", [128, 128], F32)
        b_in_sb = sb("b_in_sb", [128, 4], F32)
        b_a_sb = sb("b_a_sb", [128, 4], F32)
        ap_sb = sb("ap_sb", [128, 4], F32)
        onec = sb("onec", [128, 1], F32)
        wi_sb = sb("wi_sb", [128, 4 * 128], BF16)
        wa_sb = sb("wa_sb", [128, 4 * 128], BF16)
        zeros = sb("zeros", [128, T], F32)
        xF = sb("xF", [128, T], F32)
        xFb = sb("xFb", [128, T], BF16)
        gx = sb("gx", [128, T], F32)
        ga = sb("ga", [128, T], F32)
        av = sb("av", [128, T], F32)
        mlt = sb("mlt", [128, T], F32)
        nx = sb("nx", [128, T], F32)
        cpb = sb("cpb", [128, T + 8], F32)
        pcl = sb("pcl", [128, T], F32)
        rp = sb("rp", [128, T], F32)
        Sb = sb("Sb", [128, T], F32)
        dcol = sb("dcol", [128, 4], F32)
        stgF = sb("stgF", [128, 512], F32)
        stg2 = sb("stg2", [128, 512], F32)

        p.dma("sync", ident_f[:, :], ident_in, sLD)
        p.dma("sync", b_in_sb[:, :], b_in_c, sLD)
        p.dma("sync", b_a_sb[:, :], b_a_c, sLD)
        p.dma("sync", ap_sb[:, :], ap_c, sLD)
        for hb in range(4):
            p.dma(g, wi_sb[:, 128 * hb:128 * (hb + 1)], w_in_s[hb], sLD)
            p.dma(g, wa_sb[:, 128 * hb:128 * (hb + 1)], w_a_s[hb], sLD)
        n_pre = sLD.n
        p.op("vector", lambda e: e.memset(zeros[:, :], 0.0), incs=(sDVE, 1))
        p.op("vector", lambda e: e.memset(onec[:, :], 1.0), incs=(sDVE, 1))
        p.op("vector", lambda e: e.memset(cpb[:, 0:1], 1.0), incs=(sDVE, 1))

        p.wait("vector", sLD, n_pre)
        p.wait("scalar", sLD, n_pre)
        p.op("vector", lambda e: e.tensor_scalar(dcol[:, :], ap_sb[:, :], -1.0, None, ALU.mult), incs=(sDVE, 1))
        p.wait("vector", sDVE)
        p.op("vector", lambda e: e.tensor_tensor(dcol[:, :], dcol[:, :], ap_sb[:, :], ALU.max), incs=(sDVE, 1))
        dv = sDVE.n
        p.wait("scalar", sDVE, dv)
        p.op("scalar", lambda e: e.activation(dcol[:, :], dcol[:, :], AF.Exp, bias=0.0, scale=-1.0), incs=(sACT, 1))
        p.wait("scalar", sACT)
        p.op("scalar", lambda e: e.activation(dcol[:, :], dcol[:, :], AF.Ln, bias=onec[:, 0:1], scale=1.0), incs=(sACT, 1))
        p.wait("vector", sACT)
        p.op("vector", lambda e: e.tensor_scalar(stgF[:, 0:4], ap_sb[:, :], 0.0, None, ALU.max), incs=(sDVE, 1))
        p.wait("vector", sDVE)
        p.op("vector", lambda e: e.tensor_tensor(dcol[:, :], dcol[:, :], stgF[:, 0:4], ALU.add), incs=(sDVE, 1))
        p.wait("vector", sDVE)
        p.op("vector", lambda e: e.tensor_scalar(dcol[:, :], dcol[:, :], 1e-3, None, ALU.add), incs=(sDVE, 1))

        with nc.psum_tensor("pT0", [128, 2048], F32) as pT0, \
             nc.psum_tensor("pG1", [128, 2048], F32) as pG1:
            pGs = [pT0, pG1]
            for cc in range(4):
                p.wait("tensor", sACT)
                for tt in range(32):
                    par = tt % 2
                    stg = stgF if par == 0 else stg2
                    p.wait(g, sPE)
                    p.wait(g, sDVE)
                    p.dma(g, stg[:, 0:128], x2s[128 * tt:128 * (tt + 1), 128 * cc:128 * (cc + 1)], sLD)
                    p.wait("tensor", sLD)
                    if tt % 4 == 0 and tt > 0:
                        p.wait("tensor", sDVE)
                    p.op("tensor", lambda e, stg=stg, tt=tt: e.transpose(
                        pT0[:, 512 * (tt % 4):512 * (tt % 4) + 128],
                        stg[:, 0:128], ident_f[:, :]), incs=(sPE, 1))
                    if tt % 4 == 3:
                        p.wait("vector", sPE)
                        p.op("vector", lambda e, tt=tt: e.tensor_copy(
                            xF[:, 128 * (tt - 3):128 * (tt + 1)],
                            pT0.ap().rearrange("p (c n) -> p c n", c=4)[:, :, 0:128]),
                            incs=(sDVE, 1))
                p.wait("vector", sDVE)
                p.op("vector", lambda e: e.tensor_copy(xFb[:, :], xF[:, :]), incs=(sDVE, 1))
                dvb = sDVE.n

                for gi, (wsb, bsb, dst) in enumerate(((wi_sb, b_in_sb, gx), (wa_sb, b_a_sb, ga))):
                    for grp in range(2):
                        pG = pGs[grp]
                        p.wait("tensor", sDVE)
                        p.wait("tensor", sACT)
                        for t5 in range(4):
                            p.op("tensor", lambda e, cc=cc, wsb=wsb, t5=t5, grp=grp, pG=pG: e.matmul(
                                pG[:, 512 * t5:512 * (t5 + 1)],
                                wsb[:, 128 * cc:128 * (cc + 1)],
                                xFb[:, 2048 * grp + 512 * t5:2048 * grp + 512 * (t5 + 1)],
                                start=True, stop=True),
                                incs=(sPE, 1) if t5 == 3 else ())
                        p.wait("scalar", sPE)
                        p.op("scalar", lambda e, cc=cc, bsb=bsb, dst=dst, grp=grp, pG=pG: e.activation(
                            dst[:, 2048 * grp:2048 * (grp + 1)], pG[:, :], AF.Sigmoid,
                            bias=bsb[:, cc:cc + 1], scale=1.0), incs=(sACT, 1))

                a_done = sACT.n
                p.wait("vector", sACT, a_done)
                p.op("vector", lambda e, cc=cc: e.tensor_scalar(
                    ga[:, :], ga[:, :], dcol[:, cc:cc + 1], -1.0, ALU.mult, ALU.mult), incs=(sDVE, 1))
                dv = sDVE.n
                p.wait("scalar", sDVE, dv)
                p.op("scalar", lambda e: e.activation(av[:, :], ga[:, :], AF.Exp, bias=0.0, scale=1.0), incs=(sACT, 1))
                p.wait("vector", sACT)
                p.op("vector", lambda e: e.tensor_scalar(av[:, :], av[:, :], 0.9999899864196777, None, ALU.min), incs=(sDVE, 1))
                p.wait("vector", sDVE)
                p.op("vector", lambda e: e.tensor_tensor(mlt[:, :], av[:, :], av[:, :], ALU.mult), incs=(sDVE, 1))
                p.wait("vector", sDVE)
                p.op("vector", lambda e: e.tensor_scalar(mlt[:, :], mlt[:, :], -1.0, 1.0, ALU.mult, ALU.add), incs=(sDVE, 1))
                p.wait("vector", sDVE)
                p.op("vector", lambda e: e.tensor_scalar(mlt[:, :], mlt[:, :], 1e-5, None, ALU.max), incs=(sDVE, 1))
                dv = sDVE.n
                p.wait("scalar", sDVE, dv)
                p.op("scalar", lambda e: e.activation(mlt[:, :], mlt[:, :], AF.Ln, bias=0.0, scale=1.0), incs=(sACT, 1))
                p.wait("scalar", sACT)
                p.op("scalar", lambda e: e.activation(mlt[:, :], mlt[:, :], AF.Exp, bias=0.0, scale=0.5), incs=(sACT, 1))
                p.wait("vector", sACT)
                p.op("vector", lambda e: e.tensor_tensor(nx[:, :], gx[:, :], xF[:, :], ALU.mult), incs=(sDVE, 1))
                p.wait("vector", sDVE)
                p.op("vector", lambda e: e.tensor_tensor(nx[:, :], nx[:, :], mlt[:, :], ALU.mult), incs=(sDVE, 1))
                p.wait("vector", sDVE)
                p.op("vector", lambda e: e.tensor_tensor_scan(
                    cpb[:, 1:T + 1], av[:, :], zeros[:, :], 1.0, ALU.mult, ALU.add), incs=(sDVE, 1))
                p.wait("vector", sDVE)
                p.op("vector", lambda e: e.tensor_scalar(pcl[:, :], cpb[:, 0:T], 1e-7, None, ALU.max), incs=(sDVE, 1))
                dv = sDVE.n
                p.wait("scalar", sDVE, dv)
                p.op("scalar", lambda e: e.activation(rp[:, :], pcl[:, :], AF.Ln, bias=0.0, scale=1.0), incs=(sACT, 1))
                p.wait("scalar", sACT)
                p.op("scalar", lambda e: e.activation(rp[:, :], rp[:, :], AF.Exp, bias=0.0, scale=-1.0), incs=(sACT, 1))
                p.wait("vector", sACT)
                p.op("vector", lambda e: e.tensor_scalar(nx[:, :], nx[:, :], 1e-7, None, ALU.add), incs=(sDVE, 1))
                p.wait("vector", sDVE)
                p.op("vector", lambda e: e.tensor_tensor(nx[:, :], nx[:, :], rp[:, :], ALU.mult), incs=(sDVE, 1))
                p.wait("vector", sDVE)
                p.op("vector", lambda e: e.tensor_tensor_scan(
                    Sb[:, :], nx[:, :], zeros[:, :], 0.0, ALU.add, ALU.add), incs=(sDVE, 1))
                p.wait("vector", sDVE)
                p.op("vector", lambda e: e.tensor_tensor(Sb[:, :], Sb[:, :], pcl[:, :], ALU.mult), incs=(sDVE, 1))
                hv_done = sDVE.n

                for tt in range(32):
                    par = tt % 2
                    stg = stgF if par == 0 else stg2
                    slot = tt % 4
                    p.wait("tensor", sDVE, hv_done)
                    if tt >= 4:
                        p.wait("tensor", sDVE)
                    p.op("tensor", lambda e, tt=tt, slot=slot: e.transpose(
                        pG1[:, 512 * slot:512 * slot + 128],
                        Sb[:, 128 * tt:128 * (tt + 1)], ident_f[:, :]), incs=(sPE, 1))
                    pe = sPE.n
                    p.wait("vector", sPE, pe)
                    p.wait("vector", sST)
                    p.op("vector", lambda e, stg=stg, slot=slot: e.tensor_copy(
                        stg[:, 0:128], pG1[:, 512 * slot:512 * slot + 128]), incs=(sDVE, 1))
                    dvc = sDVE.n
                    p.wait("sync", sDVE, dvc)
                    p.dma("sync", h_out[128 * tt:128 * (tt + 1), 128 * cc:128 * (cc + 1)], stg[:, 0:128], sST)
            p.wait("sync", sST)
            p.flush()
    return nc



def build_launch4():
    nc = bass.Bass("TRN2", target_bir_lowering=False, detect_race_conditions=False)
    dt = nc.dram_tensor
    x2_in = dt("x2", [TC, D], F32, kind="ExternalInput").ap()
    h_in = dt("h", [TC, D], F32, kind="ExternalInput").ap()
    win = dt("win", [D, 64], F32, kind="ExternalInput").ap()
    bin_c = dt("bin_c", [64, 1], F32, kind="ExternalInput").ap()
    w1c = dt("w1c", [64, 16], F32, kind="ExternalInput").ap()
    b1c = dt("b1c", [64, 16], F32, kind="ExternalInput").ap()
    w2c = dt("w2c", [64, 16], F32, kind="ExternalInput").ap()
    b2c = dt("b2c", [64, 1], F32, kind="ExternalInput").ap()
    wout = dt("wout", [64, D], F32, kind="ExternalInput").ap()
    bout_row = dt("bout_row", [1, D], F32, kind="ExternalInput").ap()
    ones_row = dt("ones_row", [1, 128], F32, kind="ExternalInput").ap()
    ident_in = dt("ident_in", [128, 128], F32, kind="ExternalInput").ap()
    out_t = dt("out", [TC, D], F32, kind="ExternalOutput").ap()

    with contextlib.ExitStack() as ctx:
        p = Prog(nc, ctx)
        sLD = p.dsem("sLD")
        sPE = p.sem("sPE")
        sACT = p.sem("sACT")
        sDVE = p.sem("sDVE")
        sST = p.dsem("sST")

        def sb(name, shape, dtyp):
            return ctx.enter_context(nc.sbuf_tensor(name, shape, dtyp))

        g = "gpsimd"
        onesr_bf = sb("onesr_bf", [1, 128], BF16)
        ident_bf = sb("ident_bf", [128, 128], BF16)
        epsc = sb("epsc", [128, 2], F32)
        bin_sb = sb("bin_sb", [64, 1], F32)
        w1_sb = sb("w1_sb", [64, 16], F32)
        b1_sb = sb("b1_sb", [64, 16], F32)
        w2_sb = sb("w2_sb", [64, 16], F32)
        b2_sb = sb("b2_sb", [64, 1], F32)
        win_sb = sb("win_sb", [128, 8 * 64], BF16)
        wout_sb = sb("wout_sb", [64, D], BF16)
        bout_hi = sb("bout_hi", [1, D], BF16)
        bout_lo = sb("bout_lo", [1, D], BF16)
        brow = sb("brow", [1, D], F32)
        xs = sb("xs", [128, 16 * 1024], F32)
        x3T = sb("x3T", [128, 8 * TC], BF16)
        scr2 = sb("scr2", [128, 1024], F32)
        sA2a = sb("sA2a", [128, 16], F32)
        sQ2a = sb("sQ2a", [128, 16], F32)
        lnx = sb("lnx", [128, 8 * 16], F32)
        lnmean = lnx.ap()[:, 32:48]
        lnmsq = lnx.ap()[:, 48:64]
        lnvar = lnx.ap()[:, 64:80]
        lnrstd = lnx.ap()[:, 80:96]
        lnnm = lnx.ap()[:, 96:112]

        p.dma(g, onesr_bf[:, :], ones_row, sLD)
        p.dma(g, ident_bf[:, :], ident_in, sLD)
        p.dma("sync", bin_sb[:, :], bin_c, sLD)
        p.dma("sync", w1_sb[:, :], w1c, sLD)
        p.dma("sync", b1_sb[:, :], b1c, sLD)
        p.dma("sync", w2_sb[:, :], w2c, sLD)
        p.dma("sync", b2_sb[:, :], b2c, sLD)
        p.dma("sync", brow[:, :], bout_row, sLD)
        for k in range(8):
            p.dma(g, win_sb[:, 64 * k:64 * (k + 1)], win[128 * k:128 * (k + 1), :], sLD)
        p.dma(g, wout_sb[:, :], wout, sLD)
        n_pre = sLD.n

        p.op("vector", lambda e: e.memset(epsc[:, 0:1], 1e-20), incs=(sDVE, 1))
        p.op("vector", lambda e: e.memset(epsc[:, 1:2], 1e-5), incs=(sDVE, 1))
        p.wait("vector", sLD, n_pre)
        p.wait("scalar", sLD, n_pre)
        p.op("vector", lambda e: e.tensor_copy(bout_hi[:, :], brow[:, :]), incs=(sDVE, 1))
        p.wait("vector", sDVE)
        p.op("vector", lambda e: e.tensor_tensor(brow[:, :], brow[:, :], bout_hi[:, :], ALU.subtract), incs=(sDVE, 1))
        p.wait("vector", sDVE)
        p.op("vector", lambda e: e.tensor_copy(bout_lo[:, :], brow[:, :]), incs=(sDVE, 1))

        def emit_ln(src_wait_fn, out_idx_iter):
            for ti in out_idx_iter:
                src_wait_fn(ti)
                p.op("scalar", lambda e, ti=ti: e.activation(
                    scr2[:, :], xs[:, 1024 * ti:1024 * (ti + 1)], AF.Copy, bias=0.0,
                    scale=1.0, accum_out=sA2a[:, ti:ti + 1]), incs=(sACT, 1))
                p.op("scalar", lambda e, ti=ti: e.activation(
                    scr2[:, :], xs[:, 1024 * ti:1024 * (ti + 1)], AF.Square, bias=0.0,
                    scale=1.0, accum_out=sQ2a[:, ti:ti + 1]), incs=(sACT, 1))
            p.wait("vector", sACT)
            p.op("vector", lambda e: e.tensor_scalar(lnmean, sA2a[:, :], 1.0 / D, None, ALU.mult), incs=(sDVE, 1))
            p.wait("vector", sDVE)
            p.op("vector", lambda e: e.tensor_tensor(lnmsq, lnmean, lnmean, ALU.mult), incs=(sDVE, 1))
            p.op("vector", lambda e: e.tensor_scalar(lnvar, sQ2a[:, :], 1.0 / D, None, ALU.mult), incs=(sDVE, 1))
            p.wait("vector", sDVE)
            p.op("vector", lambda e: e.tensor_tensor(lnvar, lnvar, lnmsq, ALU.subtract), incs=(sDVE, 1))
            p.wait("scalar", sDVE)
            p.op("scalar", lambda e: e.activation(lnvar, lnvar, AF.Ln, bias=epsc[:, 1:2], scale=1.0), incs=(sACT, 1))
            p.wait("scalar", sACT)
            p.op("scalar", lambda e: e.activation(lnrstd, lnvar, AF.Exp, bias=0.0, scale=-0.5), incs=(sACT, 1))
            p.wait("vector", sACT)
            p.op("vector", lambda e: e.tensor_scalar(lnnm, lnmean, -1.0, None, ALU.mult), incs=(sDVE, 1))
            p.wait("vector", sDVE)
            for ti in range(16):
                p.op("vector", lambda e, ti=ti: e.tensor_scalar(
                    xs[:, 1024 * ti:1024 * (ti + 1)], xs[:, 1024 * ti:1024 * (ti + 1)],
                    lnnm[:, ti:ti + 1], lnrstd[:, ti:ti + 1], ALU.add, ALU.mult), incs=(sDVE, 1))

        with nc.sbuf_tensor("hs", [128, 16 * 1024], F32) as hs, \
             nc.sbuf_tensor("stgC", [128, 1024], BF16) as stgC, \
             nc.sbuf_tensor("stgD", [128, 1024], BF16) as stgD, \
             nc.psum_tensor("pT0", [128, 1024], BF16) as pT0, \
             nc.psum_tensor("pT1", [128, 1024], BF16) as pT1:
            for i in range(16):
                p.dma("sync", xs[:, 1024 * i:1024 * (i + 1)], x2_in[128 * i:128 * (i + 1), :], sLD)
                p.dma(g, hs[:, 1024 * i:1024 * (i + 1)], h_in[128 * i:128 * (i + 1), :], sLD)
            all_ld = sLD.n
            p.wait("vector", sLD, all_ld)
            res_done = {}
            for ti in range(16):
                p.op("vector", lambda e, ti=ti: e.tensor_tensor(
                    xs[:, 1024 * ti:1024 * (ti + 1)], hs[:, 1024 * ti:1024 * (ti + 1)],
                    xs[:, 1024 * ti:1024 * (ti + 1)], ALU.add), incs=(sDVE, 1))
                res_done[ti] = sDVE.n
            emit_ln(lambda ti: p.wait("scalar", sDVE, res_done[ti]), range(16))
            ln3_done = sDVE.n

            pTs = [pT0, pT1]
            for i in range(16):
                stg = stgC if i % 2 == 0 else stgD
                pt = pTs[i % 2]
                p.wait("vector", sDVE, ln3_done)
                p.wait("vector", sPE)
                p.op("vector", lambda e, i=i, stg=stg: e.tensor_copy(
                    stg[:, :], xs[:, 1024 * i:1024 * (i + 1)]), incs=(sDVE, 1))
                dvs = sDVE.n
                p.wait("tensor", sDVE, dvs)
                for c in range(8):
                    p.op("tensor", lambda e, stg=stg, pt=pt, c=c: e.transpose(
                        pt[:, 128 * c:128 * (c + 1)], stg[:, 128 * c:128 * (c + 1)],
                        ident_bf[:, :]), incs=(sPE, 1) if c == 7 else ())
                p.wait("vector", sPE)
                p.op("vector", lambda e, i=i, pt=pt: e.tensor_copy(
                    x3T.ap().rearrange("p (c n) -> p c n", c=8)[:, :, 128 * i:128 * (i + 1)],
                    pt.ap().rearrange("p (c n) -> p c n", c=8)), incs=(sDVE, 1))
            p.flush()

        with nc.sbuf_tensor("s_sb", [64, TC], F32) as s_sb, \
             nc.sbuf_tensor("hb0", [64, TC], F32) as hb0, \
             nc.sbuf_tensor("hb1", [64, TC], F32) as hb1, \
             nc.sbuf_tensor("ga0", [64, TC], F32) as ga0, \
             nc.sbuf_tensor("ga1", [64, TC], F32) as ga1, \
             nc.sbuf_tensor("gbf", [64, TC], BF16) as gbf, \
             nc.psum_tensor("pS0", [64, 512], F32) as pS0, \
             nc.psum_tensor("pS1", [64, 512], F32) as pS1, \
             nc.psum_tensor("pO0", [128, 1024], F32) as pO0, \
             nc.psum_tensor("pO1", [128, 1024], F32) as pO1:
            pSs = [pS0, pS1]
            pOs = [pO0, pO1]
            p.wait("tensor", sDVE)
            s_done = {}
            for ts in range(4):
                pS = pSs[ts % 2]
                if ts >= 2:
                    p.wait("tensor", sDVE, s_done[ts - 2])
                for k in range(8):
                    p.op("tensor", lambda e, ts=ts, k=k, pS=pS: e.matmul(
                        pS[:, :], win_sb[:, 64 * k:64 * (k + 1)],
                        x3T[:, 2048 * k + 512 * ts:2048 * k + 512 * (ts + 1)],
                        start=(k == 0), stop=(k == 7)),
                        incs=(sPE, 1) if k == 7 else ())
                p.wait("vector", sPE)
                p.op("vector", lambda e, ts=ts, pS=pS: e.tensor_scalar(
                    s_sb[:, 512 * ts:512 * (ts + 1)], pS[:, :], bin_sb[:, 0:1], None,
                    ALU.add), incs=(sDVE, 1))
                s_done[ts] = sDVE.n

            p.wait("scalar", sDVE)
            gas = [ga0, ga1]
            hbs = [hb0, hb1]
            for hh in range(16):
                hbuf = hbs[hh % 2]
                p.wait("scalar", sDVE)
                p.op("scalar", lambda e, hh=hh, hbuf=hbuf: e.activation(
                    hbuf[:, :], s_sb[:, :], AF.Relu, bias=b1_sb[:, hh:hh + 1],
                    scale=w1_sb[:, hh:hh + 1]), incs=(sACT, 1))
                av = sACT.n
                p.wait("vector", sACT, av)
                if hh == 0:
                    p.op("vector", lambda e, hbuf=hbuf: e.tensor_scalar(
                        gas[0][:, :], hbuf[:, :], w2_sb[:, 0:1], None, ALU.mult), incs=(sDVE, 1))
                else:
                    src_g, dst_g = gas[(hh + 1) % 2], gas[hh % 2]
                    p.op("vector", lambda e, hh=hh, hbuf=hbuf, src_g=src_g, dst_g=dst_g: e.scalar_tensor_tensor(
                        dst_g[:, :], hbuf[:, :], w2_sb[:, hh:hh + 1], src_g[:, :],
                        ALU.mult, ALU.add), incs=(sDVE, 1))
                p.wait("vector", sDVE)
            gfin = gas[15 % 2]
            p.op("vector", lambda e, gfin=gfin: e.tensor_scalar(
                gfin[:, :], gfin[:, :], b2_sb[:, 0:1], None, ALU.add), incs=(sDVE, 1))
            p.wait("vector", sDVE)
            p.op("vector", lambda e, gfin=gfin: e.tensor_copy(gbf[:, :], gfin[:, :]), incs=(sDVE, 1))
            g_done = sDVE.n

            p.wait("tensor", sDVE, g_done)
            o_done = {}
            for ti in range(16):
                pO = pOs[ti % 2]
                if ti >= 2:
                    p.wait("tensor", sDVE, o_done[ti - 2])
                for cc in range(2):
                    p.op("tensor", lambda e, ti=ti, cc=cc, pO=pO: e.matmul(
                        pO[:, 512 * cc:512 * (cc + 1)], gbf[:, 128 * ti:128 * (ti + 1)],
                        wout_sb[:, 512 * cc:512 * (cc + 1)], start=True, stop=False))
                    p.op("tensor", lambda e, cc=cc, pO=pO: e.matmul(
                        pO[:, 512 * cc:512 * (cc + 1)], onesr_bf[:, :],
                        bout_hi[:, 512 * cc:512 * (cc + 1)], start=False, stop=False))
                    p.op("tensor", lambda e, cc=cc, pO=pO: e.matmul(
                        pO[:, 512 * cc:512 * (cc + 1)], onesr_bf[:, :],
                        bout_lo[:, 512 * cc:512 * (cc + 1)], start=False, stop=True),
                        incs=(sPE, 1) if cc == 1 else ())
                p.wait("vector", sPE)
                p.op("vector", lambda e, ti=ti, pO=pO: e.tensor_tensor(
                    xs[:, 1024 * ti:1024 * (ti + 1)], pO[:, :],
                    xs[:, 1024 * ti:1024 * (ti + 1)], ALU.add), incs=(sDVE, 1))
                o_done[ti] = sDVE.n
            emit_ln(lambda ti: p.wait("scalar", sDVE, o_done[ti]), range(16))
            for ti in range(16):
                dv = sDVE.n
                p.wait("sync", sDVE, dv)
                p.dma("sync", out_t[128 * ti:128 * (ti + 1), :], xs[:, 1024 * ti:1024 * (ti + 1)], sST)
            p.wait("sync", sST)
            p.flush()
    return nc



def build_fused():
    nc = bass.Bass("TRN2", target_bir_lowering=False, detect_race_conditions=False)
    dt = nc.dram_tensor
    x_own = dt("x_own", [TC, D], F32, kind="ExternalInput").ap()
    x_oth = dt("x_oth", [TC, D], F32, kind="ExternalInput").ap()
    wq_d = dt("wq_d", [QL, D], F32, kind="ExternalInput").ap()
    w_qk = dt("w_qk", [D, QL], F32, kind="ExternalInput").ap()
    wkv_d = dt("wkv_d", [KV, D], F32, kind="ExternalInput").ap()
    wv_u = dt("wv_u", [D, KV], F32, kind="ExternalInput").ap()
    w_proj = dt("w_proj", [D, D], F32, kind="ExternalInput").ap()
    bq_cols = dt("bq_cols", [128, 4], F32, kind="ExternalInput").ap()
    bqk_cols = dt("bqk_cols", [64, 16], F32, kind="ExternalInput").ap()
    bkv_col = dt("bkv_col", [64, 1], F32, kind="ExternalInput").ap()
    bvu_row = dt("bvu_row", [1, D], F32, kind="ExternalInput").ap()
    bprj_row = dt("bprj_row", [1, D], F32, kind="ExternalInput").ap()
    ones_row = dt("ones_row", [1, 128], F32, kind="ExternalInput").ap()
    ident_in = dt("ident_in", [128, 128], F32, kind="ExternalInput").ap()
    halves = dt("halves", [128, 2], F32, kind="ExternalInput").ap()
    e16 = dt("e16", [8, 16, 128], F32, kind="ExternalInput").ap()
    wg_st = dt("wg_st", [128, 64], F32, kind="ExternalInput").ap()
    bg_bc = dt("bg_bc", [128, 8], F32, kind="ExternalInput").ap()
    w1 = dt("w1", [8, D, D], F32, kind="ExternalInput").ap()
    w2 = dt("w2", [8, D, D], F32, kind="ExternalInput").ap()
    b1cols = dt("b1cols", [128, 64], F32, kind="ExternalInput").ap()
    b2m = dt("b2m", [8, D], F32, kind="ExternalInput").ap()
    w_in_a = dt("w_in_a", [8, 128, 128], F32, kind="ExternalInput").ap()
    w_a_a = dt("w_a_a", [8, 128, 128], F32, kind="ExternalInput").ap()
    b_in_c8 = dt("b_in_c8", [128, 8], F32, kind="ExternalInput").ap()
    b_a_c8 = dt("b_a_c8", [128, 8], F32, kind="ExternalInput").ap()
    ap_c8 = dt("ap_c8", [128, 8], F32, kind="ExternalInput").ap()
    cmask_in = dt("cmask_in", [128, 16], F32, kind="ExternalInput").ap()
    cneut_in = dt("cneut_in", [128, 16], F32, kind="ExternalInput").ap()
    win = dt("win", [D, 64], F32, kind="ExternalInput").ap()
    bin_c = dt("bin_c", [64, 1], F32, kind="ExternalInput").ap()
    w1c = dt("w1c", [64, 16], F32, kind="ExternalInput").ap()
    b1c = dt("b1c", [64, 16], F32, kind="ExternalInput").ap()
    w2c = dt("w2c", [64, 16], F32, kind="ExternalInput").ap()
    b2c = dt("b2c", [64, 1], F32, kind="ExternalInput").ap()
    wout = dt("wout", [64, D], F32, kind="ExternalInput").ap()
    bout_row = dt("bout_row", [1, D], F32, kind="ExternalInput").ap()
    out_t = dt("out", [TC, D], F32, kind="ExternalOutput").ap()
    cc_in = dt("cc_in", [128, 16], F32, kind="Internal")
    cc_out = dt("cc_out", [256, 16], F32, kind="Internal")

    with contextlib.ExitStack() as ctx:
        p = Prog(nc, ctx)
        sLD = p.dsem("sLD")
        sPE = p.sem("sPE")
        sACT = p.sem("sACT")
        sDVE = p.sem("sDVE")
        sST = p.dsem("sST")
        sCC = p.sem("sCC")
        allsems = [sLD, sPE, sACT, sDVE, sST]

        def sb(name, shape, dtyp):
            return ctx.enter_context(nc.sbuf_tensor(name, shape, dtyp))

        g = "gpsimd"
        onesr_bf = sb("onesr_bf", [1, 128], BF16)
        ident_bf = sb("ident_bf", [128, 128], BF16)
        ident_f = sb("ident_f", [128, 128], F32)
        halves_bf = sb("halves_bf", [128, 2], BF16)
        e16_bf = sb("e16_bf", [16, 8 * 128], BF16)
        epsc = sb("epsc", [128, 3], F32)
        scr2 = sb("scr2", [128, 1024], F32)
        sA2a = sb("sA2a", [128, 16], F32)
        sQ2a = sb("sQ2a", [128, 16], F32)
        lnx = sb("lnx", [128, 8 * 16], F32)
        lnmean = lnx.ap()[:, 32:48]
        lnmsq = lnx.ap()[:, 48:64]
        lnvar = lnx.ap()[:, 64:80]
        lnrstd = lnx.ap()[:, 80:96]
        lnnm = lnx.ap()[:, 96:112]
        srow = sb("srow", [64, 1024], F32)

        p.dma(g, onesr_bf[:, :], ones_row, sLD)
        p.dma(g, ident_bf[:, :], ident_in, sLD)
        p.dma("sync", ident_f[:, :], ident_in, sLD)
        p.dma(g, halves_bf[:, :], halves, sLD)
        for c in range(8):
            p.dma(g, e16_bf[:, 128 * c:128 * (c + 1)], e16[c], sLD)
        p.op("vector", lambda e: e.memset(epsc[:, 0:1], 1e-20), incs=(sDVE, 1))
        p.op("vector", lambda e: e.memset(epsc[:, 1:2], 1e-5), incs=(sDVE, 1))
        p.op("vector", lambda e: e.memset(epsc[:, 2:3], 1.0), incs=(sDVE, 1))

        def emit_ln(wait_fn):
            for ti in range(16):
                wait_fn(ti)
                p.op("scalar", lambda e, ti=ti: e.activation(
                    scr2[:, :], xs[:, 1024 * ti:1024 * (ti + 1)], AF.Copy, bias=0.0,
                    scale=1.0, accum_out=sA2a[:, ti:ti + 1]), incs=(sACT, 1))
                p.op("scalar", lambda e, ti=ti: e.activation(
                    scr2[:, :], xs[:, 1024 * ti:1024 * (ti + 1)], AF.Square, bias=0.0,
                    scale=1.0, accum_out=sQ2a[:, ti:ti + 1]), incs=(sACT, 1))
            p.wait("vector", sACT)
            p.op("vector", lambda e: e.tensor_scalar(lnmean, sA2a[:, :], 1.0 / D, None, ALU.mult), incs=(sDVE, 1))
            p.wait("vector", sDVE)
            p.op("vector", lambda e: e.tensor_tensor(lnmsq, lnmean, lnmean, ALU.mult), incs=(sDVE, 1))
            p.op("vector", lambda e: e.tensor_scalar(lnvar, sQ2a[:, :], 1.0 / D, None, ALU.mult), incs=(sDVE, 1))
            p.wait("vector", sDVE)
            p.op("vector", lambda e: e.tensor_tensor(lnvar, lnvar, lnmsq, ALU.subtract), incs=(sDVE, 1))
            p.wait("scalar", sDVE)
            p.op("scalar", lambda e: e.activation(lnvar, lnvar, AF.Ln, bias=epsc[:, 1:2], scale=1.0), incs=(sACT, 1))
            p.wait("scalar", sACT)
            p.op("scalar", lambda e: e.activation(lnrstd, lnvar, AF.Exp, bias=0.0, scale=-0.5), incs=(sACT, 1))
            p.wait("vector", sACT)
            p.op("vector", lambda e: e.tensor_scalar(lnnm, lnmean, -1.0, None, ALU.mult), incs=(sDVE, 1))
            p.wait("vector", sDVE)
            for ti in range(16):
                p.op("vector", lambda e, ti=ti: e.tensor_scalar(
                    xs[:, 1024 * ti:1024 * (ti + 1)], xs[:, 1024 * ti:1024 * (ti + 1)],
                    lnnm[:, ti:ti + 1], lnrstd[:, ti:ti + 1], ALU.add, ALU.mult), incs=(sDVE, 1))

        bprj_hi = sb("bprj_hi", [1, D], BF16)
        bprj_lo = sb("bprj_lo", [1, D], BF16)

        p.dma("sync", srow[0:1, 0:D], bprj_row, sLD)
        n_pre0 = sLD.n
        p.wait("vector", sLD, n_pre0)
        p.op("vector", lambda e: e.tensor_copy(bprj_hi[:, :], srow[0:1, 0:D]), incs=(sDVE, 1))
        p.wait("vector", sDVE)
        p.op("vector", lambda e: e.tensor_tensor(srow[0:1, 0:D], srow[0:1, 0:D], bprj_hi[:, :], ALU.subtract), incs=(sDVE, 1))
        p.wait("vector", sDVE)
        p.op("vector", lambda e: e.tensor_copy(bprj_lo[:, :], srow[0:1, 0:D]), incs=(sDVE, 1))

        spanB = ctx.enter_context(contextlib.ExitStack())

        def sbB(name, shape, dtyp):
            return spanB.enter_context(nc.sbuf_tensor(name, shape, dtyp))

        maxq = sbB("maxq", [1, 16], F32)
        maxk = sbB("maxk", [1, 2], F32)
        bndrow = sbB("bndrow", [1, 16], F32)
        bnd_bf = sbB("bnd_bf", [1, 16], BF16)
        negbnd = sbB("negbnd", [128, 16], F32)
        wv_uT = sbB("wv_uT", [65, D], BF16)
        ckvT = sbB("ckvT", [64, T], BF16)
        ckv_tok = sbB("ckv_tok", [128, 32 * 65], BF16)
        qT = sbB("qT", [64, 16 * TC], BF16)
        p.dma(g, wv_uT[64:65, :], bvu_row, sLD)

        with nc.psum_tensor("pT0", [128, 1024], BF16) as pT0, \
             nc.psum_tensor("pT1", [128, 1024], BF16) as pT1, \
             nc.sbuf_tensor("xTo", [128, 8 * TC], BF16) as xTo, \
             nc.sbuf_tensor("c_qT", [128, 4 * TC], BF16) as c_qT, \
             nc.sbuf_tensor("qsq", [128, TC], BF16) as qsq, \
             nc.sbuf_tensor("ksq", [64, T], BF16) as ksq, \
             nc.sbuf_tensor("wq_dT", [128, 8 * QL], BF16) as wq_dT, \
             nc.sbuf_tensor("w_qkT", [128, 4 * D], BF16) as w_qkT, \
             nc.sbuf_tensor("wkv_dT", [128, 8 * KV], BF16) as wkv_dT, \
             nc.sbuf_tensor("bq_sb", [128, 4], F32) as bq_sb, \
             nc.sbuf_tensor("bqk_sb", [64, 16], F32) as bqk_sb, \
             nc.sbuf_tensor("bkv_sb", [64, 1], F32) as bkv_sb, \
             nc.sbuf_tensor("stgA", [128, 1024], BF16) as stgA, \
             nc.sbuf_tensor("stgB", [128, 1024], BF16) as stgB:
            p.dma("sync", bq_sb[:, :], bq_cols, sLD)
            p.dma("sync", bqk_sb[:, :], bqk_cols, sLD)
            p.dma("sync", bkv_sb[:, :], bkv_col, sLD)
            n_pre = sLD.n
            pTs = [pT0, pT1]
            tcnt = [0]
            tl_tp_done = [0, 0]
            tl_cp_done = [0, 0]
            tl_ld_done = [None, None]

            def tload(dram, R, C, dest_fn, dest_war=0):
                nR = (R + 127) // 128
                nCc = (C + 127) // 128
                cw = min(128, C)
                assert C % 128 == 0 or nCc == 1
                for r in range(nR):
                    Rw = min(128, R - 128 * r)
                    sl = tcnt[0] % 2
                    stg = stgA if sl == 0 else stgB
                    pt = pTs[sl]
                    tcnt[0] += 1
                    p.wait(g, sPE, tl_tp_done[sl])
                    p.dma(g, stg[0:Rw, 0:C], dram[128 * r:128 * r + Rw, :], sLD)
                    tl_ld_done[sl] = sLD.n
                    p.wait("tensor", sLD, tl_ld_done[sl])
                    p.wait("tensor", sDVE, tl_cp_done[sl])
                    for c in range(nCc):
                        ccw = min(128, C - 128 * c)
                        p.op("tensor", lambda e, stg=stg, pt=pt, r=r, c=c, Rw=Rw, ccw=ccw: e.transpose(
                            pt[0:ccw, 128 * c:128 * c + Rw],
                            stg[0:Rw, 128 * c:128 * c + ccw],
                            ident_bf[0:Rw, 0:Rw]),
                            incs=(sPE, 1) if c == nCc - 1 else ())
                    tl_tp_done[sl] = sPE.n
                    p.wait("vector", sPE, tl_tp_done[sl])
                    if dest_war:
                        p.wait("vector", sPE, dest_war)
                    if nCc == 1:
                        src_v = pt[0:cw, 0:Rw]
                    else:
                        src_v = pt.ap().rearrange("p (c n) -> p c n", c=8)[:, 0:nCc, 0:Rw]
                    p.op("vector", lambda e, d=dest_fn(r), s=src_v: e.tensor_copy(d, s),
                         incs=(sDVE, 1))
                    tl_cp_done[sl] = sDVE.n

            xT_r = xTo.ap().rearrange("p (c n) -> p c n", c=8)
            tload(x_own, TC, D, lambda r: xT_r[:, :, 128 * r:128 * r + 128])
            wqT_r = wq_dT.ap().rearrange("p (c n) -> p c n", c=8)
            tload(wq_d, QL, D, lambda r: wqT_r[:, :, 128 * r:128 * r + 128])
            wqkT_r = w_qkT.ap().rearrange("p (c n) -> p c n", c=4)
            tload(w_qk, D, QL, lambda r: wqkT_r[:, 0:4, 128 * r:128 * r + 128])
            wkvT_r = wkv_dT.ap().rearrange("p (c n) -> p c n", c=8)
            tload(wkv_d, KV, D, lambda r: wkvT_r[:, :, 0:64])
            tload(wv_u, D, KV, lambda r: wv_uT[0:64, 128 * r:128 * r + 128])

            with nc.psum_tensor("pmA", [128, 512], F32) as pmA, \
                 nc.psum_tensor("pmB", [128, 512], F32) as pmB, \
                 nc.psum_tensor("pq", [1, 2048], F32) as pq:
                pms = [pmA, pmB]
                pm_i = [0]
                pm_done = [0, 0]
                p.wait("tensor", sDVE)
                p.wait("scalar", sLD, n_pre)

                def mm_chain(out_view, lhsTs, rhss, bias_ap, raw=0):
                    sl = pm_i[0] % 2
                    pm = pms[sl]
                    pm_i[0] += 1
                    p.wait("tensor", sDVE, max(pm_done[sl], raw))
                    nk = len(lhsTs)
                    M = out_view.shape[0]
                    N = out_view.shape[-1]
                    for k in range(nk):
                        p.op("tensor", lambda e, k=k, pm=pm, M=M, N=N, lh=lhsTs[k], rh=rhss[k], nk=nk: e.matmul(
                            pm[0:M, 0:N], lh, rh, start=(k == 0), stop=(k == nk - 1)),
                            incs=(sPE, 1) if k == nk - 1 else ())
                    p.wait("vector", sPE)
                    p.op("vector", lambda e, pm=pm, M=M, N=N, ov=out_view, b=bias_ap: e.tensor_scalar(
                        ov, pm[0:M, 0:N], b, None, ALU.add),
                        incs=(sDVE, 1))
                    pm_done[sl] = sDVE.n
                    return pm

                def ckv_slices(lo, hi, raw=0):
                    for sl in range(lo, hi):
                        off = 512 * (sl % 4)
                        mm_chain(ckvT[0:64, 512 * sl:512 * (sl + 1)],
                                 [wkv_dT[:, 64 * k:64 * (k + 1)] for k in range(8)],
                                 [xTo[:, 2048 * k + off: 2048 * k + off + 512] for k in range(8)],
                                 bkv_sb[0:64, 0:1], raw=raw)

                tl_all = sDVE.n
                ckv_slices(0, 4, raw=tl_all)
                for oc in range(4):
                    for ts in range(4):
                        mm_chain(c_qT[:, 2048 * oc + 512 * ts: 2048 * oc + 512 * (ts + 1)],
                                 [wq_dT[:, 512 * k + 128 * oc: 512 * k + 128 * (oc + 1)] for k in range(8)],
                                 [xTo[:, 2048 * k + 512 * ts: 2048 * k + 512 * (ts + 1)] for k in range(8)],
                                 bq_sb[:, oc:oc + 1], raw=tl_all)
                cq_all = sDVE.n
                for h in range(16):
                    for ts in range(4):
                        mm_chain(qT[0:64, 2048 * h + 512 * ts: 2048 * h + 512 * (ts + 1)],
                                 [w_qkT[:, 1024 * k + 64 * h: 1024 * k + 64 * (h + 1)] for k in range(4)],
                                 [c_qT[:, 2048 * k + 512 * ts: 2048 * k + 512 * (ts + 1)] for k in range(4)],
                                 bqk_sb[:, h:h + 1], raw=cq_all)

                tload(x_oth, TC, D, lambda r: xT_r[:, :, 128 * r:128 * r + 128],
                      dest_war=sPE.n)
                ckv_slices(4, 8, raw=sDVE.n)

                tok_r = ckv_tok.ap().rearrange("p (s n) -> p s n", s=32)
                p.wait("tensor", sDVE)
                for gq in range(8):
                    pt = pTs[gq % 2]
                    p.wait("tensor", sDVE)
                    for j in range(4):
                        st = 4 * gq + j
                        p.op("tensor", lambda e, st=st, j=j, pt=pt: e.transpose(
                            pt[:, 64 * j:64 * j + 64],
                            ckvT[0:64, 128 * st:128 * (st + 1)],
                            ident_bf[0:64, 0:64]),
                            incs=(sPE, 1) if j == 3 else ())
                    p.wait("vector", sPE)
                    p.op("vector", lambda e, gq=gq, pt=pt: e.tensor_copy(
                        tok_r[:, 4 * gq:4 * (gq + 1), 0:64],
                        pt.ap().rearrange("p (c n) -> p c n", c=16)[:, 0:4, 0:64]),
                        incs=(sDVE, 1))
                p.op("vector", lambda e: e.memset(tok_r[:, :, 64:65], 1.0), incs=(sDVE, 1))

                p.wait("vector", sDVE)
                p.op("vector", lambda e: e.tensor_tensor(ksq[:, :], ckvT[0:64, :], ckvT[0:64, :], ALU.mult), incs=(sDVE, 1))
                p.wait("tensor", sDVE)
                for rnd in range(2):
                    for sl in range(4):
                        p.op("tensor", lambda e, rnd=rnd, sl=sl: e.matmul(
                            pq[0:1, 512 * sl:512 * (sl + 1)], halves_bf[0:64, 0:1],
                            ksq[0:64, 512 * (4 * rnd + sl):512 * (4 * rnd + sl + 1)],
                            start=True, stop=True),
                            incs=(sPE, 1) if sl == 3 else ())
                    p.wait("vector", sPE)
                    p.op("vector", lambda e, rnd=rnd: e.tensor_reduce(
                        maxk[0:1, rnd:rnd + 1], pq[0:1, :], AX.X, ALU.max), incs=(sDVE, 1))
                    p.wait("tensor", sDVE)
                p.wait("vector", sDVE)
                p.op("vector", lambda e: e.tensor_tensor(
                    maxk[0:1, 0:1], maxk[0:1, 0:1], maxk[0:1, 1:2], ALU.max), incs=(sDVE, 1))
                for h in range(16):
                    p.wait("vector", sDVE)
                    p.op("vector", lambda e, h=h: e.tensor_tensor(
                        qsq[0:64, :], qT[0:64, 2048 * h:2048 * (h + 1)], qT[0:64, 2048 * h:2048 * (h + 1)], ALU.mult),
                        incs=(sDVE, 1))
                    p.wait("tensor", sDVE)
                    for ts in range(4):
                        p.op("tensor", lambda e, ts=ts: e.matmul(
                            pq[0:1, 512 * ts:512 * (ts + 1)],
                            halves_bf[0:64, 0:1], qsq[0:64, 512 * ts:512 * (ts + 1)],
                            start=True, stop=True),
                            incs=(sPE, 1) if ts == 3 else ())
                    p.wait("vector", sPE)
                    p.op("vector", lambda e, h=h: e.tensor_reduce(
                        maxq[0:1, h:h + 1], pq[0:1, :], AX.X, ALU.max), incs=(sDVE, 1))
                    p.wait("tensor", sDVE)
                p.wait("vector", sDVE)
                p.op("vector", lambda e: e.tensor_scalar(
                    bndrow[:, :], maxq[:, :], maxk[0:1, 0:1], None, ALU.mult), incs=(sDVE, 1))
                p.wait("scalar", sDVE)
                p.op("scalar", lambda e: e.activation(bndrow[:, :], bndrow[:, :], AF.Ln, bias=epsc[0:1, 0:1], scale=1.0), incs=(sACT, 1))
                p.wait("scalar", sACT)
                p.op("scalar", lambda e: e.activation(bndrow[:, :], bndrow[:, :], AF.Exp, bias=0.0, scale=0.5), incs=(sACT, 1))
                p.wait("vector", sACT)
                p.op("vector", lambda e: e.tensor_scalar(
                    bndrow[:, :], bndrow[:, :], -0.13125, -0.5, ALU.mult, ALU.add), incs=(sDVE, 1))
                p.wait("vector", sDVE)
                p.op("vector", lambda e: e.tensor_copy(bnd_bf[:, :], bndrow[:, :]), incs=(sDVE, 1))
                p.wait("tensor", sDVE)
                p.op("tensor", lambda e: e.matmul(
                    pmA[0:128, 0:16], onesr_bf[:, :], bnd_bf[:, :], start=True, stop=True), incs=(sPE, 1))
                p.wait("vector", sPE)
                p.op("vector", lambda e: e.tensor_copy(negbnd[:, :], pmA[0:128, 0:16]), incs=(sDVE, 1))
                p.flush()

        groups = [(0, 4), (4, 2), (6, 4), (10, 2), (12, 4), (16, 2), (18, 4),
                  (22, 2), (24, 4), (28, 2), (30, 2)]
        NG = len(groups)

        yall_stack = ctx.enter_context(contextlib.ExitStack())
        yallT = yall_stack.enter_context(nc.sbuf_tensor("yallT", [128, 8 * TC], BF16, side="right"))
        with nc.sbuf_tensor("sums16", [16, TC], F32) as sums16, \
             nc.sbuf_tensor("rec_hi", [16, TC], BF16) as rec_hi, \
             nc.sbuf_tensor("rec_lo", [16, TC], BF16) as rec_lo:
            with nc.sbuf_tensor("stripA", [128, 2048], BF16) as stripA, \
                 nc.sbuf_tensor("stripB", [128, 1024], BF16) as stripB, \
                 nc.sbuf_tensor("zsb0", [65, 512], BF16) as zsb0, \
                 nc.sbuf_tensor("zsb1", [65, 512], BF16) as zsb1, \
                 nc.psum_tensor("psA", [128, 2048], F32) as psA, \
                 nc.psum_tensor("psB", [128, 1024], F32) as psB, \
                 nc.psum_tensor("psZ0", [65, 512], F32) as psZ0, \
                 nc.psum_tensor("psZ1", [65, 512], F32) as psZ1:
                zsbs = [zsb0, zsb1]
                psG = [psA, psB]
                psZ = [psZ0, psZ1]
                tok_r = ckv_tok.ap().rearrange("p (s n) -> p s n", s=32)
                last_exp = [0, 0]
                scores_done = {}
                exp_done = {}
                last_strip_reader = {}
                yraw_done = [0, 0]
                srow_dma = [0, 0]
                prev_iter = [None]
                p.wait("tensor", sACT)
                p.wait("tensor", sDVE)
                p.wait("scalar", sDVE)

                def emit_scores(h, t5, gi):
                    s0, glen = groups[gi]
                    par = gi % 2
                    pg = psG[par]
                    p.wait("tensor", sACT, last_exp[par])
                    for j in range(glen):
                        st = s0 + j
                        p.op("tensor", lambda e, st=st, j=j, pg=pg, h=h, t5=t5: e.matmul(
                            pg[:, 512 * j:512 * (j + 1)],
                            ckvT[0:64, 128 * st:128 * (st + 1)],
                            qT[0:64, 2048 * h + 512 * t5: 2048 * h + 512 * (t5 + 1)],
                            start=True, stop=True),
                            incs=(sPE, 1) if j == glen - 1 else ())
                    scores_done[gi] = sPE.n

                def emit_exp(h, t5, gi):
                    s0, glen = groups[gi]
                    par = gi % 2
                    pg = psG[par]
                    strip = stripA if par == 0 else stripB
                    p.wait("scalar", sPE, scores_done[gi])
                    lsr = last_strip_reader.get(par)
                    if lsr:
                        p.wait("scalar", sPE, lsr)
                    p.op("scalar", lambda e, pg=pg, strip=strip, glen=glen, h=h: e.activation(
                        strip[:, 0:512 * glen], pg[:, 0:512 * glen], AF.Exp,
                        bias=negbnd[:, h:h + 1], scale=0.125),
                        incs=(sACT, 1))
                    exp_done[gi] = sACT.n
                    last_exp[par] = sACT.n

                def emit_stage1(h, t5, gi, zp):
                    s0, glen = groups[gi]
                    par = gi % 2
                    strip = stripA if par == 0 else stripB
                    p.wait("tensor", sACT, exp_done[gi])
                    if gi == 0:
                        p.wait("tensor", sDVE, yraw_done[zp])
                    for j in range(glen):
                        st = s0 + j
                        p.op("tensor", lambda e, st=st, j=j, strip=strip, zp=zp,
                             first=(gi == 0 and j == 0), last=(gi == NG - 1 and j == glen - 1): e.matmul(
                            psZ[zp][:, :], tok_r[:, st, :], strip[:, 512 * j:512 * (j + 1)],
                            start=first, stop=last),
                            incs=(sPE, 1) if j == glen - 1 else ())
                    last_strip_reader[par] = sPE.n

                def emit_tail(h, t5, zp):
                    p.wait("scalar", sPE)
                    p.op("scalar", lambda e, zp=zp: e.activation(
                        zsbs[zp][:, :], psZ[zp][:, :], AF.Copy, bias=0.0, scale=1.0),
                        incs=(sACT, 1))
                    a_after = sACT.n
                    slot = (4 * h + t5) % 2
                    p.wait("vector", sACT, a_after)
                    p.wait("vector", sST, srow_dma[slot])
                    p.op("vector", lambda e, zp=zp, slot=slot: e.tensor_copy(
                        srow[0:1, 512 * slot:512 * (slot + 1)], psZ[zp][64:65, :]),
                        incs=(sDVE, 1))
                    d_after = sDVE.n
                    p.wait("sync", sDVE, d_after)
                    p.dma("sync", sums16[h:h + 1, 512 * t5:512 * (t5 + 1)],
                          srow[0:1, 512 * slot:512 * (slot + 1)], sST)
                    srow_dma[slot] = sST.n
                    return d_after, a_after

                def emit_stage2(info):
                    zp, h, t5, d_after, a_after = info
                    c, qrow = h // 2, (h % 2) * 64
                    p.wait("tensor", sDVE, d_after)
                    p.wait("tensor", sACT, a_after)
                    p.op("tensor", lambda e, zp=zp, h=h: e.matmul(
                        psZ[zp][0:64, :], wv_uT[:, 64 * h:64 * (h + 1)], zsbs[zp][:, :],
                        start=True, stop=True), incs=(sPE, 1))
                    pe_after = sPE.n
                    p.wait("vector", sPE, pe_after)
                    p.op("vector", lambda e, zp=zp, c=c, qrow=qrow, t5=t5: e.tensor_copy(
                        yallT[qrow:qrow + 64, 2048 * c + 512 * t5: 2048 * c + 512 * (t5 + 1)],
                        psZ[zp][0:64, :]), incs=(sDVE, 1))
                    yraw_done[zp] = sDVE.n

                for h in range(16):
                    for t5 in range(4):
                        it = 4 * h + t5
                        zp = it % 2
                        emit_scores(h, t5, 0)
                        for gi in range(NG):
                            if gi + 1 < NG:
                                emit_scores(h, t5, gi + 1)
                            if gi == 0 and prev_iter[0] is not None:
                                emit_stage2(prev_iter[0])
                                prev_iter[0] = None
                            if gi >= 1:
                                emit_stage1(h, t5, gi - 1, zp)
                            emit_exp(h, t5, gi)
                        emit_stage1(h, t5, NG - 1, zp)
                        prev_iter[0] = (zp, h, t5) + emit_tail(h, t5, zp)
                emit_stage2(prev_iter[0])

                p.wait("vector", sST)
                p.op("vector", lambda e: e.reciprocal(sums16[:, :], sums16[:, :]), incs=(sDVE, 1))
                p.wait("vector", sDVE)
                p.op("vector", lambda e: e.tensor_copy(rec_hi[:, :], sums16[:, :]), incs=(sDVE, 1))
                p.wait("vector", sDVE)
                p.op("vector", lambda e: e.tensor_tensor(sums16[:, :], sums16[:, :], rec_hi[:, :], ALU.subtract), incs=(sDVE, 1))
                p.wait("vector", sDVE)
                p.op("vector", lambda e: e.tensor_copy(rec_lo[:, :], sums16[:, :]), incs=(sDVE, 1))
                p.flush()

            with nc.psum_tensor("pS0", [128, 512], F32) as pS0, \
                 nc.psum_tensor("pS1", [128, 512], F32) as pS1:
                pSs = [pS0, pS1]
                p.barrier_all(allsems)
                scale_done = {}
                k_i = 0
                for c in range(8):
                    for t5 in range(4):
                        pS = pSs[k_i % 2]
                        if k_i >= 2:
                            p.wait("tensor", sDVE, scale_done[k_i - 2])
                        p.op("tensor", lambda e, c=c, t5=t5, pS=pS: e.matmul(
                            pS[:, :], e16_bf[:, 128 * c:128 * (c + 1)], rec_hi[:, 512 * t5:512 * (t5 + 1)],
                            start=True, stop=False))
                        p.op("tensor", lambda e, c=c, t5=t5, pS=pS: e.matmul(
                            pS[:, :], e16_bf[:, 128 * c:128 * (c + 1)], rec_lo[:, 512 * t5:512 * (t5 + 1)],
                            start=False, stop=True), incs=(sPE, 1))
                        p.wait("vector", sPE)
                        p.op("vector", lambda e, c=c, t5=t5, pS=pS: e.tensor_tensor(
                            yallT[:, 2048 * c + 512 * t5: 2048 * c + 512 * (t5 + 1)],
                            yallT[:, 2048 * c + 512 * t5: 2048 * c + 512 * (t5 + 1)],
                            pS[:, :], ALU.mult), incs=(sDVE, 1))
                        scale_done[k_i] = sDVE.n
                        k_i += 1
                p.flush()
        spanB.close()

        xs = ctx.enter_context(nc.sbuf_tensor("xs", [128, 16 * 1024], F32))
        for i in range(16):
            p.dma("sync", xs[:, 1024 * i:1024 * (i + 1)], x_own[128 * i:128 * (i + 1), :], sLD)
        xs_loaded = sLD.n

        with nc.sbuf_tensor("w_projT", [128, 8 * D], BF16) as w_projT, \
             nc.sbuf_tensor("stgP", [128, 1024], BF16) as stgP, \
             nc.psum_tensor("pTP", [128, 1024], BF16) as pTP, \
             nc.psum_tensor("pP0", [128, 512], F32) as pP0, \
             nc.psum_tensor("pP1", [128, 512], F32) as pP1:
            p.barrier_all(allsems)
            wpT_r = w_projT.ap().rearrange("p (c n) -> p c n", c=8)
            for r in range(8):
                p.wait(g, sPE)
                p.wait(g, sDVE)
                p.dma(g, stgP[:, :], w_proj[128 * r:128 * (r + 1), :], sLD)
                p.wait("tensor", sLD)
                p.wait("tensor", sDVE)
                for c in range(8):
                    p.op("tensor", lambda e, pt=pTP, c=c, r=r: e.transpose(
                        pTP[:, 128 * c:128 * (c + 1)], stgP[:, 128 * c:128 * (c + 1)],
                        ident_bf[:, :]), incs=(sPE, 1) if c == 7 else ())
                p.wait("vector", sPE)
                p.op("vector", lambda e, r=r: e.tensor_copy(
                    wpT_r[:, :, 128 * r:128 * r + 128],
                    pTP.ap().rearrange("p (c n) -> p c n", c=8)), incs=(sDVE, 1))

            pPs = [pP0, pP1]
            p.wait("tensor", sDVE)
            pr_i = 0
            prj_done = {}
            res_done = {}
            for ti in range(16):
                for cc in range(2):
                    pP = pPs[pr_i % 2]
                    if pr_i >= 2:
                        p.wait("tensor", sDVE, prj_done[pr_i - 2])
                    for k in range(8):
                        p.op("tensor", lambda e, ti=ti, cc=cc, k=k, pP=pP: e.matmul(
                            pP[:, :], yallT[:, 2048 * k + 128 * ti: 2048 * k + 128 * (ti + 1)],
                            w_projT[:, 1024 * k + 512 * cc: 1024 * k + 512 * (cc + 1)],
                            start=(k == 0), stop=False))
                    p.op("tensor", lambda e, cc=cc, pP=pP: e.matmul(
                        pP[:, :], onesr_bf[:, :], bprj_hi[:, 512 * cc:512 * (cc + 1)],
                        start=False, stop=False))
                    p.op("tensor", lambda e, cc=cc, pP=pP: e.matmul(
                        pP[:, :], onesr_bf[:, :], bprj_lo[:, 512 * cc:512 * (cc + 1)],
                        start=False, stop=True), incs=(sPE, 1))
                    p.wait("vector", sPE)
                    if pr_i == 0:
                        p.wait("vector", sLD, xs_loaded)
                    p.op("vector", lambda e, ti=ti, cc=cc, pP=pP: e.tensor_tensor(
                        xs[:, 1024 * ti + 512 * cc: 1024 * ti + 512 * (cc + 1)],
                        pP[:, :], xs[:, 1024 * ti + 512 * cc: 1024 * ti + 512 * (cc + 1)],
                        ALU.add), incs=(sDVE, 1))
                    prj_done[pr_i] = sDVE.n
                    pr_i += 1
                res_done[ti] = sDVE.n
            emit_ln(lambda ti: p.wait("scalar", sDVE, res_done[ti]))
            p.flush()
        yall_stack.close()

        moe_stack = ctx.enter_context(contextlib.ExitStack())

        def sbM(name, shape, dtyp):
            return moe_stack.enter_context(nc.sbuf_tensor(name, shape, dtyp))

        bg_sb = sbM("bg_sb", [128, 8], F32)
        b1_sb = sbM("b1_sb", [128, 64], F32)
        b2f = sbM("b2f", [8, D], F32)
        b2hi = sbM("b2hi", [8, D], BF16)
        b2lo = sbM("b2lo", [8, D], BF16)
        wgf = sbM("wgf", [128, 64], F32)
        wghi = sbM("wghi", [128, 64], BF16)
        wglo = sbM("wglo", [128, 64], BF16)
        maskF = sbM("maskF", [128, 16 * 8], F32)
        m8 = sbM("m8", [8, TC], BF16)
        x1T = sbM("x1T", [128, 8 * TC], BF16)

        p.dma("sync", bg_sb[:, :], bg_bc, sLD)
        p.dma("sync", b1_sb[:, :], b1cols, sLD)
        p.dma("sync", b2f[:, :], b2m, sLD)
        p.dma("sync", wgf[:, :], wg_st, sLD)
        n_pre2 = sLD.n
        p.wait("vector", sLD, n_pre2)
        p.wait("scalar", sLD, n_pre2)
        p.op("vector", lambda e: e.tensor_copy(b2hi[:, :], b2f[:, :]), incs=(sDVE, 1))
        p.op("vector", lambda e: e.tensor_copy(wghi[:, :], wgf[:, :]), incs=(sDVE, 1))
        p.wait("vector", sDVE)
        p.op("vector", lambda e: e.tensor_tensor(b2f[:, :], b2f[:, :], b2hi[:, :], ALU.subtract), incs=(sDVE, 1))
        p.op("vector", lambda e: e.tensor_tensor(wgf[:, :], wgf[:, :], wghi[:, :], ALU.subtract), incs=(sDVE, 1))
        p.wait("vector", sDVE)
        p.op("vector", lambda e: e.tensor_copy(b2lo[:, :], b2f[:, :]), incs=(sDVE, 1))
        p.op("vector", lambda e: e.tensor_copy(wglo[:, :], wgf[:, :]), incs=(sDVE, 1))

        with nc.sbuf_tensor("xloT", [128, 8 * TC], BF16) as xloT, \
             nc.sbuf_tensor("stgA2", [128, 1024], BF16) as stgA2, \
             nc.sbuf_tensor("stgB2", [128, 1024], BF16) as stgB2, \
             nc.sbuf_tensor("rt0", [128, 8], F32) as rt0, \
             nc.sbuf_tensor("rt1", [128, 8], F32) as rt1, \
             nc.sbuf_tensor("rt2", [128, 8], F32) as rt2, \
             nc.sbuf_tensor("rmx", [128, 1], F32) as rmx, \
             nc.sbuf_tensor("mbf", [128, 8], BF16) as mbf, \
             nc.psum_tensor("pT0b", [128, 1024], BF16) as pT0b, \
             nc.psum_tensor("pT1b", [128, 1024], BF16) as pT1b, \
             nc.psum_tensor("pL", [128, 8], F32) as pL, \
             nc.psum_tensor("pM8", [8, 128], BF16) as pM8:
            p.barrier_all(allsems)
            pTs = [pT0b, pT1b]
            stgs = [stgA2, stgB2]
            x1T_r = x1T.ap().rearrange("p (c n) -> p c n", c=8)
            xloT_r = xloT.ap().rearrange("p (c n) -> p c n", c=8)
            for i in range(16):
                stg = stgs[i % 2]
                pt = pTs[i % 2]
                p.wait("vector", sPE)
                p.wait("vector", sDVE)
                p.op("vector", lambda e, i=i, stg=stg: e.tensor_copy(
                    stg[:, :], xs[:, 1024 * i:1024 * (i + 1)]), incs=(sDVE, 1))
                dv_hi = sDVE.n
                p.wait("tensor", sDVE, dv_hi)
                p.wait("tensor", sDVE)
                for c in range(8):
                    p.op("tensor", lambda e, stg=stg, pt=pt, c=c: e.transpose(
                        pt[:, 128 * c:128 * (c + 1)], stg[:, 128 * c:128 * (c + 1)],
                        ident_bf[:, :]), incs=(sPE, 1) if c == 7 else ())
                p.wait("vector", sPE)
                p.op("vector", lambda e, i=i, pt=pt: e.tensor_copy(
                    x1T_r[:, :, 128 * i:128 * (i + 1)],
                    pt.ap().rearrange("p (c n) -> p c n", c=8)), incs=(sDVE, 1))
                p.wait("vector", sPE)
                p.op("vector", lambda e, i=i, stg=stg: e.tensor_tensor(
                    stg[:, :], xs[:, 1024 * i:1024 * (i + 1)], stg[:, :], ALU.subtract),
                    incs=(sDVE, 1))
                dv_lo = sDVE.n
                p.wait("tensor", sDVE, dv_lo)
                for c in range(8):
                    p.op("tensor", lambda e, stg=stg, pt=pt, c=c: e.transpose(
                        pt[:, 128 * c:128 * (c + 1)], stg[:, 128 * c:128 * (c + 1)],
                        ident_bf[:, :]), incs=(sPE, 1) if c == 7 else ())
                p.wait("vector", sPE)
                p.op("vector", lambda e, i=i, pt=pt: e.tensor_copy(
                    xloT_r[:, :, 128 * i:128 * (i + 1)],
                    pt.ap().rearrange("p (c n) -> p c n", c=8)), incs=(sDVE, 1))

            wg_r = wghi.ap().rearrange("p (c n) -> p c n", c=8)
            wgl_r = wglo.ap().rearrange("p (c n) -> p c n", c=8)
            p.wait("tensor", sDVE)
            for ti in range(16):
                nmm = [0]
                p.wait("tensor", sDVE)
                for k in range(8):
                    for lh, rh in ((x1T_r[:, k, 128 * ti:128 * (ti + 1)], wg_r[:, k, :]),
                                   (x1T_r[:, k, 128 * ti:128 * (ti + 1)], wgl_r[:, k, :]),
                                   (xloT_r[:, k, 128 * ti:128 * (ti + 1)], wg_r[:, k, :])):
                        first, last = nmm[0] == 0, nmm[0] == 23
                        p.op("tensor", lambda e, lh=lh, rh=rh, first=first, last=last: e.matmul(
                            pL[:, :], lh, rh, start=first, stop=last),
                            incs=(sPE, 1) if last else ())
                        nmm[0] += 1
                p.wait("vector", sPE)
                p.op("vector", lambda e: e.tensor_tensor(rt0[:, :], pL[:, :], bg_sb[:, :], ALU.add), incs=(sDVE, 1))
                p.wait("vector", sDVE)
                p.op("vector", lambda e: e.tensor_reduce(rmx[:, :], rt0[:, :], AX.X, ALU.max), incs=(sDVE, 1))
                p.wait("vector", sDVE)
                p.op("vector", lambda e: e.tensor_scalar(rt1[:, :], rt0[:, :], rmx[:, :], None, ALU.is_ge), incs=(sDVE, 1))
                p.wait("vector", sDVE)
                p.op("vector", lambda e: e.tensor_copy(rt2[:, 0:1], rt1[:, 0:1]), incs=(sDVE, 1))
                p.op("vector", lambda e: e.tensor_tensor(rt2[:, 1:8], rt1[:, 1:8], rt1[:, 0:7], ALU.add), incs=(sDVE, 1))
                p.wait("vector", sDVE)
                p.op("vector", lambda e: e.tensor_copy(rt0[:, 0:2], rt2[:, 0:2]), incs=(sDVE, 1))
                p.op("vector", lambda e: e.tensor_tensor(rt0[:, 2:8], rt2[:, 2:8], rt2[:, 0:6], ALU.add), incs=(sDVE, 1))
                p.wait("vector", sDVE)
                p.op("vector", lambda e: e.tensor_copy(rt2[:, 0:4], rt0[:, 0:4]), incs=(sDVE, 1))
                p.op("vector", lambda e: e.tensor_tensor(rt2[:, 4:8], rt0[:, 4:8], rt0[:, 0:4], ALU.add), incs=(sDVE, 1))
                p.wait("vector", sDVE)
                p.op("vector", lambda e: e.tensor_scalar(rt0[:, :], rt2[:, :], 1.5, None, ALU.is_le), incs=(sDVE, 1))
                p.wait("vector", sDVE)
                p.op("vector", lambda e, ti=ti: e.tensor_tensor(
                    maskF[:, 8 * ti:8 * (ti + 1)], rt1[:, :], rt0[:, :], ALU.mult), incs=(sDVE, 1))
                p.wait("vector", sDVE)
                p.op("vector", lambda e, ti=ti: e.tensor_copy(mbf[:, :], maskF[:, 8 * ti:8 * (ti + 1)]), incs=(sDVE, 1))
                dvm = sDVE.n
                p.wait("tensor", sDVE, dvm)
                p.op("tensor", lambda e: e.transpose(pM8[:, :], mbf[:, :], ident_bf[:, :]), incs=(sPE, 1))
                p.wait("vector", sPE)
                p.op("vector", lambda e, ti=ti: e.tensor_copy(m8[:, 128 * ti:128 * (ti + 1)], pM8[:, :]), incs=(sDVE, 1))
            p.flush()

        with nc.sbuf_tensor("w1sb", [128, 8 * D], BF16) as w1sb, \
             nc.sbuf_tensor("w2sb", [128, 8 * D], BF16) as w2sb, \
             nc.sbuf_tensor("h_e", [128, 8 * TC], BF16) as h_e, \
             nc.psum_tensor("pMa", [128, 1024], F32) as pMa, \
             nc.psum_tensor("pMb", [128, 1024], F32) as pMb, \
             nc.psum_tensor("pY0", [128, 1024], F32) as pY0, \
             nc.psum_tensor("pY1", [128, 1024], F32) as pY1:
            p.barrier_all(allsems)
            pYs = [pY0, pY1]
            pMs = [pMa, pMb]
            p.wait("tensor", sDVE)
            py_free = [0, 0]
            for ti in range(16):
                pY = pYs[ti % 2]
                p.wait("tensor", sDVE, py_free[ti % 2])
                for cc in range(2):
                    p.op("tensor", lambda e, ti=ti, cc=cc, pY=pY: e.matmul(
                        pY[:, 512 * cc:512 * (cc + 1)], m8[:, 128 * ti:128 * (ti + 1)],
                        b2hi[:, 512 * cc:512 * (cc + 1)], start=True, stop=False))
                    p.op("tensor", lambda e, ti=ti, cc=cc, pY=pY: e.matmul(
                        pY[:, 512 * cc:512 * (cc + 1)], m8[:, 128 * ti:128 * (ti + 1)],
                        b2lo[:, 512 * cc:512 * (cc + 1)], start=False, stop=True),
                        incs=(sPE, 1) if cc == 1 else ())
                p.wait("vector", sPE)
                p.op("vector", lambda e, ti=ti, pY=pY: e.tensor_tensor(
                    xs[:, 1024 * ti:1024 * (ti + 1)], pY[:, :],
                    xs[:, 1024 * ti:1024 * (ti + 1)], ALU.add), incs=(sDVE, 1))
                py_free[ti % 2] = sDVE.n

            gelu_done = {}
            gl_i = [0]

            def load_w1(e_i, wait_pe):
                p.wait(g, sPE, wait_pe)
                for k in range(8):
                    p.dma(g, w1sb[:, 1024 * k:1024 * (k + 1)], w1[e_i, 128 * k:128 * (k + 1), :], sLD)
                return sLD.n

            def load_w2(e_i, wait_pe):
                p.wait(g, sPE, wait_pe)
                for k in range(8):
                    p.dma(g, w2sb[:, 1024 * k:1024 * (k + 1)], w2[e_i, 128 * k:128 * (k + 1), :], sLD)
                return sLD.n

            w1_ld = load_w1(0, 0)
            w2_ld = load_w2(0, 0)
            for e_i in range(8):
                for ic in range(8):
                    for th in range(2):
                        u = gl_i[0]
                        pM = pMs[u % 2]
                        p.wait("tensor", sACT, gelu_done.get(u - 2, 0))
                        if ic == 0 and th == 0:
                            p.wait("tensor", sLD, w1_ld)
                        for t5 in range(2):
                            for k in range(8):
                                p.op("tensor", lambda e, ic=ic, th=th, t5=t5, k=k, pM=pM: e.matmul(
                                    pM[:, 512 * t5:512 * (t5 + 1)],
                                    w1sb[:, 1024 * k + 128 * ic:1024 * k + 128 * (ic + 1)],
                                    x1T[:, 2048 * k + 1024 * th + 512 * t5:2048 * k + 1024 * th + 512 * (t5 + 1)],
                                    start=(k == 0), stop=(k == 7)),
                                    incs=(sPE, 1) if (t5 == 1 and k == 7) else ())
                        p.wait("scalar", sPE)
                        p.op("scalar", lambda e, e_i=e_i, ic=ic, th=th, pM=pM: e.activation(
                            h_e[:, 2048 * ic + 1024 * th:2048 * ic + 1024 * (th + 1)],
                            pM[:, :], AF.Gelu, bias=b1_sb[:, 8 * e_i + ic:8 * e_i + ic + 1],
                            scale=1.0), incs=(sACT, 1))
                        gelu_done[u] = sACT.n
                        gl_i[0] += 1
                a_need = sACT.n
                w1_free = sPE.n
                if e_i + 1 < 8:
                    w1_ld = load_w1(e_i + 1, w1_free)
                for ti in range(16):
                    pY = pYs[ti % 2]
                    p.wait("tensor", sDVE, py_free[ti % 2])
                    if ti == 0:
                        p.wait("tensor", sACT, a_need)
                        p.wait("tensor", sLD, w2_ld)
                    for ic in range(8):
                        for cc in range(2):
                            p.op("tensor", lambda e, ti=ti, ic=ic, cc=cc, pY=pY: e.matmul(
                                pY[:, 512 * cc:512 * (cc + 1)],
                                h_e[:, 2048 * ic + 128 * ti:2048 * ic + 128 * (ti + 1)],
                                w2sb[:, 1024 * ic + 512 * cc:1024 * ic + 512 * (cc + 1)],
                                start=(ic == 0), stop=(ic == 7)),
                                incs=(sPE, 1) if (ic == 7 and cc == 1) else ())
                    p.wait("vector", sPE)
                    p.op("vector", lambda e, ti=ti, e_i=e_i, pY=pY: e.scalar_tensor_tensor(
                        xs[:, 1024 * ti:1024 * (ti + 1)], pY[:, :],
                        maskF[:, 8 * ti + e_i:8 * ti + e_i + 1],
                        xs[:, 1024 * ti:1024 * (ti + 1)],
                        ALU.mult, ALU.add), incs=(sDVE, 1))
                    py_free[ti % 2] = sDVE.n
                w2_free = sPE.n
                if e_i + 1 < 8:
                    w2_ld = load_w2(e_i + 1, w2_free)
            moe_done = dict(py_free=list(py_free))
            final_res = sDVE.n
            emit_ln(lambda ti: p.wait("scalar", sDVE, final_res))
            p.flush()
        moe_stack.close()

        scan_stack = ctx.enter_context(contextlib.ExitStack())

        def sbS(name, shape, dtyp):
            return scan_stack.enter_context(nc.sbuf_tensor(name, shape, dtyp))

        wi_sb = sbS("wi_sb", [128, 8 * 128], BF16)
        wa_sb = sbS("wa_sb", [128, 8 * 128], BF16)
        bi8_sb = sbS("bi8_sb", [128, 8], F32)
        ba8_sb = sbS("ba8_sb", [128, 8], F32)
        dc8 = sbS("dc8", [128, 8], F32)
        ctab = sbS("ctab", [128, 16], F32)
        gath = sbS("gath", [128, 16], F32)
        carry = sbS("carry", [128, 16], F32)
        cmask_sb = sbS("cmask_sb", [128, 16], F32)
        cneut_sb = sbS("cneut_sb", [128, 16], F32)

        for hb in range(8):
            p.dma(g, wi_sb[:, 128 * hb:128 * (hb + 1)], w_in_a[hb], sLD)
            p.dma(g, wa_sb[:, 128 * hb:128 * (hb + 1)], w_a_a[hb], sLD)
        p.dma("sync", bi8_sb[:, :], b_in_c8, sLD)
        p.dma("sync", ba8_sb[:, :], b_a_c8, sLD)
        p.dma("sync", dc8[:, :], ap_c8, sLD)
        p.dma("sync", cmask_sb[:, :], cmask_in, sLD)
        p.dma("sync", cneut_sb[:, :], cneut_in, sLD)
        n_pre3 = sLD.n
        p.wait("vector", sLD, n_pre3)
        p.wait("scalar", sLD, n_pre3)
        p.op("vector", lambda e: e.tensor_scalar(ctab[:, 0:8], dc8[:, :], -1.0, None, ALU.mult), incs=(sDVE, 1))
        p.wait("vector", sDVE)
        p.op("vector", lambda e: e.tensor_tensor(ctab[:, 0:8], ctab[:, 0:8], dc8[:, :], ALU.max), incs=(sDVE, 1))
        p.op("vector", lambda e: e.tensor_scalar(ctab[:, 8:16], dc8[:, :], 0.0, None, ALU.max), incs=(sDVE, 1))
        dv = sDVE.n
        p.wait("scalar", sDVE, dv)
        p.op("scalar", lambda e: e.activation(ctab[:, 0:8], ctab[:, 0:8], AF.Exp, bias=0.0, scale=-1.0), incs=(sACT, 1))
        p.wait("scalar", sACT)
        p.op("scalar", lambda e: e.activation(ctab[:, 0:8], ctab[:, 0:8], AF.Ln, bias=epsc[:, 2:3], scale=1.0), incs=(sACT, 1))
        p.wait("vector", sACT)
        p.op("vector", lambda e: e.tensor_tensor(dc8[:, :], ctab[:, 0:8], ctab[:, 8:16], ALU.add), incs=(sDVE, 1))
        p.wait("vector", sDVE)
        p.op("vector", lambda e: e.tensor_scalar(dc8[:, :], dc8[:, :], 1e-3, None, ALU.add), incs=(sDVE, 1))

        xF2 = sbS("xF2", [128, TC], F32)
        gx2 = sbS("gx2", [128, TC], F32)
        ga2 = sbS("ga2", [128, TC], F32)
        with nc.sbuf_tensor("xF", [128, TC], F32) as xF, \
             nc.sbuf_tensor("xFb", [128, TC], BF16) as xFb, \
             nc.sbuf_tensor("gx", [128, TC], F32) as gx, \
             nc.sbuf_tensor("ga", [128, TC], F32) as ga, \
             nc.sbuf_tensor("av", [128, TC], F32) as av, \
             nc.sbuf_tensor("mlt", [128, TC], F32) as mlt, \
             nc.sbuf_tensor("nx", [128, TC], F32) as nx, \
             nc.sbuf_tensor("cpb", [128, TC + 8], F32) as cpb, \
             nc.sbuf_tensor("Sb", [128, TC], F32) as Sb, \
             nc.sbuf_tensor("zeros", [128, TC], F32) as zeros, \
             nc.psum_tensor("pT3", [128, 2048], F32) as pT3, \
             nc.psum_tensor("pG1", [128, 2048], F32) as pG1:
            p.barrier_all(allsems)
            p.op("vector", lambda e: e.memset(zeros[:, :], 0.0), incs=(sDVE, 1))

            xFs = [xF, xF2]
            gxs = [gx, gx2]
            gas = [ga, ga2]
            sig_done = {}
            xfcopy_done = {}
            gatemm_done = {}
            bk_xf_free = {}
            bk_gx_free = {}
            bk_ga_free = {}
            bk_hadd = {}

            def scan_front(i, cc, pass2):
                par = i % 2
                xFp, gxp, gap = xFs[par], gxs[par], gas[par]
                p.wait("tensor", sACT, sig_done.get(i - 1, 0))
                p.wait("tensor", sDVE, max(bk_hadd.get(i - 2, 0), bk_xf_free.get(i - 2, 0)))
                for tt in range(16):
                    slot = tt % 4
                    if slot == 0 and tt > 0:
                        p.wait("tensor", sDVE)
                    p.op("tensor", lambda e, tt=tt, cc=cc, slot=slot: e.transpose(
                        pT3[:, 512 * slot:512 * slot + 128],
                        xs[:, 1024 * tt + 128 * cc:1024 * tt + 128 * (cc + 1)],
                        ident_f[:, :]), incs=(sPE, 1))
                    if slot == 3:
                        p.wait("vector", sPE)
                        p.op("vector", lambda e, tt=tt, xFp=xFp: e.tensor_copy(
                            xFp[:, 128 * (tt - 3):128 * (tt + 1)],
                            pT3.ap().rearrange("p (c n) -> p c n", c=4)[:, :, 0:128]),
                            incs=(sDVE, 1))
                p.wait("vector", sPE, gatemm_done.get(i - 1, 0))
                p.wait("vector", sDVE)
                p.op("vector", lambda e, xFp=xFp: e.tensor_copy(xFb[:, :], xFp[:, :]), incs=(sDVE, 1))
                xfcopy_done[i] = dvb = sDVE.n
                for wsb, bsb, dst, pG in ((wi_sb, bi8_sb, gxp, pT3), (wa_sb, ba8_sb, gap, pG1)):
                    p.wait("tensor", sDVE, dvb)
                    for t5 in range(4):
                        p.op("tensor", lambda e, cc=cc, wsb=wsb, t5=t5, pG=pG: e.matmul(
                            pG[:, 512 * t5:512 * (t5 + 1)],
                            wsb[:, 128 * cc:128 * (cc + 1)],
                            xFb[:, 512 * t5:512 * (t5 + 1)],
                            start=True, stop=True),
                            incs=(sPE, 1) if t5 == 3 else ())
                    p.wait("scalar", sPE)
                    if dst is gxp:
                        p.wait("scalar", sDVE, bk_gx_free.get(i - 2, 0))
                    else:
                        p.wait("scalar", sACT, bk_ga_free.get(i - 2, 0))
                    p.op("scalar", lambda e, cc=cc, bsb=bsb, dst=dst, pG=pG: e.activation(
                        dst[:, :], pG[:, :], AF.Sigmoid,
                        bias=bsb[:, cc:cc + 1], scale=1.0), incs=(sACT, 1))
                gatemm_done[i] = sPE.n
                sig_done[i] = sACT.n

            def scan_back(i, cc, pass2):
                par = i % 2
                xFp, gxp, gap = xFs[par], gxs[par], gas[par]
                p.wait("vector", sACT, sig_done[i])
                p.op("vector", lambda e, cc=cc, gap=gap: e.tensor_scalar(
                    gap[:, :], gap[:, :], dc8[:, cc:cc + 1], -1.0, ALU.mult, ALU.mult), incs=(sDVE, 1))
                dv = sDVE.n
                p.wait("scalar", sDVE, dv)
                p.op("scalar", lambda e, gap=gap: e.activation(av[:, :], gap[:, :], AF.Exp, bias=0.0, scale=1.0), incs=(sACT, 1))
                bk_ga_free[i] = sACT.n
                p.wait("vector", sACT)
                p.op("vector", lambda e: e.tensor_scalar(av[:, :], av[:, :], 0.9999899864196777, None, ALU.min), incs=(sDVE, 1))
                p.wait("vector", sDVE)
                p.op("vector", lambda e: e.scalar_tensor_tensor(
                    mlt[:, :], av[:, :], -1.0, av[:, :], ALU.mult, ALU.mult), incs=(sDVE, 1))
                p.wait("vector", sDVE)
                p.op("vector", lambda e: e.tensor_scalar(mlt[:, :], mlt[:, :], 1.0, 1e-5, ALU.add, ALU.max), incs=(sDVE, 1))
                dv = sDVE.n
                p.wait("scalar", sDVE, dv)
                p.op("scalar", lambda e: e.activation(mlt[:, :], mlt[:, :], AF.Ln, bias=0.0, scale=1.0), incs=(sACT, 1))
                p.wait("scalar", sACT)
                p.op("scalar", lambda e: e.activation(mlt[:, :], mlt[:, :], AF.Exp, bias=0.0, scale=0.5), incs=(sACT, 1))
                p.wait("vector", sACT)
                p.op("vector", lambda e, gxp=gxp, xFp=xFp: e.tensor_tensor(nx[:, :], gxp[:, :], xFp[:, :], ALU.mult), incs=(sDVE, 1))
                bk_xf_free[i] = sDVE.n
                p.wait("vector", sDVE)
                p.op("vector", lambda e: e.tensor_tensor(nx[:, :], nx[:, :], mlt[:, :], ALU.mult), incs=(sDVE, 1))
                p.wait("vector", sDVE)
                if pass2:
                    p.op("vector", lambda e, cc=cc: e.tensor_copy(cpb[:, 0:1], carry[:, cc:cc + 1]), incs=(sDVE, 1))
                    p.wait("vector", sDVE)
                    p.op("vector", lambda e, cc=cc: e.tensor_tensor_scan(
                        cpb[:, 1:TC + 1], av[:, :], zeros[:, :], carry[:, cc:cc + 1],
                        ALU.mult, ALU.add), incs=(sDVE, 1))
                else:
                    p.op("vector", lambda e: e.memset(cpb[:, 0:1], 1.0), incs=(sDVE, 1))
                    p.wait("vector", sDVE)
                    p.op("vector", lambda e: e.tensor_tensor_scan(
                        cpb[:, 1:TC + 1], av[:, :], zeros[:, :], 1.0,
                        ALU.mult, ALU.add), incs=(sDVE, 1))
                p.wait("vector", sDVE)
                p.op("vector", lambda e: e.tensor_scalar(mlt[:, :], cpb[:, 0:TC], 1e-7, None, ALU.max), incs=(sDVE, 1))
                dv = sDVE.n
                p.wait("scalar", sDVE, dv)
                p.op("scalar", lambda e, gxp=gxp: e.activation(gxp[:, :], mlt[:, :], AF.Ln, bias=0.0, scale=1.0), incs=(sACT, 1))
                p.wait("scalar", sACT)
                p.op("scalar", lambda e, gxp=gxp: e.activation(gxp[:, :], gxp[:, :], AF.Exp, bias=0.0, scale=-1.0), incs=(sACT, 1))
                p.wait("vector", sACT)
                p.op("vector", lambda e, gxp=gxp: e.scalar_tensor_tensor(
                    nx[:, :], nx[:, :], 1e-7, gxp[:, :], ALU.add, ALU.mult), incs=(sDVE, 1))
                bk_gx_free[i] = sDVE.n
                p.wait("vector", sDVE)
                if pass2:
                    p.op("vector", lambda e, cc=cc: e.tensor_tensor_scan(
                        Sb[:, :], nx[:, :], zeros[:, :], carry[:, 8 + cc:8 + cc + 1],
                        ALU.add, ALU.add), incs=(sDVE, 1))
                else:
                    p.op("vector", lambda e: e.tensor_tensor_scan(
                        Sb[:, :], nx[:, :], zeros[:, :], 0.0,
                        ALU.add, ALU.add), incs=(sDVE, 1))
                p.wait("vector", sDVE)
                if not pass2:
                    p.op("vector", lambda e, cc=cc: e.tensor_copy(ctab[:, cc:cc + 1], cpb[:, TC:TC + 1]), incs=(sDVE, 1))
                    p.op("vector", lambda e, cc=cc: e.tensor_copy(ctab[:, 8 + cc:8 + cc + 1], Sb[:, TC - 1:TC]), incs=(sDVE, 1))
                    bk_hadd[i] = sDVE.n
                else:
                    p.op("vector", lambda e: e.tensor_tensor(Sb[:, :], Sb[:, :], mlt[:, :], ALU.mult), incs=(sDVE, 1))
                    hv_done = sDVE.n
                    p.wait("tensor", sDVE, hv_done)
                    p.wait("tensor", sACT, sig_done.get(i + 1, sig_done[i]))
                    for tt in range(16):
                        slot = tt % 4
                        if slot == 0:
                            p.wait("tensor", sDVE)
                        p.op("tensor", lambda e, tt=tt, slot=slot: e.transpose(
                            pG1[:, 512 * slot:512 * slot + 128],
                            Sb[:, 128 * tt:128 * (tt + 1)], ident_f[:, :]), incs=(sPE, 1))
                        pe = sPE.n
                        p.wait("vector", sPE, pe)
                        p.op("vector", lambda e, tt=tt, cc=cc, slot=slot: e.tensor_tensor(
                            xs[:, 1024 * tt + 128 * cc:1024 * tt + 128 * (cc + 1)],
                            pG1[:, 512 * slot:512 * slot + 128],
                            xs[:, 1024 * tt + 128 * cc:1024 * tt + 128 * (cc + 1)],
                            ALU.add), incs=(sDVE, 1))
                    bk_hadd[i] = sDVE.n

            seq = [(cc, False) for cc in range(8)] + [(cc, True) for cc in range(8)]
            scan_front(0, *seq[0])
            for i in range(16):
                if i + 1 < 16:
                    scan_front(i + 1, *seq[i + 1])
                if i == 8:
                    dvt = bk_hadd[7]
                    p.wait("sync", sDVE, dvt)
                    p.dma("sync", cc_in[:, :], ctab[:, :], sST)
                    cin_done = sST.n
                    p.wait(g, sST, cin_done)
                    p.op(g, lambda e: e.collective_compute(
                        "AllGather", ALU.bypass,
                        replica_groups=[[0, 1], [2, 3], [4, 5], [6, 7]],
                        ins=[cc_in.ap().opt()], outs=[cc_out.ap().opt()]), incs=(sCC, 1))
                    p.wait(g, sCC, 1)
                    p.dma(g, gath[:, :], cc_out.ap()[0:128, :], sLD)
                    gth_done = sLD.n
                    p.wait("vector", sLD, gth_done)
                    p.op("vector", lambda e: e.tensor_tensor(carry[:, :], gath[:, :], cmask_sb[:, :], ALU.mult), incs=(sDVE, 1))
                    p.wait("vector", sDVE)
                    p.op("vector", lambda e: e.tensor_tensor(carry[:, :], carry[:, :], cneut_sb[:, :], ALU.add), incs=(sDVE, 1))
                    p.wait("vector", sDVE)
                scan_back(i, *seq[i])
            scan_res = sDVE.n
            emit_ln(lambda ti: p.wait("scalar", sDVE, scan_res))
            p.flush()
        scan_stack.close()

        bin_sb = sb("bin_sb", [64, 1], F32)
        w1_sbf = sb("w1_sbf", [64, 16], F32)
        b1_sbf = sb("b1_sbf", [64, 16], F32)
        w2_sbf = sb("w2_sbf", [64, 16], F32)
        b2_sbf = sb("b2_sbf", [64, 1], F32)
        win_sb = sb("win_sb", [128, 8 * 64], BF16)
        wout_sb = sb("wout_sb", [64, D], BF16)
        bout_hi = sb("bout_hi", [1, D], BF16)
        bout_lo = sb("bout_lo", [1, D], BF16)
        x3T = sb("x3T", [128, 8 * TC], BF16)
        hb2 = sb("hb2", [64, TC], F32)
        hb3 = sb("hb3", [64, TC], F32)

        p.dma("sync", bin_sb[:, :], bin_c, sLD)
        p.dma("sync", w1_sbf[:, :], w1c, sLD)
        p.dma("sync", b1_sbf[:, :], b1c, sLD)
        p.dma("sync", w2_sbf[:, :], w2c, sLD)
        p.dma("sync", b2_sbf[:, :], b2c, sLD)
        p.dma("sync", srow[0:1, 0:D], bout_row, sLD)
        for k in range(8):
            p.dma(g, win_sb[:, 64 * k:64 * (k + 1)], win[128 * k:128 * (k + 1), :], sLD)
        p.dma(g, wout_sb[:, :], wout, sLD)
        n_pre4 = sLD.n
        p.wait("vector", sLD, n_pre4)
        p.wait("scalar", sLD, n_pre4)
        p.op("vector", lambda e: e.tensor_copy(bout_hi[:, :], srow[0:1, 0:D]), incs=(sDVE, 1))
        p.wait("vector", sDVE)
        p.op("vector", lambda e: e.tensor_tensor(srow[0:1, 0:D], srow[0:1, 0:D], bout_hi[:, :], ALU.subtract), incs=(sDVE, 1))
        p.wait("vector", sDVE)
        p.op("vector", lambda e: e.tensor_copy(bout_lo[:, :], srow[0:1, 0:D]), incs=(sDVE, 1))

        with nc.sbuf_tensor("stgC", [128, 1024], BF16) as stgC, \
             nc.sbuf_tensor("stgD", [128, 1024], BF16) as stgD, \
             nc.sbuf_tensor("s_sb", [64, TC], F32) as s_sb, \
             nc.sbuf_tensor("hb0", [64, TC], F32) as hb0, \
             nc.sbuf_tensor("hb1", [64, TC], F32) as hb1, \
             nc.sbuf_tensor("ga0", [64, TC], F32) as ga0, \
             nc.sbuf_tensor("ga1", [64, TC], F32) as ga1, \
             nc.sbuf_tensor("gbf", [64, TC], BF16) as gbf, \
             nc.psum_tensor("pT0c", [128, 1024], BF16) as pT0c, \
             nc.psum_tensor("pT1c", [128, 1024], BF16) as pT1c, \
             nc.psum_tensor("pS0c", [64, 512], F32) as pS0c, \
             nc.psum_tensor("pS1c", [64, 512], F32) as pS1c, \
             nc.psum_tensor("pO0", [128, 1024], F32) as pO0, \
             nc.psum_tensor("pO1", [128, 1024], F32) as pO1:
            p.barrier_all(allsems)
            pTs = [pT0c, pT1c]
            stgs = [stgC, stgD]
            tp3_done = [0, 0]
            cp3_done = [0, 0]
            for i in range(16):
                sl = i % 2
                stg = stgs[sl]
                pt = pTs[sl]
                p.wait("vector", sPE, tp3_done[sl])
                p.op("vector", lambda e, i=i, stg=stg: e.tensor_copy(
                    stg[:, :], xs[:, 1024 * i:1024 * (i + 1)]), incs=(sDVE, 1))
                dvs = sDVE.n
                p.wait("tensor", sDVE, max(dvs, cp3_done[sl]))
                for c in range(8):
                    p.op("tensor", lambda e, stg=stg, pt=pt, c=c: e.transpose(
                        pt[:, 128 * c:128 * (c + 1)], stg[:, 128 * c:128 * (c + 1)],
                        ident_bf[:, :]), incs=(sPE, 1) if c == 7 else ())
                tp3_done[sl] = sPE.n
                p.wait("vector", sPE, tp3_done[sl])
                p.op("vector", lambda e, i=i, pt=pt: e.tensor_copy(
                    x3T.ap().rearrange("p (c n) -> p c n", c=8)[:, :, 128 * i:128 * (i + 1)],
                    pt.ap().rearrange("p (c n) -> p c n", c=8)), incs=(sDVE, 1))
                cp3_done[sl] = sDVE.n

            pSs = [pS0c, pS1c]
            pOs = [pO0, pO1]
            p.wait("tensor", sDVE)
            s_done = {}
            for ts in range(4):
                pS = pSs[ts % 2]
                if ts >= 2:
                    p.wait("tensor", sDVE, s_done[ts - 2])
                for k in range(8):
                    p.op("tensor", lambda e, ts=ts, k=k, pS=pS: e.matmul(
                        pS[:, :], win_sb[:, 64 * k:64 * (k + 1)],
                        x3T[:, 2048 * k + 512 * ts:2048 * k + 512 * (ts + 1)],
                        start=(k == 0), stop=(k == 7)),
                        incs=(sPE, 1) if k == 7 else ())
                p.wait("vector", sPE)
                p.op("vector", lambda e, ts=ts, pS=pS: e.tensor_scalar(
                    s_sb[:, 512 * ts:512 * (ts + 1)], pS[:, :], bin_sb[:, 0:1], None,
                    ALU.add), incs=(sDVE, 1))
                s_done[ts] = sDVE.n

            all_s = sDVE.n
            p.wait("scalar", sDVE, all_s)
            gas = [ga0, ga1]
            hbs = [hb0, hb1, hb2, hb3]
            stt_done = {}
            for hh in range(16):
                hbuf = hbs[hh % 4]
                p.wait("scalar", sDVE, stt_done.get(hh - 4, 0))
                p.op("scalar", lambda e, hh=hh, hbuf=hbuf: e.activation(
                    hbuf[:, :], s_sb[:, :], AF.Relu, bias=b1_sbf[:, hh:hh + 1],
                    scale=w1_sbf[:, hh:hh + 1]), incs=(sACT, 1))
                av_ = sACT.n
                p.wait("vector", sACT, av_)
                if hh == 0:
                    p.op("vector", lambda e, hbuf=hbuf: e.tensor_scalar(
                        gas[0][:, :], hbuf[:, :], w2_sbf[:, 0:1], None, ALU.mult), incs=(sDVE, 1))
                else:
                    src_g, dst_g = gas[(hh + 1) % 2], gas[hh % 2]
                    p.op("vector", lambda e, hh=hh, hbuf=hbuf, src_g=src_g, dst_g=dst_g: e.scalar_tensor_tensor(
                        dst_g[:, :], hbuf[:, :], w2_sbf[:, hh:hh + 1], src_g[:, :],
                        ALU.mult, ALU.add), incs=(sDVE, 1))
                stt_done[hh] = sDVE.n
            gfin = gas[15 % 2]
            p.op("vector", lambda e, gfin=gfin: e.tensor_scalar(
                gfin[:, :], gfin[:, :], b2_sbf[:, 0:1], None, ALU.add), incs=(sDVE, 1))
            p.wait("vector", sDVE)
            p.op("vector", lambda e, gfin=gfin: e.tensor_copy(gbf[:, :], gfin[:, :]), incs=(sDVE, 1))
            g_done = sDVE.n

            p.wait("tensor", sDVE, g_done)
            o_done = {}
            for tp in range(8):
                if tp >= 1:
                    p.wait("tensor", sDVE, o_done[2 * tp - 1])
                pe_marks = [0, 0]
                for sub in range(2):
                    ti = 2 * tp + sub
                    pO = pOs[sub]
                    for cc in range(2):
                        p.op("tensor", lambda e, ti=ti, cc=cc, pO=pO: e.matmul(
                            pO[:, 512 * cc:512 * (cc + 1)], gbf[:, 128 * ti:128 * (ti + 1)],
                            wout_sb[:, 512 * cc:512 * (cc + 1)], start=True, stop=False))
                        p.op("tensor", lambda e, cc=cc, pO=pO: e.matmul(
                            pO[:, 512 * cc:512 * (cc + 1)], onesr_bf[:, :],
                            bout_hi[:, 512 * cc:512 * (cc + 1)], start=False, stop=False))
                        p.op("tensor", lambda e, cc=cc, pO=pO: e.matmul(
                            pO[:, 512 * cc:512 * (cc + 1)], onesr_bf[:, :],
                            bout_lo[:, 512 * cc:512 * (cc + 1)], start=False, stop=True),
                            incs=(sPE, 1) if cc == 1 else ())
                    pe_marks[sub] = sPE.n
                for sub in range(2):
                    ti = 2 * tp + sub
                    pO = pOs[sub]
                    p.wait("vector", sPE, pe_marks[sub])
                    p.op("vector", lambda e, ti=ti, pO=pO: e.tensor_tensor(
                        xs[:, 1024 * ti:1024 * (ti + 1)], pO[:, :],
                        xs[:, 1024 * ti:1024 * (ti + 1)], ALU.add), incs=(sDVE, 1))
                    o_done[ti] = sDVE.n
            emit_ln(lambda ti: p.wait("scalar", sDVE, o_done[ti]))
            ln4_final = sDVE.n
            for ti in range(16):
                q = "sync" if ti % 2 == 0 else "scalar"
                p.wait(q, sDVE, ln4_final - (15 - ti))
                p.dma(q, out_t[128 * ti:128 * (ti + 1), :], xs[:, 1024 * ti:1024 * (ti + 1)], sST)
            p.wait("sync", sST)
            p.flush()
    return nc


def run_fused(inputs):
    ones_row, ident, halves, e16 = _common_consts()
    ncF = _get("fused", build_fused)
    X = np.asarray(inputs["L0_attn_x"], np.float32)
    Wg = np.asarray(inputs["L0_ffn_Wg"], np.float32)
    bg = np.asarray(inputs["L0_ffn_bg"], np.float32)
    b1 = np.asarray(inputs["L0_ffn_b1"], np.float32)
    b2 = np.asarray(inputs["L0_ffn_b2"], np.float32)
    wg_st = np.ascontiguousarray(Wg.reshape(8, 128, 8).transpose(1, 0, 2).reshape(128, 64))
    bg_bc = np.ascontiguousarray(np.tile(bg.reshape(1, 8), (128, 1)))
    b1cols = np.ascontiguousarray(b1.reshape(8, 8, 128).transpose(2, 0, 1).reshape(128, 64))
    apv = np.asarray(inputs["L1_attn_a_param"], np.float32).reshape(8, 128)
    b_in = np.asarray(inputs["L1_attn_b_in"], np.float32)
    b_a = np.asarray(inputs["L1_attn_b_a"], np.float32)
    cm1 = np.ones((128, 16), np.float32)
    cm0 = np.zeros((128, 16), np.float32)
    cn_first = np.zeros((128, 16), np.float32)
    cn_first[:, 0:8] = 1.0
    in_maps = []
    for c in range(NCORES):
        b, hf = c // 2, c % 2
        in_maps.append(dict(
            x_own=np.ascontiguousarray(X[b, TC * hf:TC * (hf + 1)]),
            x_oth=np.ascontiguousarray(X[b, TC * (1 - hf):TC * (2 - hf)]),
            wq_d=np.asarray(inputs["L0_attn_wq_d"], np.float32),
            w_qk=np.asarray(inputs["L0_attn_w_qk"], np.float32),
            wkv_d=np.asarray(inputs["L0_attn_wkv_d"], np.float32),
            wv_u=np.asarray(inputs["L0_attn_wv_u"], np.float32),
            w_proj=np.asarray(inputs["L0_attn_w_proj"], np.float32),
            bq_cols=np.ascontiguousarray(np.asarray(inputs["L0_attn_bq_d"], np.float32).reshape(4, 128).T),
            bqk_cols=np.ascontiguousarray(np.asarray(inputs["L0_attn_b_qk"], np.float32).reshape(16, 64).T),
            bkv_col=np.asarray(inputs["L0_attn_bkv_d"], np.float32).reshape(64, 1),
            bvu_row=np.asarray(inputs["L0_attn_bv_u"], np.float32).reshape(1, D),
            bprj_row=np.asarray(inputs["L0_attn_b_proj"], np.float32).reshape(1, D),
            ones_row=ones_row, ident_in=ident, halves=halves, e16=e16,
            wg_st=wg_st, bg_bc=bg_bc,
            w1=np.asarray(inputs["L0_ffn_W1"], np.float32),
            w2=np.asarray(inputs["L0_ffn_W2"], np.float32),
            b1cols=b1cols, b2m=b2,
            w_in_a=np.asarray(inputs["L1_attn_w_in"], np.float32),
            w_a_a=np.asarray(inputs["L1_attn_w_a"], np.float32),
            b_in_c8=np.ascontiguousarray(b_in.T),
            b_a_c8=np.ascontiguousarray(b_a.T),
            ap_c8=np.ascontiguousarray(apv.T),
            cmask_in=cm1 if hf == 1 else cm0,
            cneut_in=cm0 if hf == 1 else cn_first,
            win=np.asarray(inputs["L1_ffn_Win"], np.float32),
            bin_c=np.asarray(inputs["L1_ffn_bin_"], np.float32).reshape(64, 1),
            w1c=np.asarray(inputs["L1_ffn_W1"], np.float32),
            b1c=np.asarray(inputs["L1_ffn_b1"], np.float32),
            w2c=np.asarray(inputs["L1_ffn_W2"], np.float32),
            b2c=np.asarray(inputs["L1_ffn_b2"], np.float32).reshape(64, 1),
            wout=np.asarray(inputs["L1_ffn_Wout"], np.float32),
            bout_row=np.asarray(inputs["L1_ffn_bout"], np.float32).reshape(1, D),
        ))
    res = run_bass_kernel_spmd(ncF, in_maps, core_ids=list(range(NCORES)))
    out = np.zeros((B, T, D), np.float32)
    for c in range(NCORES):
        b, hf = c // 2, c % 2
        out[b, TC * hf:TC * (hf + 1)] = res.results[c]["out"]
    return out



_cache = {}


def _get(name, builder):
    if name not in _cache:
        _cache[name] = builder()
    return _cache[name]


def _common_consts():
    ones_row = np.ones((1, 128), np.float32)
    ident = np.eye(128, dtype=np.float32)
    halves = np.zeros((128, 2), np.float32)
    halves[0:64, 0] = 1.0
    halves[64:128, 1] = 1.0
    e16 = np.zeros((8, 16, 128), np.float32)
    for c in range(8):
        for r in range(128):
            e16[c, 2 * c + r // 64, r] = 1.0
    return ones_row, ident, halves, e16


def run_l1(inputs, X):
    ones_row, ident, halves, e16 = _common_consts()
    nc1 = _get("l1", build_launch1)
    in_maps = []
    for c in range(NCORES):
        b, hf = c // 2, c % 2
        m = dict(
            x_own=np.ascontiguousarray(X[b, TC * hf:TC * (hf + 1)]),
            x_oth=np.ascontiguousarray(X[b, TC * (1 - hf):TC * (2 - hf)]),
            wq_d=np.asarray(inputs["L0_attn_wq_d"], np.float32),
            w_qk=np.asarray(inputs["L0_attn_w_qk"], np.float32),
            wkv_d=np.asarray(inputs["L0_attn_wkv_d"], np.float32),
            wv_u=np.asarray(inputs["L0_attn_wv_u"], np.float32),
            w_proj=np.asarray(inputs["L0_attn_w_proj"], np.float32),
            bq_cols=np.ascontiguousarray(np.asarray(inputs["L0_attn_bq_d"], np.float32).reshape(4, 128).T),
            bqk_cols=np.ascontiguousarray(np.asarray(inputs["L0_attn_b_qk"], np.float32).reshape(8, 128).T),
            bkv_col=np.asarray(inputs["L0_attn_bkv_d"], np.float32).reshape(64, 1),
            bvu_row=np.asarray(inputs["L0_attn_bv_u"], np.float32).reshape(1, D),
            bprj_row=np.asarray(inputs["L0_attn_b_proj"], np.float32).reshape(1, D),
            ones_row=ones_row, ident_in=ident, halves=halves, e16=e16,
        )
        in_maps.append(m)
    res = run_bass_kernel_spmd(nc1, in_maps, core_ids=list(range(NCORES)))
    x1 = np.zeros((B, T, D), np.float32)
    for c in range(NCORES):
        b, hf = c // 2, c % 2
        x1[b, TC * hf:TC * (hf + 1)] = res.results[c]["x1"]
    return x1


def run_l2(inputs, x1_full):
    ones_row, ident, halves, e16 = _common_consts()
    nc2 = _get("l2", build_launch2)
    Wg = np.asarray(inputs["L0_ffn_Wg"], np.float32)
    bg = np.asarray(inputs["L0_ffn_bg"], np.float32)
    b1 = np.asarray(inputs["L0_ffn_b1"], np.float32)
    b2 = np.asarray(inputs["L0_ffn_b2"], np.float32)
    wg_st = np.ascontiguousarray(Wg.reshape(8, 128, 8).transpose(1, 0, 2).reshape(128, 64))
    bg_bc = np.ascontiguousarray(np.tile(bg.reshape(1, 8), (128, 1)))
    b1cols = np.ascontiguousarray(b1.reshape(8, 8, 128).transpose(2, 0, 1).reshape(128, 64))
    w1 = np.asarray(inputs["L0_ffn_W1"], np.float32)
    w2 = np.asarray(inputs["L0_ffn_W2"], np.float32)
    in_maps = []
    for c in range(NCORES):
        b, hf = c // 2, c % 2
        in_maps.append(dict(
            x1=np.ascontiguousarray(x1_full[b, TC * hf:TC * (hf + 1)]),
            wg_st=wg_st, bg_bc=bg_bc, w1=w1, w2=w2, b1cols=b1cols, b2m=b2,
            ones_row=ones_row, ident_in=ident,
        ))
    res = run_bass_kernel_spmd(nc2, in_maps, core_ids=list(range(NCORES)))
    x2 = np.zeros((B, T, D), np.float32)
    for c in range(NCORES):
        b, hf = c // 2, c % 2
        x2[b, TC * hf:TC * (hf + 1)] = res.results[c]["x2"]
    return x2


def run_l3(inputs, x2_full):
    ones_row, ident, halves, e16 = _common_consts()
    nc3 = _get("l3", build_launch3)
    w_in = np.asarray(inputs["L1_attn_w_in"], np.float32)
    w_a = np.asarray(inputs["L1_attn_w_a"], np.float32)
    b_in = np.asarray(inputs["L1_attn_b_in"], np.float32)
    b_a = np.asarray(inputs["L1_attn_b_a"], np.float32)
    ap = np.asarray(inputs["L1_attn_a_param"], np.float32).reshape(8, 128)
    in_maps = []
    for c in range(NCORES):
        b, lh = c // 2, c % 2
        in_maps.append(dict(
            x2s=np.ascontiguousarray(x2_full[b][:, 512 * lh:512 * (lh + 1)]),
            w_in_s=np.ascontiguousarray(w_in[4 * lh:4 * (lh + 1)]),
            w_a_s=np.ascontiguousarray(w_a[4 * lh:4 * (lh + 1)]),
            b_in_c=np.ascontiguousarray(b_in[4 * lh:4 * (lh + 1)].T),
            b_a_c=np.ascontiguousarray(b_a[4 * lh:4 * (lh + 1)].T),
            ap_c=np.ascontiguousarray(ap[4 * lh:4 * (lh + 1)].T),
            ident_in=ident,
        ))
    res = run_bass_kernel_spmd(nc3, in_maps, core_ids=list(range(NCORES)))
    h = np.zeros((B, T, D), np.float32)
    for c in range(NCORES):
        b, lh = c // 2, c % 2
        h[b][:, 512 * lh:512 * (lh + 1)] = res.results[c]["h"]
    return h


def run_l4(inputs, x2_full, h_full):
    ones_row, ident, halves, e16 = _common_consts()
    nc4 = _get("l4", build_launch4)
    in_maps = []
    for c in range(NCORES):
        b, hf = c // 2, c % 2
        in_maps.append(dict(
            x2=np.ascontiguousarray(x2_full[b, TC * hf:TC * (hf + 1)]),
            h=np.ascontiguousarray(h_full[b, TC * hf:TC * (hf + 1)]),
            win=np.asarray(inputs["L1_ffn_Win"], np.float32),
            bin_c=np.asarray(inputs["L1_ffn_bin_"], np.float32).reshape(64, 1),
            w1c=np.asarray(inputs["L1_ffn_W1"], np.float32),
            b1c=np.asarray(inputs["L1_ffn_b1"], np.float32),
            w2c=np.asarray(inputs["L1_ffn_W2"], np.float32),
            b2c=np.asarray(inputs["L1_ffn_b2"], np.float32).reshape(64, 1),
            wout=np.asarray(inputs["L1_ffn_Wout"], np.float32),
            bout_row=np.asarray(inputs["L1_ffn_bout"], np.float32).reshape(1, D),
            ones_row=ones_row, ident_in=ident,
        ))
    res = run_bass_kernel_spmd(nc4, in_maps, core_ids=list(range(NCORES)))
    out = np.zeros((B, T, D), np.float32)
    for c in range(NCORES):
        b, hf = c // 2, c % 2
        out[b, TC * hf:TC * (hf + 1)] = res.results[c]["out"]
    return out


def kernel(**inputs):
    return run_fused(inputs)
